# version 14
# baseline (speedup 1.0000x reference)
import numpy as np
import ml_dtypes
from contextlib import ExitStack

import concourse.bass as bass
import concourse.mybir as mybir
from concourse.bass_utils import run_bass_kernel_spmd
from concourse.bass import IndirectOffsetOnAxis

F32 = mybir.dt.float32
BF16 = mybir.dt.bfloat16
I32 = mybir.dt.int32
AF = mybir.ActivationFunctionType
ALU = mybir.AluOpType
AX = mybir.AxisListType

T = 2048
D = 1024
NT = T // 128
ND = D // 128
NH = 16
DK = 64
NST = 16
E = 16
CAP = 256
HID = 512
QT = 512
DECAY = 0.25
NEG = -1e9
TRASH = E * CAP

BF = ml_dtypes.bfloat16


def _host_constants():
    c = {}
    c["ident"] = np.eye(128, dtype=np.float32).astype(BF)
    c["ident32"] = np.eye(128, dtype=np.float32)
    c["strictL"] = np.triu(np.ones((128, 128), dtype=np.float32), 1).astype(BF)
    c["ones1x128"] = np.ones((1, 128), dtype=np.float32).astype(BF)
    c["ones128sq"] = np.ones((128, 128), dtype=np.float32)
    c["strictL32"] = np.triu(np.ones((128, 128), dtype=np.float32), 1)
    c["iota16"] = np.tile(np.arange(16, dtype=np.float32), (128, 1))
    c["iotat"] = np.tile(np.arange(16, dtype=np.float32), (128, 16))
    COFF = QT
    width = (T - 128) + COFF + QT
    j = np.arange(width)[None, :]
    p = np.arange(128)[:, None]
    d = j - COFF - p
    lnw = np.where(d < 0, NEG, np.where(d == 0, 0.0,
                   -DECAY * np.log(np.maximum(d, 1)))).astype(np.float32)
    c["lnW"] = lnw.astype(BF)
    c["Wdecay"] = np.where(d < 0, 0.0, np.exp(np.where(d <= 0, 0.0,
                  -DECAY * np.log(np.maximum(d, 1))))).astype(np.float32).astype(BF)
    c["lnW_COFF"] = COFF
    maps = np.zeros((ND, 128, 16), dtype=np.float32)
    for fd in range(ND):
        maps[fd, 0:64, 2 * fd] = 1.0
        maps[fd, 64:128, 2 * fd + 1] = 1.0
    c["hmaps"] = maps.astype(BF)
    kv = np.zeros((128, 8), dtype=np.float32)
    kv[:, 0] = 1e-5; kv[:, 1] = 1.0; kv[:, 2] = -0.5; kv[:, 3] = -1.0 / D; kv[:, 4] = -1.0; kv[:, 5] = 0.5
    c["kvec"] = kv
    return c


CST = _host_constants()


def build_program(debug=()):
    nc = bass.Bass("TRN2", target_bir_lowering=False,
                   detect_race_conditions=False)

    din = {}
    def dram_in(name, shape, dtype=F32):
        din[name] = nc.dram_tensor(name, list(shape), dtype, kind="ExternalInput").ap()
        return din[name]

    x_in = dram_in("x", (T, D))
    W_in_ = dram_in("W_in", (D, 32))
    b_in_ = dram_in("b_in", (32, 1))
    W_dt1 = dram_in("W_dt1", (D, 64))
    b_dt1 = dram_in("b_dt1", (64, 1))
    W_dt2 = dram_in("W_dt2", (64, D))
    b_dt2 = dram_in("b_dt2", (D, 1))
    A_cols = dram_in("A_cols", (128, 128))
    D_row = dram_in("D_row", (1, D))
    Wq = dram_in("Wq", (128, ND * D)); bq = dram_in("bq", (D, 1))
    Wk = dram_in("Wk", (128, ND * D)); bk = dram_in("bk", (D, 1))
    Wv = dram_in("Wv", (128, ND * D)); bv = dram_in("bv", (D, 1))
    Wo = dram_in("Wo", (128, ND * D)); bo = dram_in("bo", (D, 1))
    w_router = dram_in("w_router", (D, 16))
    w_fc = dram_in("w_fc", (E, 128, ND * HID))
    w_gate = dram_in("w_gate", (E, 128, ND * HID))
    w_proj = dram_in("w_proj", (E, 128, 4 * D))
    ident_in = dram_in("ident", (128, 128), BF16)
    ident32_in = dram_in("ident32", (128, 128))
    strictL_in = dram_in("strictL", (128, 128), BF16)
    ones1x128_in = dram_in("ones1x128", (1, 128), BF16)
    ones128_in = dram_in("ones128sq", (128, 128))
    strictL32_in = dram_in("strictL32", (128, 128))
    iota16_in = dram_in("iota16", (128, 16))
    iotat_in = dram_in("iotat", (128, 256))
    lnW_in = dram_in("lnW", CST["lnW"].shape, BF16)
    Wdec_in = dram_in("Wdecay", CST["Wdecay"].shape, BF16)
    hmaps_in = dram_in("hmaps", (ND, 128, 16), BF16)
    kvec_in = dram_in("kvec", (128, 8))

    y_out = nc.dram_tensor("y", [T, D], F32, kind="ExternalOutput").ap()

    dbg_outs = {}
    def dbg(name, shape, dtype=F32, force=False):
        if name in debug or force:
            dbg_outs[name] = nc.dram_tensor("dbg_" + name, list(shape), dtype,
                                            kind="ExternalOutput").ap()
        return dbg_outs.get(name)

    x_d = nc.dram_tensor("x_spill", [T, D], F32, kind="Internal").ap()
    xn_d = nc.dram_tensor("xn_d", [T, D], BF16, kind="Internal").ap()
    Xtm_d = nc.dram_tensor("Xtm_d", [33 * 128, D], BF16, kind="Internal").ap()
    o_d = nc.dram_tensor("o_d", [E * CAP, D], BF16, kind="Internal").ap()
    bcast_d = nc.dram_tensor("bcast_d", [64, T], BF16, kind="Internal").ap()
    brow_d = nc.dram_tensor("brow_d", [1, D], F32, kind="Internal").ap()
    brow2_d = nc.dram_tensor("brow2_d", [1, 64], F32, kind="Internal").ap()
    srow_d = nc.dram_tensor("srow_d", [1, 1024], F32, kind="Internal").ap()
    dscr_d = nc.dram_tensor("dscr_d", [64, 1024], F32, kind="Internal").ap()
    qT_d = nc.dram_tensor("qT_d", [D, T], BF16, kind="Internal").ap()

    ctx = ExitStack()
    with ctx:
        EA = ctx.enter_context

        x_sb = EA(nc.sbuf_tensor("x_sb", [128, NT * D], F32))
        ident = EA(nc.sbuf_tensor("ident_s", [128, 128], BF16))
        ident32 = EA(nc.sbuf_tensor("ident32_s", [128, 128], F32))
        strictL = EA(nc.sbuf_tensor("strictL_s", [128, 128], BF16))
        ones1x128 = EA(nc.sbuf_tensor("ones1x128_s", [1, 128], BF16))
        ones128 = EA(nc.sbuf_tensor("ones128sq_s", [128, 128], F32))
        strictL32 = EA(nc.sbuf_tensor("strictL32_s", [128, 128], F32))
        zero_bf = EA(nc.sbuf_tensor("zero_bf", [128, 512], BF16))
        iotat = EA(nc.sbuf_tensor("iotat_s", [128, 256], F32))
        Wdec = EA(nc.sbuf_tensor("Wdec_s", list(CST["Wdecay"].shape), BF16))
        hmaps = EA(nc.sbuf_tensor("hmaps_s", [128, ND * 16], BF16))
        A_sb = EA(nc.sbuf_tensor("A_sb", [128, 128], F32))
        recipA = EA(nc.sbuf_tensor("recipA", [128, 128], F32))
        Dcol = EA(nc.sbuf_tensor("Dcol", [128, ND], F32))
        biases = EA(nc.sbuf_tensor("biases", [128, 48], F32))
        stats = EA(nc.sbuf_tensor("stats", [128, 16 * NT], F32))
        statsL = EA(nc.sbuf_tensor("statsL", [128, 2 * NT], F32))
        sq_scr = EA(nc.sbuf_tensor("sq_scr", [128, D], F32))
        kvec = EA(nc.sbuf_tensor("kvec_s", [128, 8], F32))

        block = EA(nc.Block())

        sem = {}
        cnt = {}
        for name in ["L", "T", "V", "A", "G", "S", "Q"]:
            sem[name] = EA(nc.semaphore("sem" + name))
            cnt[name] = 0

        def inc(name, instr, k=1):
            instr.then_inc(sem[name], k)
            cnt[name] += k
            return cnt[name]

        def w(eng, name, val=None):
            eng.wait_ge(sem[name], cnt[name] if val is None else val)

        def phase_barrier():
            snap = dict(cnt)
            for blk in [block.tensor, block.vector, block.scalar,
                        block.gpsimd, block.sync]:
                @blk
                def _(eng):
                    for k in ["T", "V", "A", "G", "S", "L"]:
                        eng.wait_ge(sem[k], snap[k])

        @block.sync
        def _(sync):
            for i in range(NT):
                inc("L", sync.dma_start(out=x_sb[:, i * D:(i + 1) * D],
                                        in_=x_in[i * 128:(i + 1) * 128, :]), 16)
            inc("L", sync.dma_start(out=ident[:], in_=ident_in), 16)
            inc("L", sync.dma_start(out=ident32[:], in_=ident32_in), 16)
            inc("L", sync.dma_start(out=strictL[:], in_=strictL_in), 16)
            inc("L", sync.dma_start(out=ones1x128[:], in_=ones1x128_in), 16)
            inc("L", sync.dma_start(out=ones128[:], in_=ones128_in), 16)
            inc("L", sync.dma_start(out=strictL32[:], in_=strictL32_in), 16)
            inc("L", sync.dma_start(out=iotat[:], in_=iotat_in), 16)
            inc("L", sync.dma_start(out=Wdec[:], in_=Wdec_in), 16)
            for fd in range(ND):
                inc("L", sync.dma_start(out=hmaps[:, fd * 16:(fd + 1) * 16],
                                        in_=hmaps_in[fd]), 16)
            inc("L", sync.dma_start(out=A_sb[:], in_=A_cols), 16)
            for fd in range(ND):
                inc("L", sync.dma_start(out=Dcol[:, fd:fd + 1],
                                        in_=D_row[0:1, fd * 128:(fd + 1) * 128]
                                        .rearrange("a b -> b a")), 16)
            for j, b in enumerate([bq, bk, bv, bo, b_dt2]):
                for fd in range(ND):
                    inc("L", sync.dma_start(out=biases[:, 8 * j + fd:8 * j + fd + 1],
                                            in_=b[fd * 128:(fd + 1) * 128, :]), 16)
            inc("L", sync.dma_start(out=biases[0:32, 40:41], in_=b_in_), 16)
            inc("L", sync.dma_start(out=biases[0:64, 41:42], in_=b_dt1), 16)
            inc("L", sync.dma_start(out=kvec[:], in_=kvec_in), 16)
        loads0 = cnt["L"]

        @block.vector
        def _(vector):
            w(vector, "L", loads0)
            inc("V", vector.reciprocal(recipA[:], A_sb[:]))
            inc("V", vector.memset(zero_bf[:], 0.0))
        recipA_done = cnt["V"]

        @block.gpsimd
        def _(gpsimd):
            w(gpsimd, "V", recipA_done)
            for r in range(33):
                for hf in range(2):
                    inc("G", gpsimd.dma_start(
                        out=Xtm_d[r * 128:(r + 1) * 128, hf * 512:(hf + 1) * 512],
                        in_=zero_bf[:]), 16)
        xtm_zeroed = cnt["G"]

        def emit_layernorm(xn_sb, ready=None):
            @block.vector
            def _(vector):
                if ready is not None:
                    w(vector, ready[0], ready[1])
                for i in range(NT):
                    inc("V", vector.bn_stats(stats[:, 12 * i:12 * i + 6],
                                             x_sb[:, i * D:i * D + 512]))
                    inc("V", vector.bn_stats(stats[:, 12 * i + 6:12 * i + 12],
                                             x_sb[:, i * D + 512:(i + 1) * D]))
                vector.wait_ge(sem["V"], cnt["V"])
                for i in range(NT):
                    inc("V", vector.bn_aggr(stats[:, 12 * NT + 2 * i:12 * NT + 2 * i + 2],
                                            stats[:, 12 * i:12 * i + 12]))
            v_bn = cnt["V"]

            @block.scalar
            def _(scalar):
                w(scalar, "V", v_bn)
                w(scalar, "L", loads0)
                inc("A", scalar.activation(statsL[:, 0:NT],
                                           stats[:, 12 * NT:14 * NT:2],
                                           AF.Identity, scale=kvec[:, 4:5]))
                inc("A", scalar.activation(stats[:, 15 * NT:16 * NT],
                                           stats[:, 12 * NT + 1:14 * NT:2],
                                           AF.Ln, bias=kvec[:, 0:1]))
                inc("A", scalar.activation(statsL[:, NT:2 * NT],
                                           stats[:, 15 * NT:16 * NT],
                                           AF.Exp, scale=kvec[:, 2:3]))
            a_rsig = cnt["A"]

            @block.vector
            def _(vector):
                w(vector, "A", a_rsig)
                for i in range(NT):
                    inc("V", vector.tensor_tensor(
                        sq_scr[:],
                        x_sb[:, i * D:(i + 1) * D],
                        statsL[:, i:i + 1].broadcast_to([128, D]),
                        ALU.add))
                    inc("V", vector.tensor_tensor(
                        xn_sb[:, i * D:(i + 1) * D],
                        sq_scr[:],
                        statsL[:, NT + i:NT + i + 1].broadcast_to([128, D]),
                        ALU.mult))
            return cnt["V"]

        def emit_layernorm_dram(dram_out, stage_sb, ready=None):
            @block.vector
            def _(vector):
                if ready is not None:
                    w(vector, ready[0], ready[1])
                for i in range(NT):
                    inc("V", vector.bn_stats(stats[:, 12 * i:12 * i + 6],
                                             x_sb[:, i * D:i * D + 512]))
                    inc("V", vector.bn_stats(stats[:, 12 * i + 6:12 * i + 12],
                                             x_sb[:, i * D + 512:(i + 1) * D]))
                vector.wait_ge(sem["V"], cnt["V"])
                for i in range(NT):
                    inc("V", vector.bn_aggr(stats[:, 12 * NT + 2 * i:12 * NT + 2 * i + 2],
                                            stats[:, 12 * i:12 * i + 12]))
            v_bn = cnt["V"]

            @block.scalar
            def _(scalar):
                w(scalar, "V", v_bn)
                inc("A", scalar.activation(statsL[:, 0:NT],
                                           stats[:, 12 * NT:14 * NT:2],
                                           AF.Identity, scale=kvec[:, 4:5]))
                inc("A", scalar.activation(stats[:, 15 * NT:16 * NT],
                                           stats[:, 12 * NT + 1:14 * NT:2],
                                           AF.Ln, bias=kvec[:, 0:1]))
                inc("A", scalar.activation(statsL[:, NT:2 * NT],
                                           stats[:, 15 * NT:16 * NT],
                                           AF.Exp, scale=kvec[:, 2:3]))
            a_rsig = cnt["A"]

            st_marks = {}
            @block.vector
            def _(vector):
                w(vector, "A", a_rsig)
            for i in range(NT):
                @block.vector
                def _(vector, i=i):
                    if i >= 2:
                        w(vector, "S", st_marks[i - 2])
                    inc("V", vector.tensor_tensor(
                        sq_scr[:], x_sb[:, i * D:(i + 1) * D],
                        statsL[:, i:i + 1].broadcast_to([128, D]), ALU.add))
                    inc("V", vector.tensor_tensor(
                        stage_sb[:, (i % 2) * D:(i % 2 + 1) * D], sq_scr[:],
                        statsL[:, NT + i:NT + i + 1].broadcast_to([128, D]),
                        ALU.mult))
                v_ap = cnt["V"]

                @block.sync
                def _(sync, i=i, v_ap=v_ap):
                    w(sync, "V", v_ap)
                    inc("S", sync.dma_start(out=dram_out[i * 128:(i + 1) * 128, :],
                                            in_=stage_sb[:, (i % 2) * D:(i % 2 + 1) * D]), 16)
                st_marks[i] = cnt["S"]
            return cnt["S"]

        def emit_tm_to_fm(src_sb, dst_fm, ps_pair, ready_v):
            last_copy = [ready_v]
            for k, (fd, i) in enumerate([(fd, i) for fd in range(ND) for i in range(NT)]):
                pst = ps_pair[k % 2]

                @block.tensor
                def _(tensor, fd=fd, i=i, pst=pst, k=k):
                    w(tensor, "V", last_copy[0])
                    inc("T", tensor.transpose(
                        pst[:], src_sb[:, i * D + fd * 128:i * D + fd * 128 + 128],
                        ident[:]))
                t_tr = cnt["T"]

                @block.vector
                def _(vector, fd=fd, i=i, pst=pst, t_tr=t_tr):
                    w(vector, "T", t_tr)
                    inc("V", vector.tensor_copy(dst_fm[fd][:, i * 128:(i + 1) * 128],
                                                pst[:]))
                last_copy[0] = cnt["V"]
            return cnt["V"]

        with ExitStack() as mctx:
            ME = mctx.enter_context
            xn_fm = [ME(nc.sbuf_tensor(f"xn_fm{fd}", [128, T], BF16)) for fd in range(ND)]
            ps_p_ctx = nc.psum_tensor("ps_p", [64, T], F32)
            ps_p = ps_p_ctx.__enter__()
            Win_sb = ME(nc.sbuf_tensor("Win_sb", [128, ND * 32], BF16))
            Wdt1_sb = ME(nc.sbuf_tensor("Wdt1_sb", [128, ND * 64], BF16))
            Wdt2_sb = ME(nc.sbuf_tensor("Wdt2_sb", [64, D], BF16))
            BCu = ME(nc.sbuf_tensor("BCu", [32, T], BF16))
            u_sb = ME(nc.sbuf_tensor("u_sb", [64, T], BF16))
            with nc.sbuf_tensor("xn_tm", [128, NT * D], BF16) as xn_tm, \
                 nc.psum_tensor("ps_trA", [128, 128], BF16) as ps_trA, \
                 nc.psum_tensor("ps_trB", [128, 128], BF16) as ps_trB:
                ln1_v = emit_layernorm(xn_tm, ready=("L", loads0))
                if "lnstats" in debug:
                    dbg("lnstats", (128, 2 * NT))
                    @block.sync
                    def _(sync):
                        w(sync, "V", ln1_v)
                        inc("S", sync.dma_start(out=dbg_outs["lnstats"], in_=statsL[:]), 16)
                if "xn_tm" in debug:
                    dbg("xn_tm", (128, NT * D), BF16)
                    @block.sync
                    def _(sync):
                        w(sync, "V", ln1_v)
                        inc("S", sync.dma_start(out=dbg_outs["xn_tm"], in_=xn_tm[:]), 16)
                tr1_v = emit_tm_to_fm(xn_tm, xn_fm, [ps_trA, ps_trB], ln1_v)

            if "xn_fm" in debug:
                dbg("xn_fm", (D, T), BF16)
                @block.sync
                def _(sync):
                    w(sync, "V", tr1_v)
                    for fd in range(ND):
                        inc("S", sync.dma_start(out=dbg_outs["xn_fm"][fd * 128:(fd + 1) * 128, :],
                                                in_=xn_fm[fd][:]), 16)

            @block.gpsimd
            def _(gpsimd):
                for fd in range(ND):
                    inc("G", gpsimd.dma_start(out=Win_sb[:, fd * 32:(fd + 1) * 32],
                                              in_=W_in_[fd * 128:(fd + 1) * 128, :]), 16)
                    inc("G", gpsimd.dma_start(out=Wdt1_sb[:, fd * 64:(fd + 1) * 64],
                                              in_=W_dt1[fd * 128:(fd + 1) * 128, :]), 16)
                inc("G", gpsimd.dma_start(out=Wdt2_sb[:], in_=W_dt2), 16)
            mw_l = cnt["G"]

            if "Win" in debug:
                dbg("Win", (128, ND * 32), BF16)
                @block.sync
                def _(sync):
                    w(sync, "G", mw_l)
                    inc("S", sync.dma_start(out=dbg_outs["Win"], in_=Win_sb[:]), 16)


            @block.tensor
            def _(tensor):
                w(tensor, "G", mw_l)
                w(tensor, "V", tr1_v)
                for hf in range(4):
                    for fd in range(ND):
                        inc("T", tensor.matmul(ps_p[0:32, hf * 512:(hf + 1) * 512],
                                               Win_sb[:, fd * 32:(fd + 1) * 32],
                                               xn_fm[fd][:, hf * 512:(hf + 1) * 512],
                                               start=(fd == 0), stop=(fd == ND - 1)))
            t_bc = cnt["T"]

            @block.scalar
            def _(scalar):
                w(scalar, "A", 0)
                w(scalar, "T", t_bc)
                inc("A", scalar.activation(BCu[0:32, :], ps_p[0:32, :], AF.Identity,
                                           bias=biases[0:32, 40:41]))
            a_bc = cnt["A"]

            @block.tensor
            def _(tensor):
                w(tensor, "A", a_bc)
                for hf in range(4):
                    for fd in range(ND):
                        inc("T", tensor.matmul(ps_p[:, hf * 512:(hf + 1) * 512],
                                               Wdt1_sb[:, fd * 64:(fd + 1) * 64],
                                               xn_fm[fd][:, hf * 512:(hf + 1) * 512],
                                               start=(fd == 0), stop=(fd == ND - 1)))
            t_u = cnt["T"]

            @block.scalar
            def _(scalar):
                w(scalar, "T", t_u)
                inc("A", scalar.activation(u_sb[:], ps_p[:], AF.Identity,
                                           bias=biases[0:64, 41:42]))
            a_u = cnt["A"]

            @block.sync
            def _(sync):
                w(sync, "A", a_u)
                inc("S", sync.dma_start(out=bcast_d[0:32, :], in_=BCu[:]), 16)
            s_bc = cnt["S"]

            if "BCu" in debug:
                dbg("BCu", (96, T), BF16)
                @block.sync
                def _(sync):
                    w(sync, "A", a_u)
                    inc("S", sync.dma_start(out=dbg_outs["BCu"][0:32, :], in_=BCu[:]), 16)
                    inc("S", sync.dma_start(out=dbg_outs["BCu"][32:96, :], in_=u_sb[:]), 16)

            ps_p_ctx.__exit__(None, None, None)

            dt_sb = ME(nc.sbuf_tensor("dt_sb", [128, T], BF16))
            e_scr = ME(nc.sbuf_tensor("e_scr", [128, 1024], F32))
            abar = [ME(nc.sbuf_tensor(f"abar{i}", [128, T], F32)) for i in range(2)]
            wr_sb = [ME(nc.sbuf_tensor(f"wr_sb{i}", [128, T], BF16)) for i in range(2)]
            dum_s = ME(nc.sbuf_tensor("dum_s", [1, 64], BF16))
            bx_sb = [ME(nc.sbuf_tensor(f"bx_sb{i}", [128, T], BF16)) for i in range(2)]
            h_sb = [ME(nc.sbuf_tensor(f"h_sb{i}", [128, T], BF16)) for i in range(2)]
            z_sb = [ME(nc.sbuf_tensor(f"z_sb{i}", [128, T], BF16)) for i in range(2)]
            bmc = [ME(nc.sbuf_tensor(f"bmc{i}", [128, T], BF16)) for i in range(3)]
            bmcC = [ME(nc.sbuf_tensor(f"bmcC{i}", [128, T], BF16)) for i in range(3)]
            ps_dt = ME(nc.psum_tensor("ps_dt", [128, 1024], F32))
            ps_y = ME(nc.psum_tensor("ps_y", [128, T], F32))
            ps_yt = [ME(nc.psum_tensor(f"ps_yt{i}", [128, 128], F32)) for i in range(2)]

            bmc_marks, ab_marks, wr_marks, bx_marks = {}, {}, {}, {}
            scan_marks, z_marks, zmm_marks, ty_marks, add_marks = {}, {}, {}, {}, {}
            v_scan_prev = [0]

            if "dt" in debug:
                dbg("dt", (D, T), BF16)
            if "y_fm" in debug:
                dbg("y_fm", (D, T))
            if "abar0" in debug:
                dbg("abar0", (128, T))

            for db in range(ND):
                T0, A0, V0g = cnt["T"], cnt["A"], cnt["V"]

                @block.tensor
                def _(tensor, db=db, A0=A0):
                    if db == 0:
                        w(tensor, "A", a_u)
                    for hf in range(2):
                        if hf == 1:
                            tensor.wait_ge(sem["A"], A0 + 1)
                        else:
                            tensor.wait_ge(sem["A"], A0)
                        for qq in range(2):
                            inc("T", tensor.matmul(
                                ps_dt[:, qq * 512:(qq + 1) * 512],
                                Wdt2_sb[:, db * 128:(db + 1) * 128],
                                u_sb[:, hf * 1024 + qq * 512:hf * 1024 + (qq + 1) * 512],
                                start=True, stop=True))

                @block.scalar
                def _(scalar, db=db, T0=T0, A0=A0, V0g=V0g):
                    for hf in range(2):
                        scalar.wait_ge(sem["T"], T0 + 2 * (hf + 1))
                        inc("A", scalar.activation(e_scr[:], ps_dt[:], AF.Exp,
                                                   bias=biases[:, 32 + db:33 + db]))
                        inc("A", scalar.activation(dt_sb[:, hf * 1024:(hf + 1) * 1024],
                                                   e_scr[:], AF.Ln, bias=kvec[:, 1:2]))
                    scalar.wait_ge(sem["V"], V0g + 1)
                    for n in range(NST):
                        col = db * 16 + n
                        if n < 2:
                            scalar.wait_ge(sem["V"], add_marks.get(db - 1, 0))
                        else:
                            scalar.wait_ge(sem["V"], V0g + 1 + 5 * (n - 2) + 4)
                        inc("A", scalar.activation(abar[n % 2][:], dt_sb[:], AF.Exp,
                                                   scale=A_sb[:, col:col + 1]))
                a_ab_last = cnt["A"]

                @block.vector
                def _(vector, db=db, A0=A0):
                    w(vector, "A", A0 + 4)
                    inc("V", vector.tensor_scalar(dt_sb[:], dt_sb[:], 1e-7, 1e6,
                                                  ALU.max, ALU.min))
                v_dt = cnt["V"]

                if "dt" in debug:
                    @block.sync
                    def _(sync, db=db, v_dt=v_dt):
                        w(sync, "V", v_dt)
                        inc("S", sync.dma_start(out=dbg_outs["dt"][db * 128:(db + 1) * 128, :],
                                                in_=dt_sb[:]), 16)

                ab_base = a_ab_last - NST

                for n in range(NST):
                    col = db * 16 + n
                    g = db * NST + n
                    b2 = g % 2
                    b3 = g % 3
                    ab_marks[g] = ab_base + n + 1

                    @block.sync
                    def _(sync, db=db, n=n, g=g, b3=b3):
                        if db == 0 and n == 0:
                            w(sync, "S", s_bc)
                        w(sync, "V", wr_marks.get(g - 3, 0))
                        w(sync, "V", z_marks.get(g - 3, 0))
                        inc("S", sync.dma_start(
                            out=bmc[b3][:],
                            in_=bcast_d[n:n + 1, :].partition_broadcast(128)), 16)
                        inc("S", sync.dma_start(
                            out=bmcC[b3][:],
                            in_=bcast_d[16 + n:17 + n, :].partition_broadcast(128)), 16)
                    bmc_marks[g] = cnt["S"]

                    Vb = cnt["V"]
                    @block.vector
                    def _(vector, db=db, n=n, g=g, b2=b2, b3=b3, col=col):
                        w(vector, "S", bmc_marks[g] + 32)
                        inc("V", vector.tensor_mul(wr_sb[b2][:], xn_fm[db][:],
                                                   bmc[b3][:]))
                        w(vector, "A", ab_marks[g])
                        inc("V", vector.tensor_scalar(bx_sb[b2][:], abar[b2][:],
                                                      -1.0, recipA[:, col:col + 1],
                                                      ALU.add, ALU.mult))
                        inc("V", vector.tensor_mul(bx_sb[b2][:], bx_sb[b2][:],
                                                   wr_sb[b2][:]))
                    wr_marks[g] = Vb + 1
                    bx_marks[g] = cnt["V"]

                    @block.vector
                    def _(vector, g=g, b2=b2):
                        w(vector, "T", zmm_marks.get(g - 2, 0))
                        inc("V", vector.tensor_tensor_scan(
                            h_sb[b2][:], abar[b2][:], bx_sb[b2][:], 0.0,
                            ALU.mult, ALU.add))
                    scan_marks[g] = cnt["V"]

                    @block.vector
                    def _(vector, g=g, b2=b2, b3=b3, n=n):
                        w(vector, "T", zmm_marks.get(g - 2, 0))
                        if n <= 1:
                            w(vector, "T", ty_marks.get(g // NST - 1, 0))
                        inc("V", vector.tensor_mul(z_sb[b2][:], h_sb[b2][:],
                                                   bmcC[b3][:]))
                    z_marks[g] = cnt["V"]

                    @block.tensor
                    def _(tensor, db=db, n=n, g=g, b2=b2):
                        w(tensor, "V", z_marks[g])
                        for hf in range(4):
                            inc("T", tensor.matmul(ps_y[:, hf * 512:(hf + 1) * 512],
                                                   ident[:],
                                                   z_sb[b2][:, hf * 512:(hf + 1) * 512],
                                                   start=(n == 0), stop=False))
                    zmm_marks[g] = cnt["T"]

                if db == ND - 1:
                    @block.sync
                    def _(sync):
                        inc("S", sync.dma_start(out=dum_s[0:1, 0:64],
                                                in_=bcast_d[0:1, 0:64]), 16)
                        inc("S", sync.dma_start(out=dum_s[0:1, 0:64],
                                                in_=bcast_d[0:1, 0:64]), 16)

                @block.vector
                def _(vector, db=db):
                    glast = db * NST + NST - 1
                    w(vector, "T", zmm_marks.get(glast, 0))
                    w(vector, "T", zmm_marks.get(glast - 1, 0))
                    inc("V", vector.tensor_scalar(z_sb[0][:], xn_fm[db][:],
                                                  Dcol[:, db:db + 1], None, ALU.mult))
                    inc("V", vector.scalar_tensor_tensor(z_sb[1][:], xn_fm[db][:],
                                                         Dcol[:, db:db + 1], z_sb[0][:],
                                                         ALU.mult, ALU.subtract))
                v_dxn = cnt["V"]

                @block.tensor
                def _(tensor, db=db, v_dxn=v_dxn):
                    w(tensor, "V", v_dxn)
                    for hf in range(4):
                        inc("T", tensor.matmul(ps_y[:, hf * 512:(hf + 1) * 512],
                                               ident[:], z_sb[0][:, hf * 512:(hf + 1) * 512],
                                               start=False, stop=False))
                    for hf in range(4):
                        inc("T", tensor.matmul(ps_y[:, hf * 512:(hf + 1) * 512],
                                               ident[:], z_sb[1][:, hf * 512:(hf + 1) * 512],
                                               start=False, stop=True))
                t_y = cnt["T"]
                ty_marks[db] = t_y

                @block.scalar
                def _(scalar, db=db, t_y=t_y):
                    w(scalar, "T", t_y)
                    inc("A", scalar.activation(abar[0][:], ps_y[:], AF.Copy))
                a_y = cnt["A"]

                if "y_fm" in debug:
                    @block.sync
                    def _(sync, db=db, a_y=a_y):
                        w(sync, "A", a_y)
                        inc("S", sync.dma_start(out=dbg_outs["y_fm"][db * 128:(db + 1) * 128, :],
                                                in_=abar[0][:]), 16)

                last_add = [0]
                for i in range(NT):
                    pst = ps_yt[i % 2]

                    @block.tensor
                    def _(tensor, db=db, i=i, pst=pst, a_y=a_y):
                        w(tensor, "A", a_y)
                        w(tensor, "V", last_add[0])
                        inc("T", tensor.transpose(pst[:],
                                                  abar[0][:, i * 128:(i + 1) * 128],
                                                  ident32[:]))
                    t_tr = cnt["T"]

                    @block.vector
                    def _(vector, db=db, i=i, pst=pst, t_tr=t_tr):
                        w(vector, "T", t_tr)
                        inc("V", vector.tensor_add(
                            x_sb[:, i * D + db * 128:i * D + db * 128 + 128],
                            x_sb[:, i * D + db * 128:i * D + db * 128 + 128],
                            pst[:]))
                    last_add[0] = cnt["V"]
                add_marks[db] = cnt["V"]
                @block.tensor
                def _(tensor, a_y=a_y):
                    w(tensor, "A", a_y)

            mamba_v = cnt["V"]

            if "x_after_mamba" in debug:
                dbg("x_after_mamba", (T, D))
                @block.sync
                def _(sync):
                    w(sync, "V", mamba_v)
                    for i in range(NT):
                        inc("S", sync.dma_start(
                            out=dbg_outs["x_after_mamba"][i * 128:(i + 1) * 128, :],
                            in_=x_sb[:, i * D:(i + 1) * D]), 16)

            @block.sync
            def _(sync):
                w(sync, "V", mamba_v)
                w(sync, "T", cnt["T"])
                w(sync, "A", cnt["A"])
                w(sync, "G", cnt["G"])


        def emit_moe(tag, ready_v):
            with ExitStack() as fctx:
                FE = fctx.enter_context
                rsc = FE(nc.sbuf_tensor(f"rsc{tag}", [128, 64], F32))
                tk_sb = FE(nc.sbuf_tensor(f"tk{tag}", [128, 2 * NT], F32))
                ee_sb = FE(nc.sbuf_tensor(f"ee{tag}", [128, 2 * NT], F32))
                pos_sb = FE(nc.sbuf_tensor(f"pos{tag}", [128, 2 * NT], F32))
                I1_sb = FE(nc.sbuf_tensor(f"I1{tag}", [128, 16 * NT], F32))
                I2_sb = FE(nc.sbuf_tensor(f"I2{tag}", [128, 16 * NT], F32))
                Irun = FE(nc.sbuf_tensor(f"Irun{tag}", [128, 16], F32))
                poscum = FE(nc.sbuf_tensor(f"pc{tag}", [128, 16], F32))
                sidx = FE(nc.sbuf_tensor(f"sidx{tag}", [128, 2 * NT], I32))
                gidx = FE(nc.sbuf_tensor(f"gidx{tag}", [128, 2 * NT], I32))
                pexp = FE(nc.sbuf_tensor(f"pexp{tag}", [128, 16], F32))
                logits_tm = FE(nc.sbuf_tensor(f"ltm{tag}", [128, 16 * NT], BF16))
                Wr_sb = FE(nc.sbuf_tensor(f"Wr{tag}", [128, ND * 16], BF16))
                eph_stack = ExitStack()
                _EP = eph_stack.enter_context
                xn2 = _EP(nc.sbuf_tensor(f"xn2{tag}", [128, NT * D], BF16))
                lgf32 = _EP(nc.sbuf_tensor(f"lgf32{tag}", [128, 256], F32))
                m8all = _EP(nc.sbuf_tensor(f"m8a{tag}", [128, 128], F32))
                v1a = _EP(nc.sbuf_tensor(f"v1a{tag}", [128, 16], F32))
                v2a = _EP(nc.sbuf_tensor(f"v2a{tag}", [128, 16], F32))
                v2a_raw = _EP(nc.sbuf_tensor(f"v2ar{tag}", [128, 16], F32))
                suba = _EP(nc.sbuf_tensor(f"suba{tag}", [128, 256], F32))
                expa = _EP(nc.sbuf_tensor(f"expa{tag}", [128, 256], F32))
                suma = _EP(nc.sbuf_tensor(f"suma{tag}", [128, 16], F32))
                cuma = _EP(nc.sbuf_tensor(f"cuma{tag}", [128, 256], F32))
                z16 = _EP(nc.sbuf_tensor(f"z16{tag}", [128, 16], F32))
                fscr = _EP(nc.sbuf_tensor(f"fscr{tag}", [128, 16], F32))
                poscumA = _EP(nc.sbuf_tensor(f"pcA{tag}", [128, 256], F32))
                poscumB = _EP(nc.sbuf_tensor(f"pcB{tag}", [128, 256], F32))

                ln_v = emit_layernorm(xn2, ready=("V", ready_v))

                @block.sync
                def _(sync):
                    w(sync, "V", ln_v)
                    for i in range(NT):
                        inc("S", sync.dma_start(out=xn_d[i * 128:(i + 1) * 128, :],
                                                in_=xn2[:, i * D:(i + 1) * D]), 16)
                s_xnd = cnt["S"]

                @block.gpsimd
                def _(gpsimd):
                    w(gpsimd, "V", ready_v)
                    inc("G", gpsimd.dma_start(
                        out=Wr_sb[:].rearrange("p (c e) -> p c e", c=ND),
                        in_=w_router.rearrange("(c p) e -> p c e", p=128)), 16)
                g_wr = cnt["G"]

                with ExitStack() as rctx:
                    RE = rctx.enter_context
                    xnT = [RE(nc.sbuf_tensor(f"xnT{tag}{fd}", [128, T], BF16))
                           for fd in range(ND)]
                    ps_r = RE(nc.psum_tensor(f"ps_r{tag}", [16, T], F32))
                    ps_rt = RE(nc.psum_tensor(f"ps_rt{tag}", [128, 16], BF16))
                    logF = RE(nc.sbuf_tensor(f"logF{tag}", [16, T], BF16))

                    @block.sync
                    def _(sync):
                        w(sync, "S", s_xnd)
                        for fd in range(ND):
                            inc("L", sync.dma_start(
                                out=xnT[fd][:],
                                in_=xn_d[:, fd * 128:(fd + 1) * 128],
                                transpose=True), 16)
                    l_xnT = cnt["L"]

                    @block.tensor
                    def _(tensor):
                        w(tensor, "L", l_xnT)
                        w(tensor, "G", g_wr)
                        for q in range(4):
                            for fd in range(ND):
                                inc("T", tensor.matmul(
                                    ps_r[:, q * 512:(q + 1) * 512],
                                    Wr_sb[:, fd * 16:(fd + 1) * 16],
                                    xnT[fd][:, q * 512:(q + 1) * 512],
                                    start=(fd == 0), stop=(fd == ND - 1)))
                    t_rlog = cnt["T"]

                    @block.scalar
                    def _(scalar):
                        w(scalar, "T", t_rlog)
                        inc("A", scalar.activation(logF[:], ps_r[:], AF.Copy))
                    a_rlog = cnt["A"]

                    lastc = [0]
                    for i in range(NT):
                        @block.tensor
                        def _(tensor, i=i, a_rlog=a_rlog):
                            w(tensor, "A", a_rlog)
                            w(tensor, "V", lastc[0])
                            inc("T", tensor.transpose(
                                ps_rt[:], logF[:, i * 128:(i + 1) * 128],
                                ident[0:16, 0:16]))
                        t_tr = cnt["T"]

                        @block.vector
                        def _(vector, i=i, t_tr=t_tr):
                            w(vector, "T", t_tr)
                            inc("V", vector.tensor_copy(
                                logits_tm[:, i * 16:(i + 1) * 16], ps_rt[:]))
                        lastc[0] = cnt["V"]
                    v_log = cnt["V"]


                def grp(ap3):
                    return ap3.rearrange("p (g c) -> p g c", g=16)

                def bgrp(col16):
                    return col16.unsqueeze(2).broadcast_to([128, 16, 16])

                @block.vector
                def _(vector):
                    def flr():
                        inc("V", vector.tensor_copy(fscr[:], z16[:]))
                    w(vector, "V", v_log)
                    inc("V", vector.memset(z16[:], 0.0))
                    inc("V", vector.tensor_copy(lgf32[:], logits_tm[:]))
                    flr(); flr()
                    for i in range(NT):
                        inc("V", vector.max(m8all[:, 8 * i:8 * i + 8],
                                            lgf32[:, 16 * i:16 * i + 16]))
                    flr(); flr()
                    inc("V", vector.tensor_copy(v1a[:], m8all[:, 0:128:8]))
                    inc("V", vector.tensor_copy(v2a[:], m8all[:, 1:128:8]))
                    inc("V", vector.tensor_copy(v2a_raw[:], m8all[:, 1:128:8]))
                    flr(); flr()
                    inc("V", vector.tensor_tensor(grp(suba[:]), grp(lgf32[:]),
                                                  bgrp(v1a[:]), ALU.subtract))
                    inc("V", vector.tensor_sub(fscr[:], v2a[:], v1a[:]))
                v_sub = cnt["V"]

                @block.scalar
                def _(scalar):
                    w(scalar, "V", v_sub)
                    inc("A", scalar.activation(expa[:], suba[:], AF.Exp))
                    inc("A", scalar.activation(v2a[:], fscr[:], AF.Exp))
                a_exp = cnt["A"]

                @block.vector
                def _(vector):
                    def flr():
                        inc("V", vector.tensor_copy(fscr[:], z16[:]))
                    w(vector, "A", a_exp)
                    inc("V", vector.tensor_reduce(suma[:], grp(expa[:]),
                                                  AX.X, ALU.add))
                    flr(); flr()
                    inc("V", vector.reciprocal(tk_sb[:, 0:NT], suma[:]))
                    flr(); flr()
                    inc("V", vector.tensor_mul(tk_sb[:, NT:2 * NT], v2a[:],
                                               tk_sb[:, 0:NT]))
                    inc("V", vector.tensor_tensor(grp(I1_sb[:]), grp(lgf32[:]),
                                                  bgrp(v1a[:]), ALU.is_equal))
                    flr(); flr()
                    for i in range(NT):
                        inc("V", vector.tensor_tensor_scan(
                            cuma[:, 16 * i:16 * i + 16], I1_sb[:, 16 * i:16 * i + 16],
                            z16[:], 0.0, ALU.add, ALU.add))
                    flr(); flr()
                    inc("V", vector.tensor_scalar(cuma[:], cuma[:], 1.0, None,
                                                  ALU.is_equal))
                    flr(); flr()
                    inc("V", vector.tensor_mul(I1_sb[:], I1_sb[:], cuma[:]))
                    flr(); flr()
                    inc("V", vector.tensor_mul(cuma[:], I1_sb[:], iotat[:]))
                    flr(); flr()
                    inc("V", vector.tensor_reduce(ee_sb[:, 0:NT], grp(cuma[:]),
                                                  AX.X, ALU.add))
                    inc("V", vector.scalar_tensor_tensor(lgf32[:], I1_sb[:], -1e9,
                                                         lgf32[:], ALU.mult, ALU.add))
                    flr(); flr()
                    inc("V", vector.tensor_tensor(grp(I2_sb[:]), grp(lgf32[:]),
                                                  bgrp(v2a_raw[:]), ALU.is_equal))
                    flr(); flr()
                    for i in range(NT):
                        inc("V", vector.tensor_tensor_scan(
                            cuma[:, 16 * i:16 * i + 16], I2_sb[:, 16 * i:16 * i + 16],
                            z16[:], 0.0, ALU.add, ALU.add))
                    flr(); flr()
                    inc("V", vector.tensor_scalar(cuma[:], cuma[:], 1.0, None,
                                                  ALU.is_equal))
                    flr(); flr()
                    inc("V", vector.tensor_mul(I2_sb[:], I2_sb[:], cuma[:]))
                    flr(); flr()
                    inc("V", vector.tensor_mul(cuma[:], I2_sb[:], iotat[:]))
                    flr(); flr()
                    inc("V", vector.tensor_reduce(ee_sb[:, NT:2 * NT], grp(cuma[:]),
                                                  AX.X, ALU.add))
                v_route = cnt["V"]

                with nc.psum_tensor(f"ps_pos{tag}", [128, 16], F32) as ps_pos:
                    for k in range(2):
                        Ik = I1_sb if k == 0 else I2_sb
                        pcX = poscumA if k == 0 else poscumB

                        @block.vector
                        def _(vector, k=k):
                            if k == 0:
                                w(vector, "V", v_route)
                                inc("V", vector.memset(Irun[:], 0.0))
                        for i in range(NT):
                            @block.tensor
                            def _(tensor, k=k, i=i, Ik=Ik):
                                w(tensor, "V", cnt["V"])
                                inc("T", tensor.matmul(ps_pos[:], strictL32[:],
                                                       Ik[:, i * 16:(i + 1) * 16],
                                                       start=True, stop=False))
                                inc("T", tensor.matmul(ps_pos[:], ones128[:],
                                                       Irun[:],
                                                       start=False, stop=True))
                            t_p = cnt["T"]

                            @block.vector
                            def _(vector, k=k, i=i, Ik=Ik, pcX=pcX, t_p=t_p):
                                w(vector, "T", t_p)
                                inc("V", vector.tensor_copy(
                                    pcX[:, i * 16:(i + 1) * 16], ps_pos[:]))
                                inc("V", vector.tensor_add(Irun[:], Irun[:],
                                                           Ik[:, i * 16:(i + 1) * 16]))

                @block.vector
                def _(vector):
                    def flr():
                        inc("V", vector.tensor_copy(fscr[:], z16[:]))
                    flr(); flr()
                    inc("V", vector.tensor_mul(poscumA[:], poscumA[:], I1_sb[:]))
                    inc("V", vector.tensor_mul(poscumB[:], poscumB[:], I2_sb[:]))
                    flr(); flr()
                    inc("V", vector.tensor_reduce(pos_sb[:, 0:NT], grp(poscumA[:]),
                                                  AX.X, ALU.add))
                    inc("V", vector.tensor_reduce(pos_sb[:, NT:2 * NT], grp(poscumB[:]),
                                                  AX.X, ALU.add))
                    flr(); flr()
                    inc("V", vector.scalar_tensor_tensor(
                        poscumA[:, 0:32], ee_sb[:], 256.0, pos_sb[:],
                        ALU.mult, ALU.add))
                    inc("V", vector.tensor_scalar(
                        poscumA[:, 32:64], pos_sb[:], 255.5, None, ALU.is_gt))
                    flr(); flr()
                    inc("V", vector.tensor_scalar(
                        poscumA[:, 64:96], poscumA[:, 0:32], -1.0, 4096.0,
                        ALU.mult, ALU.add))
                    flr(); flr()
                    inc("V", vector.tensor_mul(poscumA[:, 64:96], poscumA[:, 64:96],
                                               poscumA[:, 32:64]))
                    flr(); flr()
                    inc("V", vector.tensor_add(poscumA[:, 0:32], poscumA[:, 0:32],
                                               poscumA[:, 64:96]))
                    flr(); flr()
                    inc("V", vector.tensor_copy(sidx[:], poscumA[:, 0:32]))
                    inc("V", vector.tensor_scalar(
                        poscumB[:, 0:32], pos_sb[:], 255.0, None, ALU.min))
                    flr(); flr()
                    inc("V", vector.scalar_tensor_tensor(
                        poscumB[:, 32:64], ee_sb[:], 256.0, poscumB[:, 0:32],
                        ALU.mult, ALU.add))
                    flr(); flr()
                    inc("V", vector.tensor_copy(gidx[:], poscumB[:, 32:64]))
                v_idx = cnt["V"]

                if f"route{tag}" in debug:
                    dbg(f"route{tag}_tk", (128, 2 * NT), force=True)
                    dbg(f"route{tag}_ee", (128, 2 * NT), force=True)
                    dbg(f"route{tag}_pos", (128, 2 * NT), force=True)
                    dbg(f"route{tag}_sidx", (128, 2 * NT), I32, force=True)
                    dbg(f"route{tag}_gidx", (128, 2 * NT), I32, force=True)
                    dbg(f"route{tag}_log", (128, 16 * NT), BF16, force=True)
                    @block.sync
                    def _(sync):
                        w(sync, "V", v_idx)
                        inc("S", sync.dma_start(out=dbg_outs[f"route{tag}_tk"], in_=tk_sb[:]), 16)
                        inc("S", sync.dma_start(out=dbg_outs[f"route{tag}_ee"], in_=ee_sb[:]), 16)
                        inc("S", sync.dma_start(out=dbg_outs[f"route{tag}_pos"], in_=pos_sb[:]), 16)
                        inc("S", sync.dma_start(out=dbg_outs[f"route{tag}_sidx"], in_=sidx[:]), 16)
                        inc("S", sync.dma_start(out=dbg_outs[f"route{tag}_gidx"], in_=gidx[:]), 16)
                        inc("S", sync.dma_start(out=dbg_outs[f"route{tag}_log"], in_=logits_tm[:]), 16)

                @block.gpsimd
                def _(gpsimd):
                    w(gpsimd, "V", v_idx)
                    w(gpsimd, "G", xtm_zeroed)
                    for k in range(2):
                        for i in range(NT):
                            c = k * NT + i
                            inc("G", gpsimd.indirect_dma_start(
                                out=Xtm_d,
                                out_offset=IndirectOffsetOnAxis(sidx[:, c:c + 1], 0),
                                in_=xn2[:, i * D:(i + 1) * D], in_offset=None), 16)
                g_scat = cnt["G"]
                eph_stack.close()

                with ExitStack() as ectx:
                    EE = ectx.enter_context
                    XT = [EE(nc.sbuf_tensor(f"XT{tag}{fd}", [128, E * CAP], BF16))
                          for fd in range(ND)]
                    Wg_sb = [EE(nc.sbuf_tensor(f"Wg{tag}{b}", [128, 8 * 512], BF16)) for b in range(2)]
                    Wf_sb = [EE(nc.sbuf_tensor(f"Wf{tag}{b}", [128, 8 * 512], BF16)) for b in range(2)]
                    Wp_sb = [EE(nc.sbuf_tensor(f"Wp{tag}{b}", [128, 4 * 1024], BF16)) for b in range(2)]
                    sg_sb = EE(nc.sbuf_tensor(f"sg{tag}", [128, 1024], BF16))
                    hh_sb = EE(nc.sbuf_tensor(f"hh{tag}", [128, 1024], BF16))
                    o_sb = [EE(nc.sbuf_tensor(f"o{tag}{b}", [128, 1024], BF16)) for b in range(2)]
                    ps_g = EE(nc.psum_tensor(f"ps_g{tag}", [128, 1024], F32))
                    ps_f = EE(nc.psum_tensor(f"ps_f{tag}", [128, 1024], F32))
                    ps_o2 = [EE(nc.psum_tensor(f"ps_o{tag}{b}", [128, 1024], F32)) for b in range(2)]

                    @block.sync
                    def _(sync):
                        w(sync, "G", g_scat)
                        for fd in range(ND):
                            inc("L", sync.dma_start(
                                out=XT[fd][:],
                                in_=Xtm_d[0:E * CAP, fd * 128:(fd + 1) * 128],
                                transpose=True), 16)
                    l_XT = cnt["L"]

                    a_prev_sil = [0]
                    v_prev_hh = [0]
                    t_prev_o = [0]
                    t_prev_o_guard = [0]
                    t_gf_mark = {}
                    g_we_marks = {}
                    s_o_marks = {}
                    base_TAV = (cnt["T"], cnt["A"], cnt["V"])
                    for e in range(E + 1):
                        if e < E:
                            @block.gpsimd
                            def _(gpsimd, e=e):
                                eb = e % 2
                                if e < 2:
                                    w(gpsimd, "T", base_TAV[0])
                                    w(gpsimd, "A", base_TAV[1])
                                    w(gpsimd, "V", base_TAV[2])
                                w(gpsimd, "T", t_gf_mark.get(e - 2, 0))
                                w(gpsimd, "T", t_prev_o[0] if e >= 2 else 0)
                                inc("G", gpsimd.dma_start(
                                    out=Wg_sb[e % 2][:], in_=w_gate[e]), 16)
                                inc("G", gpsimd.dma_start(
                                    out=Wf_sb[e % 2][:], in_=w_fc[e]), 16)
                                inc("G", gpsimd.dma_start(
                                    out=Wp_sb[e % 2][:], in_=w_proj[e]), 16)
                            g_we_marks[e] = cnt["G"]
                        if e == 0:
                            continue
                        e = e - 1
                        g_we = g_we_marks[e]

                        T_b = cnt["T"]
                        A_b = cnt["A"]
                        V_b = cnt["V"]
                        t_gf_c = {0: T_b + 32, 1: T_b + 64}

                        @block.tensor
                        def _(tensor, e=e, g_we=g_we, A_b=A_b, V_b=V_b):
                            def gf(C):
                                for ps_x, W_sb in ((ps_g, Wg_sb), (ps_f, Wf_sb)):
                                    for hc in (2 * C, 2 * C + 1):
                                        for c in range(ND):
                                            inc("T", tensor.matmul(
                                                ps_x[:, hc * 256:(hc + 1) * 256],
                                                W_sb[e % 2][:, c * 512 + hc * 128:
                                                     c * 512 + (hc + 1) * 128],
                                                XT[c][:, e * CAP:(e + 1) * CAP],
                                                start=(c == 0),
                                                stop=(c == ND - 1)))

                            def pj(C):
                                tensor.wait_ge(sem["V"], V_b + C + 1)
                                for hc in (2 * C, 2 * C + 1):
                                    for ss in range(2):
                                        for half in range(2):
                                            inc("T", tensor.matmul(
                                                ps_o2[ss][:, half * 512:(half + 1) * 512],
                                                hh_sb[:, hc * 256 + ss * 128:
                                                      hc * 256 + ss * 128 + 128],
                                                Wp_sb[e % 2][:, hc * 1024 + half * 512:
                                                      hc * 1024 + (half + 1) * 512],
                                                start=(hc == 0), stop=(hc == 3)))

                            w(tensor, "G", g_we)
                            if e == 0:
                                w(tensor, "L", l_XT)
                            w(tensor, "A", a_prev_sil[0])
                            w(tensor, "V", v_prev_hh[0])
                            gf(0); gf(1); pj(0); pj(1)
                        t_o_end = cnt["T"]
                        t_gf_mark[e] = t_o_end
                        t_prev_o[0] = t_o_end

                        @block.scalar
                        def _(scalar, e=e, t_gf_c=t_gf_c, t_o_end=t_o_end,
                              som0=s_o_marks.get((e - 1, 0), 0),
                              som1=s_o_marks.get((e - 1, 1), 0)):
                            for C in range(2):
                                scalar.wait_ge(sem["T"], t_gf_c[C])
                                inc("A", scalar.activation(
                                    sg_sb[:, C * 512:(C + 1) * 512],
                                    ps_g[:, C * 512:(C + 1) * 512], AF.Silu))
                            for ss in range(2):
                                scalar.wait_ge(sem["T"], t_o_end)
                                scalar.wait_ge(sem["S"], (som0, som1)[ss])
                                inc("A", scalar.activation(o_sb[ss][:], ps_o2[ss][:],
                                                           AF.Copy))
                        a_sil = cnt["A"] - 2
                        a_prev_sil[0] = a_sil
                        a_oev = cnt["A"]

                        @block.vector
                        def _(vector, e=e, A_b=A_b, tpo=t_prev_o_guard[0]):
                            w(vector, "T", tpo)
                            for C in range(2):
                                vector.wait_ge(sem["A"], A_b + C + 1)
                                inc("V", vector.tensor_mul(
                                    hh_sb[:, C * 512:(C + 1) * 512],
                                    sg_sb[:, C * 512:(C + 1) * 512],
                                    ps_f[:, C * 512:(C + 1) * 512]))
                        v_hh = cnt["V"]
                        v_prev_hh[0] = v_hh
                        t_prev_o_guard[0] = t_o_end

                        @block.sync
                        def _(sync, e=e, a_oev=a_oev):
                            for ss in range(2):
                                sync.wait_ge(sem["A"], a_oev - (1 - ss))
                                inc("S", sync.dma_start(
                                    out=o_d[e * CAP + ss * 128:e * CAP + ss * 128 + 128, :],
                                    in_=o_sb[ss][:]), 16)
                                s_o_marks[(e, ss)] = cnt["S"]
                    s_o = cnt["S"]

                A0 = FE(nc.sbuf_tensor(f"A0{tag}", [128, D], BF16))
                A1 = FE(nc.sbuf_tensor(f"A1{tag}", [128, D], BF16))
                lastcomb = [0]
                for i in range(NT):
                    @block.gpsimd
                    def _(gpsimd, i=i):
                        if i == 0:
                            w(gpsimd, "S", s_o)
                        w(gpsimd, "V", lastcomb[0])
                        inc("G", gpsimd.indirect_dma_start(
                            out=A0[:], out_offset=None, in_=o_d,
                            in_offset=IndirectOffsetOnAxis(gidx[:, i:i + 1], 0)), 16)
                        inc("G", gpsimd.indirect_dma_start(
                            out=A1[:], out_offset=None, in_=o_d,
                            in_offset=IndirectOffsetOnAxis(gidx[:, NT + i:NT + i + 1], 0)), 16)
                    g_ga = cnt["G"]

                    @block.vector
                    def _(vector, i=i, g_ga=g_ga):
                        w(vector, "G", g_ga)
                        inc("V", vector.scalar_tensor_tensor(
                            x_sb[:, i * D:(i + 1) * D], A0[:], tk_sb[:, i:i + 1],
                            x_sb[:, i * D:(i + 1) * D], ALU.mult, ALU.add))
                        inc("V", vector.scalar_tensor_tensor(
                            x_sb[:, i * D:(i + 1) * D], A1[:], tk_sb[:, NT + i:NT + i + 1],
                            x_sb[:, i * D:(i + 1) * D], ALU.mult, ALU.add))
                    lastcomb[0] = cnt["V"]
                moe_v = cnt["V"]

                @block.sync
                def _(sync):
                    w(sync, "V", moe_v)
                    w(sync, "T", cnt["T"])
                    w(sync, "A", cnt["A"])
                    w(sync, "G", cnt["G"])
            return moe_v


        def emit_attn(ready_v):
            COFF = CST["lnW_COFF"]
            with ExitStack() as actx:
                AE = actx.enter_context
                qstg = [AE(nc.sbuf_tensor(f"qstg{b}", [128, QT], BF16)) for b in range(2)]
                KT_sb = [AE(nc.sbuf_tensor(f"KTs{fd}", [128, T], BF16)) for fd in range(ND)]
                Vaug = AE(nc.sbuf_tensor("Vaug", [128, NT * NH * 65], BF16))
                negc = AE(nc.sbuf_tensor("negc", [128, 16], F32))
                nsc = AE(nc.sbuf_tensor("nsc", [128, 32], F32))
                cvec = AE(nc.sbuf_tensor("cvec", [128, D], F32))
                bvb16 = AE(nc.sbuf_tensor("bvb16", [128, 8], BF16))

                with nc.sbuf_tensor("xn3st", [128, 2 * D], BF16) as xn3st:
                    s_xn3 = emit_layernorm_dram(xn_d, xn3st, ready=("V", ready_v))

                @block.vector
                def _(vector):
                    w(vector, "V", ready_v)
                    inc("V", vector.tensor_scalar(nsc[:, 0:8], biases[:, 0:8],
                                                  0.125, None, ALU.mult))

                ps_n_ctx = nc.psum_tensor("ps_n", [16, T], F32)
                ps_n = ps_n_ctx.__enter__()
                with ExitStack() as pctx:
                    PE_ = pctx.enter_context
                    xnTs = [PE_(nc.sbuf_tensor(f"axnT{fd}", [128, QT], BF16))
                            for fd in range(ND)]
                    sq512 = PE_(nc.sbuf_tensor("sq512", [128, QT], BF16))
                    W_cur = PE_(nc.sbuf_tensor("W_cur", [128, ND * D], BF16))
                    ps_pj = PE_(nc.psum_tensor("ps_pj", [128, 1024], F32))

                    qst_marks = {}
                    qst_last = {}
                    for pj, Wsrc in enumerate([Wq, Wk, Wv]):
                        @block.gpsimd
                        def _(gpsimd, pj=pj, Wsrc=Wsrc):
                            w(gpsimd, "T", cnt["T"])
                            inc("G", gpsimd.dma_start(
                                out=W_cur[:], in_=Wsrc), 16)
                        g_w = cnt["G"]

                        for ts in range(4):
                            @block.sync
                            def _(sync, pj=pj, ts=ts):
                                if pj == 0 and ts == 0:
                                    w(sync, "S", s_xn3)
                                w(sync, "T", cnt["T"])
                                for fd in range(ND):
                                    inc("L", sync.dma_start(
                                        out=xnTs[fd][:],
                                        in_=xn_d[ts * QT:(ts + 1) * QT,
                                                 fd * 128:(fd + 1) * 128],
                                        transpose=True), 16)
                            l_xt = cnt["L"]

                            if pj < 2:
                                for od in range(ND):
                                    @block.tensor
                                    def _(tensor, pj=pj, ts=ts, od=od, l_xt=l_xt, g_w=g_w):
                                        w(tensor, "L", l_xt)
                                        w(tensor, "G", g_w)
                                        w(tensor, "A", cnt["A"])
                                        for fd in range(ND):
                                            inc("T", tensor.matmul(
                                                ps_pj[:, 0:QT],
                                                W_cur[:, fd * D + od * 128:
                                                      fd * D + od * 128 + 128],
                                                xnTs[fd][:],
                                                start=(fd == 0), stop=(fd == ND - 1)))
                                    t_pj = cnt["T"]

                                    @block.scalar
                                    def _(scalar, pj=pj, ts=ts, od=od, t_pj=t_pj):
                                        w(scalar, "T", t_pj)
                                        if pj == 0:
                                            w(scalar, "S", qst_last.get(od % 2, 0))
                                            inc("A", scalar.activation(
                                                qstg[od % 2][:], ps_pj[:, 0:QT],
                                                AF.Identity, bias=nsc[:, od:od + 1],
                                                scale=0.125))
                                        else:
                                            inc("A", scalar.activation(
                                                KT_sb[od][:, ts * QT:(ts + 1) * QT],
                                                ps_pj[:, 0:QT], AF.Identity,
                                                bias=biases[:, 8 + od:9 + od]))
                                    a_qe = cnt["A"]

                                    if pj == 0:
                                        @block.sync
                                        def _(sync, ts=ts, od=od, a_qe=a_qe):
                                            w(sync, "A", a_qe)
                                            inc("S", sync.dma_start(
                                                out=qT_d[od * 128:(od + 1) * 128,
                                                         ts * QT:(ts + 1) * QT],
                                                in_=qstg[od % 2][:]), 16)
                                        qst_last[od % 2] = cnt["S"]

                                        @block.scalar
                                        def _(scalar, ts=ts, od=od):
                                            w(scalar, "T", cnt["T"])
                                            inc("A", scalar.activation(
                                                sq512[:], qstg[od % 2][:], AF.Square))
                                        a_sq = cnt["A"]

                                        @block.tensor
                                        def _(tensor, ts=ts, od=od, a_sq=a_sq):
                                            w(tensor, "A", a_sq)
                                            inc("T", tensor.matmul(
                                                ps_n[:, ts * QT:(ts + 1) * QT],
                                                hmaps[:, od * 16:(od + 1) * 16],
                                                sq512[:],
                                                start=(od == 0), stop=(od == ND - 1)))
                            else:
                                for tsub in range(4):
                                    @block.tensor
                                    def _(tensor, ts=ts, tsub=tsub, l_xt=l_xt, g_w=g_w):
                                        w(tensor, "L", l_xt)
                                        w(tensor, "G", g_w)
                                        w(tensor, "A", cnt["A"])
                                        for fd in range(ND):
                                            for half in range(2):
                                                inc("T", tensor.matmul(
                                                    ps_pj[:, half * 512:(half + 1) * 512],
                                                    xnTs[fd][:, tsub * 128:(tsub + 1) * 128],
                                                    W_cur[:, fd * D + half * 512:
                                                          fd * D + (half + 1) * 512],
                                                    start=(fd == 0), stop=(fd == ND - 1)))
                                    t_v = cnt["T"]

                                    @block.scalar
                                    def _(scalar, ts=ts, tsub=tsub, t_v=t_v):
                                        w(scalar, "T", t_v)
                                        kt = ts * 4 + tsub
                                        dst = Vaug[:, kt * NH * 65:(kt + 1) * NH * 65]
                                        inc("A", scalar.activation(
                                            dst.rearrange("p (h c) -> p h c", h=NH)[:, :, 0:64],
                                            ps_pj[:].rearrange("p (h c) -> p h c", c=64),
                                            AF.Identity,
                                            bias=biases[:, 16 + (0):17]
                                            if False else 0.0))
                    t_proj = cnt["T"]
                    a_proj = cnt["A"]

                    @block.vector
                    def _(vector):
                        w(vector, "A", a_proj)
                        inc("V", vector.memset(
                            Vaug[:].rearrange("p (k h c) -> p k h c", k=NT, h=NH)
                            [:, :, :, 64:65], 1.0))
                    v_vones = cnt["V"]

                    with nc.psum_tensor("ps_bv", [1, 1024], F32) as ps_bv:
                        @block.gpsimd
                        def _(gpsimd):
                            w(gpsimd, "T", t_proj)
                            inc("G", gpsimd.dma_start(
                                out=W_cur[:], in_=Wo), 16)
                        g_wo = cnt["G"]

                        @block.vector
                        def _(vector):
                            inc("V", vector.tensor_copy(bvb16[:], biases[:, 16:24]))
                        v_bvb = cnt["V"]

                        @block.tensor
                        def _(tensor, g_wo=g_wo, v_bvb=v_bvb):
                            w(tensor, "G", g_wo)
                            w(tensor, "V", v_bvb)
                            for half in range(2):
                                for fd in range(ND):
                                    inc("T", tensor.matmul(
                                        ps_bv[:, half * 512:(half + 1) * 512],
                                        bvb16[:, fd:fd + 1],
                                        W_cur[:, fd * D + half * 512:
                                              fd * D + (half + 1) * 512],
                                        start=(fd == 0), stop=(fd == ND - 1)))
                        t_bv = cnt["T"]

                        @block.scalar
                        def _(scalar, t_bv=t_bv):
                            w(scalar, "T", t_bv)
                            inc("A", scalar.activation(cvec[0:1, :], ps_bv[:], AF.Copy))
                        a_bv = cnt["A"]

                    @block.gpsimd
                    def _(gpsimd):
                        inc("G", gpsimd.dma_start(
                            out=sq_scr[0:1, :],
                            in_=bo.rearrange("a b -> b a")), 16)
                    g_bo = cnt["G"]

                    @block.vector
                    def _(vector, a_bv=a_bv, g_bo=g_bo):
                        w(vector, "A", a_bv)
                        w(vector, "G", g_bo)
                        inc("V", vector.tensor_add(cvec[0:1, :], cvec[0:1, :],
                                                   sq_scr[0:1, :]))
                    v_cv0 = cnt["V"]

                    @block.sync
                    def _(sync, v_cv0=v_cv0):
                        w(sync, "V", v_cv0)
                        inc("S", sync.dma_start(out=brow_d[0:1, :], in_=cvec[0:1, :]), 16)
                    s_cv = cnt["S"]

                    @block.gpsimd
                    def _(gpsimd, s_cv=s_cv):
                        w(gpsimd, "S", s_cv)
                        inc("G", gpsimd.dma_start(
                            out=cvec[:], in_=brow_d.partition_broadcast(128)), 16)
                    g_cvec = cnt["G"]

                with nc.sbuf_tensor("sqs", [128, T], BF16) as sqs, \
                     nc.sbuf_tensor("nrm", [16, T], F32) as nrm:
                    @block.vector
                    def _(vector):
                        w(vector, "T", cnt["T"])
                        inc("V", vector.tensor_copy(nrm[:], ps_n[:]))
                        inc("V", vector.reduce_max(nsc[0:16, 16:17], nrm[:], axis=AX.X))
                    v_qn = cnt["V"]

                    for fd in range(ND):
                        @block.scalar
                        def _(scalar, fd=fd, v_qn=v_qn):
                            if fd == 0:
                                w(scalar, "V", v_qn)
                            w(scalar, "T", cnt["T"])
                            inc("A", scalar.activation(sqs[:], KT_sb[fd][:], AF.Square))
                        a_sq2 = cnt["A"]

                        @block.tensor
                        def _(tensor, fd=fd, a_sq2=a_sq2):
                            w(tensor, "A", a_sq2)
                            for half in range(4):
                                inc("T", tensor.matmul(
                                    ps_n[:, half * 512:(half + 1) * 512],
                                    hmaps[:, fd * 16:(fd + 1) * 16],
                                    sqs[:, half * 512:(half + 1) * 512],
                                    start=(fd == 0), stop=(fd == ND - 1)))
                    t_kn = cnt["T"]

                    @block.vector
                    def _(vector, t_kn=t_kn):
                        w(vector, "T", t_kn)
                        inc("V", vector.tensor_copy(nrm[:], ps_n[:]))
                        inc("V", vector.reduce_max(nsc[0:16, 17:18], nrm[:], axis=AX.X))
                    v_nrm = cnt["V"]

                    @block.scalar
                    def _(scalar, v_nrm=v_nrm):
                        w(scalar, "V", v_nrm)
                        inc("A", scalar.activation(nsc[0:16, 18:19],
                                                   nsc[0:16, 16:17], AF.Ln))
                        inc("A", scalar.activation(nsc[0:16, 19:20],
                                                   nsc[0:16, 17:18], AF.Ln))
                    a_ln2 = cnt["A"]

                    @block.vector
                    def _(vector, a_ln2=a_ln2):
                        w(vector, "A", a_ln2)
                        inc("V", vector.tensor_add(nsc[0:16, 20:21],
                                                   nsc[0:16, 18:19],
                                                   nsc[0:16, 19:20]))
                    v_ln2 = cnt["V"]

                    @block.scalar
                    def _(scalar, v_ln2=v_ln2):
                        w(scalar, "V", v_ln2)
                        inc("A", scalar.activation(nsc[0:16, 21:22],
                                                   nsc[0:16, 20:21], AF.Exp,
                                                   scale=kvec[0:16, 5:6]))
                    a_c = cnt["A"]

                    @block.vector
                    def _(vector, a_c=a_c):
                        w(vector, "A", a_c)
                        inc("V", vector.tensor_scalar(nsc[0:16, 22:23],
                                                      nsc[0:16, 21:22],
                                                      -0.125, -1.0,
                                                      ALU.mult, ALU.add))
                    v_c = cnt["V"]

                    @block.sync
                    def _(sync, v_c=v_c):
                        w(sync, "V", v_c)
                        inc("S", sync.dma_start(
                            out=brow2_d[0:1, 32:48].rearrange("a b -> b a"),
                            in_=nsc[0:16, 22:23]), 16)
                    s_nc = cnt["S"]

                    @block.gpsimd
                    def _(gpsimd, s_nc=s_nc):
                        w(gpsimd, "S", s_nc)
                        inc("G", gpsimd.dma_start(
                            out=negc[:],
                            in_=brow2_d[0:1, 32:48].partition_broadcast(128)), 16)
                    g_negc_l = cnt["G"]

                    @block.vector
                    def _(vector, g_negc_l=g_negc_l):
                        w(vector, "G", g_negc_l)
                        inc("V", vector.tensor_copy(nsc[:, 24:25], negc[:, 0:1]))
                        inc("V", vector.tensor_copy(nsc[:, 24:25], negc[:, 15:16]))
                    a_negc_v = cnt["V"]

                    @block.gpsimd
                    def _(gpsimd, a_negc_v=a_negc_v):
                        w(gpsimd, "V", a_negc_v)
                        inc("G", gpsimd.dma_start(out=nsc[:, 25:26],
                                                  in_=brow2_d[0:1, 32:33]
                                                  .partition_broadcast(128)), 16)
                    a_negc = cnt["G"]
                ps_n_ctx.__exit__(None, None, None)

                with ExitStack() as mctx2:
                    AE2 = mctx2.enter_context
                    outT = [AE2(nc.sbuf_tensor(f"outT{fd}", [128, QT], BF16))
                            for fd in range(ND)]
                    qsl = [AE2(nc.sbuf_tensor(f"qsl{fd}", [128, QT], BF16))
                           for fd in range(ND)]
                    expS = [AE2(nc.sbuf_tensor(f"expS{b}", [128, 1024], BF16))
                            for b in range(2)]
                    sums2 = AE2(nc.sbuf_tensor("sums2", [1, 1024], F32))
                    dumA = AE2(nc.sbuf_tensor("dumA", [1, 64], F32))
                    bcS = AE2(nc.sbuf_tensor("bcS", [64, 1024], F32))
                    Wo_sb = AE2(nc.sbuf_tensor("Wo_sb", [128, ND * D], BF16))

                    @block.gpsimd
                    def _(gpsimd):
                        w(gpsimd, "T", cnt["T"])
                        inc("G", gpsimd.dma_start(
                            out=Wo_sb[:], in_=Wo), 16)
                    g_wo2 = cnt["G"]

                    t_qk_last = [0]
                    a_exp_mark = {}
                    v_dec_mark = {}
                    t_pv_mark = {}
                    v_norm_mark = [0]
                    g_bc_mark = [0]
                    v_res_mark = [0]
                    t_wo_mark = [0]

                    for qs in range(4):
                        nkt = (qs + 1) * 4

                        @block.gpsimd
                        def _(gpsimd, qs=qs):
                            if qs == 0:
                                w(gpsimd, "S", cnt["S"])
                            else:
                                w(gpsimd, "T", t_qk_last[0])
                            for fd in range(ND):
                                inc("G", gpsimd.dma_start(
                                    out=qsl[fd][:],
                                    in_=qT_d[fd * 128:(fd + 1) * 128,
                                             qs * QT:(qs + 1) * QT]), 16)
                        g_q = cnt["G"]

                        with nc.psum_tensor(f"ps_sA{qs}", [128, 1024], F32) as psA, \
                             nc.psum_tensor(f"ps_sB{qs}", [128, 1024], F32) as psB, \
                             nc.psum_tensor(f"ps_o0{qs}", [65, QT], F32) as ps_o0, \
                             nc.psum_tensor(f"ps_o1{qs}", [65, QT], F32) as ps_o1, \
                             nc.psum_tensor(f"ps_bc{qs}", [64, 1024], F32) as ps_bc:
                            ps_s = [psA, psB]
                            ps_o = [ps_o0, ps_o1]
                            for fd in range(ND):
                                T0, A0, V0 = cnt["T"], cnt["A"], cnt["V"]
                                n_kt = nkt
                                t_qk_c, t_pv_c = {}, {}
                                tc = T0
                                order = [("qk", 0), ("qk", 1)] if n_kt >= 2 else [("qk", 0)]
                                for kt in range(2, n_kt):
                                    order += [("pv", kt - 2), ("qk", kt)]
                                order += [("pv", n_kt - 2), ("pv", n_kt - 1)] if n_kt >= 2 \
                                    else [("pv", 0)]
                                for kind, kt in order:
                                    tc += 2
                                    (t_qk_c if kind == "qk" else t_pv_c)[kt] = tc
                                t_qk_last[0] = t_qk_c[n_kt - 1]
                                a_exp_c = {kt: A0 + 2 * (kt + 1) for kt in range(n_kt)}
                                v_dec_c = {kt: V0 + kt + 1 for kt in range(n_kt)}
                                carryA = [a_exp_mark.get(0, 0), a_exp_mark.get(1, 0)]
                                carryT = [t_pv_mark.get(0, 0), t_pv_mark.get(1, 0)]

                                @block.tensor
                                def _(tensor, qs=qs, fd=fd, n_kt=n_kt, order=order,
                                      a_exp_c=a_exp_c, v_dec_c=v_dec_c,
                                      carryA=carryA, vnp=v_norm_mark[0],
                                      vres0=v_res_mark[0]):
                                    if (qs, fd) == (0, 0):
                                        w(tensor, "G", a_negc)
                                        w(tensor, "V", v_vones)
                                        w(tensor, "G", g_q)
                                    if fd == 0 and qs > 0:
                                        w(tensor, "G", g_q)
                                        w(tensor, "V", vres0)
                                    for kind, kt in order:
                                        b = kt % 2
                                        if kind == "qk":
                                            tensor.wait_ge(sem["A"],
                                                           a_exp_c[kt - 2] if kt >= 2
                                                           else carryA[b])
                                            for sub in range(2):
                                                inc("T", tensor.matmul(
                                                    ps_s[b][:, sub * 512:(sub + 1) * 512],
                                                    KT_sb[fd][sub * 64:sub * 64 + 64,
                                                              kt * 128:(kt + 1) * 128],
                                                    qsl[fd][sub * 64:sub * 64 + 64, :],
                                                    start=True, stop=True,
                                                    tile_position=(sub * 64, 0)))
                                        else:
                                            tensor.wait_ge(sem["V"], v_dec_c[kt])
                                            if kt <= 1:
                                                tensor.wait_ge(sem["V"], vnp)
                                            for sub in range(2):
                                                h = fd * 2 + sub
                                                inc("T", tensor.matmul(
                                                    ps_o[sub][:],
                                                    Vaug[:, kt * NH * 65 + h * 65:
                                                         kt * NH * 65 + (h + 1) * 65],
                                                    expS[b][:, sub * 512:(sub + 1) * 512],
                                                    start=(kt == 0),
                                                    stop=(kt == n_kt - 1)))

                                @block.scalar
                                def _(scalar, qs=qs, fd=fd, n_kt=n_kt,
                                      t_qk_c=t_qk_c, t_pv_c=t_pv_c, carryT=carryT):
                                    for kt in range(n_kt):
                                        b = kt % 2
                                        scalar.wait_ge(sem["T"], t_qk_c[kt])
                                        scalar.wait_ge(sem["T"],
                                                       t_pv_c.get(kt - 2, carryT[b]))
                                        for sub in range(2):
                                            h = fd * 2 + sub
                                            inc("A", scalar.activation(
                                                expS[b][:, sub * 512:(sub + 1) * 512],
                                                ps_s[b][:, sub * 512:(sub + 1) * 512],
                                                AF.Exp, bias=negc[:, h:h + 1]))

                                @block.vector
                                def _(vector, qs=qs, fd=fd, n_kt=n_kt,
                                      a_exp_c=a_exp_c, t_pv_c=t_pv_c, carryT=carryT):
                                    for kt in range(n_kt):
                                        b = kt % 2
                                        vector.wait_ge(sem["A"], a_exp_c[kt])
                                        vector.wait_ge(sem["T"],
                                                       t_pv_c.get(kt - 2, carryT[b]))
                                        off = qs * QT - kt * 128 + QT
                                        inc("V", vector.tensor_tensor(
                                            expS[b][:].rearrange("p (s q) -> p s q", s=2),
                                            expS[b][:].rearrange("p (s q) -> p s q", s=2),
                                            Wdec[:, off:off + 512]
                                            .unsqueeze(1).broadcast_to([128, 2, 512]),
                                            ALU.mult))
                                assert cnt["T"] == tc and cnt["A"] == a_exp_c[n_kt - 1] \
                                    and cnt["V"] == v_dec_c[n_kt - 1]
                                a_exp_mark[0] = a_exp_c[n_kt - 1 if (n_kt - 1) % 2 == 0
                                                        else n_kt - 2]
                                a_exp_mark[1] = a_exp_c[n_kt - 1 if (n_kt - 1) % 2 == 1
                                                        else n_kt - 2]
                                t_pv_mark[0] = t_pv_c[n_kt - 1 if (n_kt - 1) % 2 == 0
                                                      else n_kt - 2]
                                t_pv_mark[1] = t_pv_c[n_kt - 1 if (n_kt - 1) % 2 == 1
                                                      else n_kt - 2]
                                t_pv = cnt["T"]

                                @block.vector
                                def _(vector, qs=qs, fd=fd, t_pv=t_pv):
                                    w(vector, "T", t_pv)
                                    inc("V", vector.reciprocal(sums2[0:1, 0:QT],
                                                               ps_o[0][64:65, :]))
                                    inc("V", vector.reciprocal(sums2[0:1, QT:2 * QT],
                                                               ps_o[1][64:65, :]))
                                v_rs = cnt["V"]

                                @block.tensor
                                def _(tensor, qs=qs, fd=fd, v_rs=v_rs,
                                      vnp=v_norm_mark[0]):
                                    w(tensor, "V", v_rs)
                                    tensor.wait_ge(sem["V"], vnp)
                                    for hf in range(2):
                                        inc("T", tensor.matmul(
                                            ps_bc[:, hf * 512:(hf + 1) * 512],
                                            ones128[0:1, 0:64],
                                            sums2[0:1, hf * 512:(hf + 1) * 512],
                                            start=True, stop=True,
                                            skip_group_check=True))
                                t_bc = cnt["T"]

                                @block.vector
                                def _(vector, qs=qs, fd=fd, t_bc=t_bc):
                                    w(vector, "T", t_bc)
                                    inc("V", vector.tensor_copy(bcS[:], ps_bc[:]))
                                    inc("V", vector.tensor_mul(
                                        outT[fd][0:64, :], ps_o[0][0:64, :],
                                        bcS[:, 0:512]))
                                    inc("V", vector.tensor_mul(
                                        outT[fd][64:128, :], ps_o[1][0:64, :],
                                        bcS[:, 512:1024]))
                                v_norm_mark[0] = cnt["V"]
                            v_heads = cnt["V"]

                        with nc.psum_tensor(f"ps_y2{qs}", [128, 1024], F32) as ps_y:
                            for tsub in range(4):
                                @block.tensor
                                def _(tensor, qs=qs, tsub=tsub, v_heads=v_heads):
                                    w(tensor, "V", v_heads)
                                    if qs == 0 and tsub == 0:
                                        w(tensor, "G", g_wo2)
                                    w(tensor, "V", v_res_mark[0])
                                    for fd in range(ND):
                                        for half in range(2):
                                            inc("T", tensor.matmul(
                                                ps_y[:, half * 512:(half + 1) * 512],
                                                outT[fd][:, tsub * 128:(tsub + 1) * 128],
                                                Wo_sb[:, fd * D + half * 512:
                                                      fd * D + (half + 1) * 512],
                                                start=(fd == 0), stop=(fd == ND - 1)))
                                t_y2 = cnt["T"]
                                t_wo_mark[0] = t_y2

                                @block.vector
                                def _(vector, qs=qs, tsub=tsub, t_y2=t_y2):
                                    w(vector, "T", t_y2)
                                    i = qs * 4 + tsub
                                    inc("V", vector.tensor_add(
                                        x_sb[:, i * D:(i + 1) * D],
                                        x_sb[:, i * D:(i + 1) * D], ps_y[:]))
                                    inc("V", vector.tensor_add(
                                        x_sb[:, i * D:(i + 1) * D],
                                        x_sb[:, i * D:(i + 1) * D], cvec[:]))
                                v_res_mark[0] = cnt["V"]

                    attn_v = cnt["V"]
                    @block.sync
                    def _(sync):
                        w(sync, "V", attn_v)
                        w(sync, "T", cnt["T"])
                        w(sync, "A", cnt["A"])
                        w(sync, "G", cnt["G"])
            return attn_v

        phase_barrier()
        ffn1_v = emit_moe("f1", mamba_v)

        if "x_after_ffn1" in debug:
            dbg("x_after_ffn1", (T, D))
            @block.sync
            def _(sync):
                w(sync, "V", ffn1_v)
                for i in range(NT):
                    inc("S", sync.dma_start(
                        out=dbg_outs["x_after_ffn1"][i * 128:(i + 1) * 128, :],
                        in_=x_sb[:, i * D:(i + 1) * D]), 16)

        phase_barrier()
        attn_v = emit_attn(ffn1_v)

        if "x_after_attn" in debug:
            dbg("x_after_attn", (T, D))
            @block.sync
            def _(sync):
                w(sync, "V", attn_v)
                for i in range(NT):
                    inc("S", sync.dma_start(
                        out=dbg_outs["x_after_attn"][i * 128:(i + 1) * 128, :],
                        in_=x_sb[:, i * D:(i + 1) * D]), 16)

        phase_barrier()
        ffn2_v = emit_moe("f2", attn_v)

        @block.sync
        def _(sync):
            w(sync, "V", cnt["V"])
            for i in range(NT):
                inc("S", sync.dma_start(out=y_out[i * 128:(i + 1) * 128, :],
                                        in_=x_sb[:, i * D:(i + 1) * D]), 16)
            w(sync, "S", cnt["S"])

    return nc, dbg_outs


def _wtile(w):
    M = w.shape[1]
    return np.ascontiguousarray(
        w.reshape(ND, 128, M).transpose(1, 0, 2).reshape(128, ND * M))


def _prep_inputs(inputs, core):
    A = np.ascontiguousarray(inputs["L0_attn_a_A"].reshape(D, NST))
    A_cols = np.zeros((128, 128), dtype=np.float32)
    for db in range(ND):
        A_cols[:, db * 16:(db + 1) * 16] = A[db * 128:(db + 1) * 128, :]
    m = {
        "x": np.ascontiguousarray(inputs["L0_attn_a_x"][core]),
        "W_in": inputs["L0_attn_a_W_in"],
        "b_in": np.ascontiguousarray(inputs["L0_attn_a_b_in"].reshape(32, 1)),
        "W_dt1": inputs["L0_attn_a_W_dt1"],
        "b_dt1": np.ascontiguousarray(inputs["L0_attn_a_b_dt1"].reshape(64, 1)),
        "W_dt2": inputs["L0_attn_a_W_dt2"],
        "b_dt2": np.ascontiguousarray(inputs["L0_attn_a_b_dt2"].reshape(D, 1)),
        "A_cols": A_cols,
        "D_row": np.ascontiguousarray(inputs["L0_attn_a_D"].reshape(1, D)),
        "Wq": _wtile(inputs["L0_attn_b_Wq"]), "bq": np.ascontiguousarray(inputs["L0_attn_b_bq"].reshape(D, 1)),
        "Wk": _wtile(inputs["L0_attn_b_Wk"]), "bk": np.ascontiguousarray(inputs["L0_attn_b_bk"].reshape(D, 1)),
        "Wv": _wtile(inputs["L0_attn_b_Wv"]), "bv": np.ascontiguousarray(inputs["L0_attn_b_bv"].reshape(D, 1)),
        "Wo": _wtile(inputs["L0_attn_b_Wo"]), "bo": np.ascontiguousarray(inputs["L0_attn_b_bo"].reshape(D, 1)),
        "w_router": inputs["L0_ffn_w_router"],
        "w_fc": np.ascontiguousarray(
            inputs["L0_ffn_w_fc"].reshape(E, ND, 128, HID)
            .transpose(0, 2, 1, 3).reshape(E, 128, ND * HID)),
        "w_gate": np.ascontiguousarray(
            inputs["L0_ffn_w_gate"].reshape(E, ND, 128, HID)
            .transpose(0, 2, 1, 3).reshape(E, 128, ND * HID)),
        "w_proj": np.ascontiguousarray(
            inputs["L0_ffn_w_proj"].reshape(E, 4, 128, D)
            .transpose(0, 2, 1, 3).reshape(E, 128, 4 * D)),
        "ident": CST["ident"],
        "ident32": CST["ident32"],
        "strictL": CST["strictL"],
        "ones1x128": CST["ones1x128"],
        "ones128sq": CST["ones128sq"],
        "strictL32": CST["strictL32"],
        "iota16": CST["iota16"],
        "iotat": CST["iotat"],
        "lnW": CST["lnW"],
        "Wdecay": CST["Wdecay"],
        "hmaps": CST["hmaps"],
        "kvec": CST["kvec"],
    }
    return m


def kernel(**inputs):
    nc, _ = build_program()
    in_maps = [_prep_inputs(inputs, core) for core in range(8)]
    res = run_bass_kernel_spmd(nc, in_maps, core_ids=list(range(8)))
    return np.stack([r["y"] for r in res.results], axis=0)


# version 15
# speedup vs baseline: 1.0058x; 1.0058x over previous
import numpy as np
import ml_dtypes
from contextlib import ExitStack

import concourse.bass as bass
import concourse.mybir as mybir
from concourse.bass_utils import run_bass_kernel_spmd
from concourse.bass import IndirectOffsetOnAxis

F32 = mybir.dt.float32
BF16 = mybir.dt.bfloat16
I32 = mybir.dt.int32
AF = mybir.ActivationFunctionType
ALU = mybir.AluOpType
AX = mybir.AxisListType

T = 2048
D = 1024
NT = T // 128
ND = D // 128
NH = 16
DK = 64
NST = 16
E = 16
CAP = 256
HID = 512
QT = 512
DECAY = 0.25
NEG = -1e9
TRASH = E * CAP

BF = ml_dtypes.bfloat16


def _host_constants():
    c = {}
    c["ident"] = np.eye(128, dtype=np.float32).astype(BF)
    c["ident32"] = np.eye(128, dtype=np.float32)
    c["strictL"] = np.triu(np.ones((128, 128), dtype=np.float32), 1).astype(BF)
    c["ones1x128"] = np.ones((1, 128), dtype=np.float32).astype(BF)
    c["ones128sq"] = np.ones((128, 128), dtype=np.float32)
    c["strictL32"] = np.triu(np.ones((128, 128), dtype=np.float32), 1)
    c["iota16"] = np.tile(np.arange(16, dtype=np.float32), (128, 1))
    c["iotat"] = np.tile(np.arange(16, dtype=np.float32), (128, 16))
    COFF = QT
    width = (T - 128) + COFF + QT
    j = np.arange(width)[None, :]
    p = np.arange(128)[:, None]
    d = j - COFF - p
    lnw = np.where(d < 0, NEG, np.where(d == 0, 0.0,
                   -DECAY * np.log(np.maximum(d, 1)))).astype(np.float32)
    c["lnW"] = lnw.astype(BF)
    c["Wdecay"] = np.where(d < 0, 0.0, np.exp(np.where(d <= 0, 0.0,
                  -DECAY * np.log(np.maximum(d, 1))))).astype(np.float32).astype(BF)
    c["lnW_COFF"] = COFF
    maps = np.zeros((ND, 128, 16), dtype=np.float32)
    for fd in range(ND):
        maps[fd, 0:64, 2 * fd] = 1.0
        maps[fd, 64:128, 2 * fd + 1] = 1.0
    c["hmaps"] = maps.astype(BF)
    kv = np.zeros((128, 8), dtype=np.float32)
    kv[:, 0] = 1e-5; kv[:, 1] = 1.0; kv[:, 2] = -0.5; kv[:, 3] = -1.0 / D; kv[:, 4] = -1.0; kv[:, 5] = 0.5
    c["kvec"] = kv
    return c


CST = _host_constants()


def build_program(debug=()):
    nc = bass.Bass("TRN2", target_bir_lowering=False,
                   detect_race_conditions=False)

    din = {}
    def dram_in(name, shape, dtype=F32):
        din[name] = nc.dram_tensor(name, list(shape), dtype, kind="ExternalInput").ap()
        return din[name]

    x_in = dram_in("x", (T, D))
    W_in_ = dram_in("W_in", (D, 32))
    b_in_ = dram_in("b_in", (32, 1))
    W_dt1 = dram_in("W_dt1", (D, 64))
    b_dt1 = dram_in("b_dt1", (64, 1))
    W_dt2 = dram_in("W_dt2", (64, D))
    b_dt2 = dram_in("b_dt2", (D, 1))
    A_cols = dram_in("A_cols", (128, 128))
    D_row = dram_in("D_row", (1, D))
    Wq = dram_in("Wq", (128, ND * D)); bq = dram_in("bq", (D, 1))
    Wk = dram_in("Wk", (128, ND * D)); bk = dram_in("bk", (D, 1))
    Wv = dram_in("Wv", (128, ND * D)); bv = dram_in("bv", (D, 1))
    Wo = dram_in("Wo", (128, ND * D)); bo = dram_in("bo", (D, 1))
    w_router = dram_in("w_router", (D, 16))
    w_fc = dram_in("w_fc", (E, 128, ND * HID))
    w_gate = dram_in("w_gate", (E, 128, ND * HID))
    w_proj = dram_in("w_proj", (E, 128, 4 * D))
    ident_in = dram_in("ident", (128, 128), BF16)
    ident32_in = dram_in("ident32", (128, 128))
    strictL_in = dram_in("strictL", (128, 128), BF16)
    ones1x128_in = dram_in("ones1x128", (1, 128), BF16)
    ones128_in = dram_in("ones128sq", (128, 128))
    strictL32_in = dram_in("strictL32", (128, 128))
    iota16_in = dram_in("iota16", (128, 16))
    iotat_in = dram_in("iotat", (128, 256))
    lnW_in = dram_in("lnW", CST["lnW"].shape, BF16)
    Wdec_in = dram_in("Wdecay", CST["Wdecay"].shape, BF16)
    hmaps_in = dram_in("hmaps", (ND, 128, 16), BF16)
    kvec_in = dram_in("kvec", (128, 8))

    y_out = nc.dram_tensor("y", [T, D], F32, kind="ExternalOutput").ap()

    dbg_outs = {}
    def dbg(name, shape, dtype=F32, force=False):
        if name in debug or force:
            dbg_outs[name] = nc.dram_tensor("dbg_" + name, list(shape), dtype,
                                            kind="ExternalOutput").ap()
        return dbg_outs.get(name)

    x_d = nc.dram_tensor("x_spill", [T, D], F32, kind="Internal").ap()
    xn_d = nc.dram_tensor("xn_d", [T, D], BF16, kind="Internal").ap()
    Xtm_d = nc.dram_tensor("Xtm_d", [33 * 128, D], BF16, kind="Internal").ap()
    o_d = nc.dram_tensor("o_d", [E * CAP, D], BF16, kind="Internal").ap()
    bcast_d = nc.dram_tensor("bcast_d", [64, T], BF16, kind="Internal").ap()
    brow_d = nc.dram_tensor("brow_d", [1, D], F32, kind="Internal").ap()
    brow2_d = nc.dram_tensor("brow2_d", [1, 64], F32, kind="Internal").ap()
    srow_d = nc.dram_tensor("srow_d", [1, 1024], F32, kind="Internal").ap()
    dscr_d = nc.dram_tensor("dscr_d", [64, 1024], F32, kind="Internal").ap()
    qT_d = nc.dram_tensor("qT_d", [D, T], BF16, kind="Internal").ap()

    ctx = ExitStack()
    with ctx:
        EA = ctx.enter_context

        x_sb = EA(nc.sbuf_tensor("x_sb", [128, NT * D], F32))
        ident = EA(nc.sbuf_tensor("ident_s", [128, 128], BF16))
        ident32 = EA(nc.sbuf_tensor("ident32_s", [128, 128], F32))
        strictL = EA(nc.sbuf_tensor("strictL_s", [128, 128], BF16))
        ones1x128 = EA(nc.sbuf_tensor("ones1x128_s", [1, 128], BF16))
        ones128 = EA(nc.sbuf_tensor("ones128sq_s", [128, 128], F32))
        strictL32 = EA(nc.sbuf_tensor("strictL32_s", [128, 128], F32))
        zero_bf = EA(nc.sbuf_tensor("zero_bf", [128, 512], BF16))
        iotat = EA(nc.sbuf_tensor("iotat_s", [128, 256], F32))
        Wdec = EA(nc.sbuf_tensor("Wdec_s", list(CST["Wdecay"].shape), BF16))
        hmaps = EA(nc.sbuf_tensor("hmaps_s", [128, ND * 16], BF16))
        A_sb = EA(nc.sbuf_tensor("A_sb", [128, 128], F32))
        recipA = EA(nc.sbuf_tensor("recipA", [128, 128], F32))
        Dcol = EA(nc.sbuf_tensor("Dcol", [128, ND], F32))
        biases = EA(nc.sbuf_tensor("biases", [128, 48], F32))
        stats = EA(nc.sbuf_tensor("stats", [128, 16 * NT], F32))
        statsL = EA(nc.sbuf_tensor("statsL", [128, 2 * NT], F32))
        sq_scr = EA(nc.sbuf_tensor("sq_scr", [128, D], F32))
        kvec = EA(nc.sbuf_tensor("kvec_s", [128, 8], F32))

        block = EA(nc.Block())

        sem = {}
        cnt = {}
        for name in ["L", "T", "V", "A", "G", "S", "Q"]:
            sem[name] = EA(nc.semaphore("sem" + name))
            cnt[name] = 0

        def inc(name, instr, k=1):
            instr.then_inc(sem[name], k)
            cnt[name] += k
            return cnt[name]

        def w(eng, name, val=None):
            eng.wait_ge(sem[name], cnt[name] if val is None else val)

        def phase_barrier():
            snap = dict(cnt)
            for blk in [block.tensor, block.vector, block.scalar,
                        block.gpsimd, block.sync]:
                @blk
                def _(eng):
                    for k in ["T", "V", "A", "G", "S", "L"]:
                        eng.wait_ge(sem[k], snap[k])

        @block.sync
        def _(sync):
            for i in range(NT):
                inc("L", sync.dma_start(out=x_sb[:, i * D:(i + 1) * D],
                                        in_=x_in[i * 128:(i + 1) * 128, :]), 16)
            inc("L", sync.dma_start(out=ident[:], in_=ident_in), 16)
            inc("L", sync.dma_start(out=ident32[:], in_=ident32_in), 16)
            inc("L", sync.dma_start(out=strictL[:], in_=strictL_in), 16)
            inc("L", sync.dma_start(out=ones1x128[:], in_=ones1x128_in), 16)
            inc("L", sync.dma_start(out=ones128[:], in_=ones128_in), 16)
            inc("L", sync.dma_start(out=strictL32[:], in_=strictL32_in), 16)
            inc("L", sync.dma_start(out=iotat[:], in_=iotat_in), 16)
            inc("L", sync.dma_start(out=Wdec[:], in_=Wdec_in), 16)
            for fd in range(ND):
                inc("L", sync.dma_start(out=hmaps[:, fd * 16:(fd + 1) * 16],
                                        in_=hmaps_in[fd]), 16)
            inc("L", sync.dma_start(out=A_sb[:], in_=A_cols), 16)
            for fd in range(ND):
                inc("L", sync.dma_start(out=Dcol[:, fd:fd + 1],
                                        in_=D_row[0:1, fd * 128:(fd + 1) * 128]
                                        .rearrange("a b -> b a")), 16)
            for j, b in enumerate([bq, bk, bv, bo, b_dt2]):
                for fd in range(ND):
                    inc("L", sync.dma_start(out=biases[:, 8 * j + fd:8 * j + fd + 1],
                                            in_=b[fd * 128:(fd + 1) * 128, :]), 16)
            inc("L", sync.dma_start(out=biases[0:32, 40:41], in_=b_in_), 16)
            inc("L", sync.dma_start(out=biases[0:64, 41:42], in_=b_dt1), 16)
            inc("L", sync.dma_start(out=kvec[:], in_=kvec_in), 16)
        loads0 = cnt["L"]

        @block.vector
        def _(vector):
            w(vector, "L", loads0)
            inc("V", vector.reciprocal(recipA[:], A_sb[:]))
            inc("V", vector.memset(zero_bf[:], 0.0))
        recipA_done = cnt["V"]

        @block.gpsimd
        def _(gpsimd):
            w(gpsimd, "V", recipA_done)
            for r in range(33):
                for hf in range(2):
                    inc("G", gpsimd.dma_start(
                        out=Xtm_d[r * 128:(r + 1) * 128, hf * 512:(hf + 1) * 512],
                        in_=zero_bf[:]), 16)
        xtm_zeroed = cnt["G"]

        def emit_layernorm(xn_sb, ready=None, stream_dram=None):
            @block.vector
            def _(vector):
                if ready is not None:
                    w(vector, ready[0], ready[1])
                for i in range(NT):
                    inc("V", vector.bn_stats(stats[:, 12 * i:12 * i + 6],
                                             x_sb[:, i * D:i * D + 512]))
                    inc("V", vector.bn_stats(stats[:, 12 * i + 6:12 * i + 12],
                                             x_sb[:, i * D + 512:(i + 1) * D]))
                vector.wait_ge(sem["V"], cnt["V"])
                for i in range(NT):
                    inc("V", vector.bn_aggr(stats[:, 12 * NT + 2 * i:12 * NT + 2 * i + 2],
                                            stats[:, 12 * i:12 * i + 12]))
            v_bn = cnt["V"]

            @block.scalar
            def _(scalar):
                w(scalar, "V", v_bn)
                w(scalar, "L", loads0)
                inc("A", scalar.activation(statsL[:, 0:NT],
                                           stats[:, 12 * NT:14 * NT:2],
                                           AF.Identity, scale=kvec[:, 4:5]))
                inc("A", scalar.activation(stats[:, 15 * NT:16 * NT],
                                           stats[:, 12 * NT + 1:14 * NT:2],
                                           AF.Ln, bias=kvec[:, 0:1]))
                inc("A", scalar.activation(statsL[:, NT:2 * NT],
                                           stats[:, 15 * NT:16 * NT],
                                           AF.Exp, scale=kvec[:, 2:3]))
            a_rsig = cnt["A"]

            @block.vector
            def _(vector):
                w(vector, "A", a_rsig)
            for i in range(NT):
                @block.vector
                def _(vector, i=i):
                    inc("V", vector.tensor_tensor(
                        sq_scr[:],
                        x_sb[:, i * D:(i + 1) * D],
                        statsL[:, i:i + 1].broadcast_to([128, D]),
                        ALU.add))
                    inc("V", vector.tensor_tensor(
                        xn_sb[:, i * D:(i + 1) * D],
                        sq_scr[:],
                        statsL[:, NT + i:NT + i + 1].broadcast_to([128, D]),
                        ALU.mult))
                v_tile = cnt["V"]
                if stream_dram is not None:
                    @block.sync
                    def _(sync, i=i, v_tile=v_tile):
                        w(sync, "V", v_tile)
                        inc("S", sync.dma_start(
                            out=stream_dram[i * 128:(i + 1) * 128, :],
                            in_=xn_sb[:, i * D:(i + 1) * D]), 16)
            return cnt["V"]

        def emit_layernorm_dram(dram_out, stage_sb, ready=None):
            @block.vector
            def _(vector):
                if ready is not None:
                    w(vector, ready[0], ready[1])
                for i in range(NT):
                    inc("V", vector.bn_stats(stats[:, 12 * i:12 * i + 6],
                                             x_sb[:, i * D:i * D + 512]))
                    inc("V", vector.bn_stats(stats[:, 12 * i + 6:12 * i + 12],
                                             x_sb[:, i * D + 512:(i + 1) * D]))
                vector.wait_ge(sem["V"], cnt["V"])
                for i in range(NT):
                    inc("V", vector.bn_aggr(stats[:, 12 * NT + 2 * i:12 * NT + 2 * i + 2],
                                            stats[:, 12 * i:12 * i + 12]))
            v_bn = cnt["V"]

            @block.scalar
            def _(scalar):
                w(scalar, "V", v_bn)
                inc("A", scalar.activation(statsL[:, 0:NT],
                                           stats[:, 12 * NT:14 * NT:2],
                                           AF.Identity, scale=kvec[:, 4:5]))
                inc("A", scalar.activation(stats[:, 15 * NT:16 * NT],
                                           stats[:, 12 * NT + 1:14 * NT:2],
                                           AF.Ln, bias=kvec[:, 0:1]))
                inc("A", scalar.activation(statsL[:, NT:2 * NT],
                                           stats[:, 15 * NT:16 * NT],
                                           AF.Exp, scale=kvec[:, 2:3]))
            a_rsig = cnt["A"]

            st_marks = {}
            @block.vector
            def _(vector):
                w(vector, "A", a_rsig)
            for i in range(NT):
                @block.vector
                def _(vector, i=i):
                    if i >= 2:
                        w(vector, "S", st_marks[i - 2])
                    inc("V", vector.tensor_tensor(
                        sq_scr[:], x_sb[:, i * D:(i + 1) * D],
                        statsL[:, i:i + 1].broadcast_to([128, D]), ALU.add))
                    inc("V", vector.tensor_tensor(
                        stage_sb[:, (i % 2) * D:(i % 2 + 1) * D], sq_scr[:],
                        statsL[:, NT + i:NT + i + 1].broadcast_to([128, D]),
                        ALU.mult))
                v_ap = cnt["V"]

                @block.sync
                def _(sync, i=i, v_ap=v_ap):
                    w(sync, "V", v_ap)
                    inc("S", sync.dma_start(out=dram_out[i * 128:(i + 1) * 128, :],
                                            in_=stage_sb[:, (i % 2) * D:(i % 2 + 1) * D]), 16)
                st_marks[i] = cnt["S"]
            return cnt["S"]

        def emit_tm_to_fm(src_sb, dst_fm, ps_pair, ready_v):
            last_copy = [ready_v]
            for k, (fd, i) in enumerate([(fd, i) for fd in range(ND) for i in range(NT)]):
                pst = ps_pair[k % 2]

                @block.tensor
                def _(tensor, fd=fd, i=i, pst=pst, k=k):
                    w(tensor, "V", last_copy[0])
                    inc("T", tensor.transpose(
                        pst[:], src_sb[:, i * D + fd * 128:i * D + fd * 128 + 128],
                        ident[:]))
                t_tr = cnt["T"]

                @block.vector
                def _(vector, fd=fd, i=i, pst=pst, t_tr=t_tr):
                    w(vector, "T", t_tr)
                    inc("V", vector.tensor_copy(dst_fm[fd][:, i * 128:(i + 1) * 128],
                                                pst[:]))
                last_copy[0] = cnt["V"]
            return cnt["V"]

        with ExitStack() as mctx:
            ME = mctx.enter_context
            xn_fm = [ME(nc.sbuf_tensor(f"xn_fm{fd}", [128, T], BF16)) for fd in range(ND)]
            ps_p_ctx = nc.psum_tensor("ps_p", [64, T], F32)
            ps_p = ps_p_ctx.__enter__()
            Win_sb = ME(nc.sbuf_tensor("Win_sb", [128, ND * 32], BF16))
            Wdt1_sb = ME(nc.sbuf_tensor("Wdt1_sb", [128, ND * 64], BF16))
            Wdt2_sb = ME(nc.sbuf_tensor("Wdt2_sb", [64, D], BF16))
            BCu = ME(nc.sbuf_tensor("BCu", [32, T], BF16))
            u_sb = ME(nc.sbuf_tensor("u_sb", [64, T], BF16))
            with nc.sbuf_tensor("xn_tm", [128, NT * D], BF16) as xn_tm, \
                 nc.psum_tensor("ps_trA", [128, 128], BF16) as ps_trA, \
                 nc.psum_tensor("ps_trB", [128, 128], BF16) as ps_trB:
                ln1_v = emit_layernorm(xn_tm, ready=("L", loads0))
                if "lnstats" in debug:
                    dbg("lnstats", (128, 2 * NT))
                    @block.sync
                    def _(sync):
                        w(sync, "V", ln1_v)
                        inc("S", sync.dma_start(out=dbg_outs["lnstats"], in_=statsL[:]), 16)
                if "xn_tm" in debug:
                    dbg("xn_tm", (128, NT * D), BF16)
                    @block.sync
                    def _(sync):
                        w(sync, "V", ln1_v)
                        inc("S", sync.dma_start(out=dbg_outs["xn_tm"], in_=xn_tm[:]), 16)
                tr1_v = emit_tm_to_fm(xn_tm, xn_fm, [ps_trA, ps_trB], ln1_v)

            if "xn_fm" in debug:
                dbg("xn_fm", (D, T), BF16)
                @block.sync
                def _(sync):
                    w(sync, "V", tr1_v)
                    for fd in range(ND):
                        inc("S", sync.dma_start(out=dbg_outs["xn_fm"][fd * 128:(fd + 1) * 128, :],
                                                in_=xn_fm[fd][:]), 16)

            @block.gpsimd
            def _(gpsimd):
                for fd in range(ND):
                    inc("G", gpsimd.dma_start(out=Win_sb[:, fd * 32:(fd + 1) * 32],
                                              in_=W_in_[fd * 128:(fd + 1) * 128, :]), 16)
                    inc("G", gpsimd.dma_start(out=Wdt1_sb[:, fd * 64:(fd + 1) * 64],
                                              in_=W_dt1[fd * 128:(fd + 1) * 128, :]), 16)
                inc("G", gpsimd.dma_start(out=Wdt2_sb[:], in_=W_dt2), 16)
            mw_l = cnt["G"]

            if "Win" in debug:
                dbg("Win", (128, ND * 32), BF16)
                @block.sync
                def _(sync):
                    w(sync, "G", mw_l)
                    inc("S", sync.dma_start(out=dbg_outs["Win"], in_=Win_sb[:]), 16)


            @block.tensor
            def _(tensor):
                w(tensor, "G", mw_l)
                w(tensor, "V", tr1_v)
                for hf in range(4):
                    for fd in range(ND):
                        inc("T", tensor.matmul(ps_p[0:32, hf * 512:(hf + 1) * 512],
                                               Win_sb[:, fd * 32:(fd + 1) * 32],
                                               xn_fm[fd][:, hf * 512:(hf + 1) * 512],
                                               start=(fd == 0), stop=(fd == ND - 1)))
            t_bc = cnt["T"]

            @block.scalar
            def _(scalar):
                w(scalar, "A", 0)
                w(scalar, "T", t_bc)
                inc("A", scalar.activation(BCu[0:32, :], ps_p[0:32, :], AF.Identity,
                                           bias=biases[0:32, 40:41]))
            a_bc = cnt["A"]

            @block.tensor
            def _(tensor):
                w(tensor, "A", a_bc)
                for hf in range(4):
                    for fd in range(ND):
                        inc("T", tensor.matmul(ps_p[:, hf * 512:(hf + 1) * 512],
                                               Wdt1_sb[:, fd * 64:(fd + 1) * 64],
                                               xn_fm[fd][:, hf * 512:(hf + 1) * 512],
                                               start=(fd == 0), stop=(fd == ND - 1)))
            t_u = cnt["T"]

            @block.scalar
            def _(scalar):
                w(scalar, "T", t_u)
                inc("A", scalar.activation(u_sb[:], ps_p[:], AF.Identity,
                                           bias=biases[0:64, 41:42]))
            a_u = cnt["A"]

            @block.sync
            def _(sync):
                w(sync, "A", a_u)
                inc("S", sync.dma_start(out=bcast_d[0:32, :], in_=BCu[:]), 16)
            s_bc = cnt["S"]

            if "BCu" in debug:
                dbg("BCu", (96, T), BF16)
                @block.sync
                def _(sync):
                    w(sync, "A", a_u)
                    inc("S", sync.dma_start(out=dbg_outs["BCu"][0:32, :], in_=BCu[:]), 16)
                    inc("S", sync.dma_start(out=dbg_outs["BCu"][32:96, :], in_=u_sb[:]), 16)

            ps_p_ctx.__exit__(None, None, None)

            dt_sb = ME(nc.sbuf_tensor("dt_sb", [128, T], BF16))
            e_scr = ME(nc.sbuf_tensor("e_scr", [128, 1024], F32))
            abar = [ME(nc.sbuf_tensor(f"abar{i}", [128, T], F32)) for i in range(2)]
            wr_sb = [ME(nc.sbuf_tensor(f"wr_sb{i}", [128, T], BF16)) for i in range(2)]
            dum_s = ME(nc.sbuf_tensor("dum_s", [1, 64], BF16))
            bx_sb = [ME(nc.sbuf_tensor(f"bx_sb{i}", [128, T], BF16)) for i in range(2)]
            h_sb = [ME(nc.sbuf_tensor(f"h_sb{i}", [128, T], BF16)) for i in range(2)]
            z_sb = [ME(nc.sbuf_tensor(f"z_sb{i}", [128, T], BF16)) for i in range(2)]
            bmc = [ME(nc.sbuf_tensor(f"bmc{i}", [128, T], BF16)) for i in range(3)]
            bmcC = [ME(nc.sbuf_tensor(f"bmcC{i}", [128, T], BF16)) for i in range(3)]
            ps_dt = ME(nc.psum_tensor("ps_dt", [128, 1024], F32))
            ps_y = ME(nc.psum_tensor("ps_y", [128, T], F32))
            ps_yt = [ME(nc.psum_tensor(f"ps_yt{i}", [128, 128], F32)) for i in range(2)]

            bmc_marks, ab_marks, wr_marks, bx_marks = {}, {}, {}, {}
            scan_marks, z_marks, zmm_marks, ty_marks, add_marks = {}, {}, {}, {}, {}
            v_scan_prev = [0]

            if "dt" in debug:
                dbg("dt", (D, T), BF16)
            if "y_fm" in debug:
                dbg("y_fm", (D, T))
            if "abar0" in debug:
                dbg("abar0", (128, T))

            for db in range(ND):
                T0, A0, V0g = cnt["T"], cnt["A"], cnt["V"]

                @block.tensor
                def _(tensor, db=db, A0=A0):
                    if db == 0:
                        w(tensor, "A", a_u)
                    for hf in range(2):
                        if hf == 1:
                            tensor.wait_ge(sem["A"], A0 + 1)
                        else:
                            tensor.wait_ge(sem["A"], A0)
                        for qq in range(2):
                            inc("T", tensor.matmul(
                                ps_dt[:, qq * 512:(qq + 1) * 512],
                                Wdt2_sb[:, db * 128:(db + 1) * 128],
                                u_sb[:, hf * 1024 + qq * 512:hf * 1024 + (qq + 1) * 512],
                                start=True, stop=True))

                @block.scalar
                def _(scalar, db=db, T0=T0, A0=A0, V0g=V0g):
                    for hf in range(2):
                        scalar.wait_ge(sem["T"], T0 + 2 * (hf + 1))
                        inc("A", scalar.activation(e_scr[:], ps_dt[:], AF.Exp,
                                                   bias=biases[:, 32 + db:33 + db]))
                        inc("A", scalar.activation(dt_sb[:, hf * 1024:(hf + 1) * 1024],
                                                   e_scr[:], AF.Ln, bias=kvec[:, 1:2]))
                    scalar.wait_ge(sem["V"], V0g + 1)
                    for n in range(NST):
                        col = db * 16 + n
                        if n < 2:
                            scalar.wait_ge(sem["V"], add_marks.get(db - 1, 0))
                        else:
                            scalar.wait_ge(sem["V"], V0g + 1 + 5 * (n - 2) + 4)
                        inc("A", scalar.activation(abar[n % 2][:], dt_sb[:], AF.Exp,
                                                   scale=A_sb[:, col:col + 1]))
                a_ab_last = cnt["A"]

                @block.vector
                def _(vector, db=db, A0=A0):
                    w(vector, "A", A0 + 4)
                    inc("V", vector.tensor_scalar(dt_sb[:], dt_sb[:], 1e-7, 1e6,
                                                  ALU.max, ALU.min))
                v_dt = cnt["V"]

                if "dt" in debug:
                    @block.sync
                    def _(sync, db=db, v_dt=v_dt):
                        w(sync, "V", v_dt)
                        inc("S", sync.dma_start(out=dbg_outs["dt"][db * 128:(db + 1) * 128, :],
                                                in_=dt_sb[:]), 16)

                ab_base = a_ab_last - NST

                for n in range(NST):
                    col = db * 16 + n
                    g = db * NST + n
                    b2 = g % 2
                    b3 = g % 3
                    ab_marks[g] = ab_base + n + 1

                    @block.sync
                    def _(sync, db=db, n=n, g=g, b3=b3):
                        if db == 0 and n == 0:
                            w(sync, "S", s_bc)
                        w(sync, "V", wr_marks.get(g - 3, 0))
                        w(sync, "V", z_marks.get(g - 3, 0))
                        inc("S", sync.dma_start(
                            out=bmc[b3][:],
                            in_=bcast_d[n:n + 1, :].partition_broadcast(128)), 16)
                        inc("S", sync.dma_start(
                            out=bmcC[b3][:],
                            in_=bcast_d[16 + n:17 + n, :].partition_broadcast(128)), 16)
                    bmc_marks[g] = cnt["S"]

                    Vb = cnt["V"]
                    @block.vector
                    def _(vector, db=db, n=n, g=g, b2=b2, b3=b3, col=col):
                        w(vector, "S", bmc_marks[g] + 32)
                        inc("V", vector.tensor_mul(wr_sb[b2][:], xn_fm[db][:],
                                                   bmc[b3][:]))
                        w(vector, "A", ab_marks[g])
                        inc("V", vector.tensor_scalar(bx_sb[b2][:], abar[b2][:],
                                                      -1.0, recipA[:, col:col + 1],
                                                      ALU.add, ALU.mult))
                        inc("V", vector.tensor_mul(bx_sb[b2][:], bx_sb[b2][:],
                                                   wr_sb[b2][:]))
                    wr_marks[g] = Vb + 1
                    bx_marks[g] = cnt["V"]

                    @block.vector
                    def _(vector, g=g, b2=b2):
                        w(vector, "T", zmm_marks.get(g - 2, 0))
                        inc("V", vector.tensor_tensor_scan(
                            h_sb[b2][:], abar[b2][:], bx_sb[b2][:], 0.0,
                            ALU.mult, ALU.add))
                    scan_marks[g] = cnt["V"]

                    @block.vector
                    def _(vector, g=g, b2=b2, b3=b3, n=n):
                        w(vector, "T", zmm_marks.get(g - 2, 0))
                        if n <= 1:
                            w(vector, "T", ty_marks.get(g // NST - 1, 0))
                        inc("V", vector.tensor_mul(z_sb[b2][:], h_sb[b2][:],
                                                   bmcC[b3][:]))
                    z_marks[g] = cnt["V"]

                    @block.tensor
                    def _(tensor, db=db, n=n, g=g, b2=b2):
                        w(tensor, "V", z_marks[g])
                        for hf in range(4):
                            inc("T", tensor.matmul(ps_y[:, hf * 512:(hf + 1) * 512],
                                                   ident[:],
                                                   z_sb[b2][:, hf * 512:(hf + 1) * 512],
                                                   start=(n == 0), stop=False))
                    zmm_marks[g] = cnt["T"]

                if db == ND - 1:
                    @block.sync
                    def _(sync):
                        inc("S", sync.dma_start(out=dum_s[0:1, 0:64],
                                                in_=bcast_d[0:1, 0:64]), 16)
                        inc("S", sync.dma_start(out=dum_s[0:1, 0:64],
                                                in_=bcast_d[0:1, 0:64]), 16)

                @block.vector
                def _(vector, db=db):
                    glast = db * NST + NST - 1
                    w(vector, "T", zmm_marks.get(glast, 0))
                    w(vector, "T", zmm_marks.get(glast - 1, 0))
                    inc("V", vector.tensor_scalar(z_sb[0][:], xn_fm[db][:],
                                                  Dcol[:, db:db + 1], None, ALU.mult))
                    inc("V", vector.scalar_tensor_tensor(z_sb[1][:], xn_fm[db][:],
                                                         Dcol[:, db:db + 1], z_sb[0][:],
                                                         ALU.mult, ALU.subtract))
                v_dxn = cnt["V"]

                @block.tensor
                def _(tensor, db=db, v_dxn=v_dxn):
                    w(tensor, "V", v_dxn)
                    for hf in range(4):
                        inc("T", tensor.matmul(ps_y[:, hf * 512:(hf + 1) * 512],
                                               ident[:], z_sb[0][:, hf * 512:(hf + 1) * 512],
                                               start=False, stop=False))
                    for hf in range(4):
                        inc("T", tensor.matmul(ps_y[:, hf * 512:(hf + 1) * 512],
                                               ident[:], z_sb[1][:, hf * 512:(hf + 1) * 512],
                                               start=False, stop=True))
                t_y = cnt["T"]
                ty_marks[db] = t_y

                @block.scalar
                def _(scalar, db=db, t_y=t_y):
                    w(scalar, "T", t_y)
                    inc("A", scalar.activation(abar[0][:], ps_y[:], AF.Copy))
                a_y = cnt["A"]

                if "y_fm" in debug:
                    @block.sync
                    def _(sync, db=db, a_y=a_y):
                        w(sync, "A", a_y)
                        inc("S", sync.dma_start(out=dbg_outs["y_fm"][db * 128:(db + 1) * 128, :],
                                                in_=abar[0][:]), 16)

                last_add = [0]
                for i in range(NT):
                    pst = ps_yt[i % 2]

                    @block.tensor
                    def _(tensor, db=db, i=i, pst=pst, a_y=a_y):
                        w(tensor, "A", a_y)
                        w(tensor, "V", last_add[0])
                        inc("T", tensor.transpose(pst[:],
                                                  abar[0][:, i * 128:(i + 1) * 128],
                                                  ident32[:]))
                    t_tr = cnt["T"]

                    @block.vector
                    def _(vector, db=db, i=i, pst=pst, t_tr=t_tr):
                        w(vector, "T", t_tr)
                        inc("V", vector.tensor_add(
                            x_sb[:, i * D + db * 128:i * D + db * 128 + 128],
                            x_sb[:, i * D + db * 128:i * D + db * 128 + 128],
                            pst[:]))
                    last_add[0] = cnt["V"]
                add_marks[db] = cnt["V"]
                @block.tensor
                def _(tensor, a_y=a_y):
                    w(tensor, "A", a_y)

            mamba_v = cnt["V"]

            if "x_after_mamba" in debug:
                dbg("x_after_mamba", (T, D))
                @block.sync
                def _(sync):
                    w(sync, "V", mamba_v)
                    for i in range(NT):
                        inc("S", sync.dma_start(
                            out=dbg_outs["x_after_mamba"][i * 128:(i + 1) * 128, :],
                            in_=x_sb[:, i * D:(i + 1) * D]), 16)

            @block.sync
            def _(sync):
                w(sync, "V", mamba_v)
                w(sync, "T", cnt["T"])
                w(sync, "A", cnt["A"])
                w(sync, "G", cnt["G"])


        def emit_moe(tag, ready_v):
            with ExitStack() as fctx:
                FE = fctx.enter_context
                rsc = FE(nc.sbuf_tensor(f"rsc{tag}", [128, 64], F32))
                tk_sb = FE(nc.sbuf_tensor(f"tk{tag}", [128, 2 * NT], F32))
                ee_sb = FE(nc.sbuf_tensor(f"ee{tag}", [128, 2 * NT], F32))
                pos_sb = FE(nc.sbuf_tensor(f"pos{tag}", [128, 2 * NT], F32))
                I1_sb = FE(nc.sbuf_tensor(f"I1{tag}", [128, 16 * NT], F32))
                I2_sb = FE(nc.sbuf_tensor(f"I2{tag}", [128, 16 * NT], F32))
                Irun = FE(nc.sbuf_tensor(f"Irun{tag}", [128, 16], F32))
                poscum = FE(nc.sbuf_tensor(f"pc{tag}", [128, 16], F32))
                sidx = FE(nc.sbuf_tensor(f"sidx{tag}", [128, 2 * NT], I32))
                gidx = FE(nc.sbuf_tensor(f"gidx{tag}", [128, 2 * NT], I32))
                pexp = FE(nc.sbuf_tensor(f"pexp{tag}", [128, 16], F32))
                logits_tm = FE(nc.sbuf_tensor(f"ltm{tag}", [128, 16 * NT], BF16))
                Wr_sb = FE(nc.sbuf_tensor(f"Wr{tag}", [128, ND * 16], BF16))
                eph_stack = ExitStack()
                _EP = eph_stack.enter_context
                xn2 = _EP(nc.sbuf_tensor(f"xn2{tag}", [128, NT * D], BF16))
                lgf32 = _EP(nc.sbuf_tensor(f"lgf32{tag}", [128, 256], F32))
                m8all = _EP(nc.sbuf_tensor(f"m8a{tag}", [128, 128], F32))
                v1a = _EP(nc.sbuf_tensor(f"v1a{tag}", [128, 16], F32))
                v2a = _EP(nc.sbuf_tensor(f"v2a{tag}", [128, 16], F32))
                v2a_raw = _EP(nc.sbuf_tensor(f"v2ar{tag}", [128, 16], F32))
                suba = _EP(nc.sbuf_tensor(f"suba{tag}", [128, 256], F32))
                expa = _EP(nc.sbuf_tensor(f"expa{tag}", [128, 256], F32))
                suma = _EP(nc.sbuf_tensor(f"suma{tag}", [128, 16], F32))
                cuma = _EP(nc.sbuf_tensor(f"cuma{tag}", [128, 256], F32))
                z16 = _EP(nc.sbuf_tensor(f"z16{tag}", [128, 16], F32))
                fscr = _EP(nc.sbuf_tensor(f"fscr{tag}", [128, 16], F32))
                poscumA = _EP(nc.sbuf_tensor(f"pcA{tag}", [128, 256], F32))
                poscumB = _EP(nc.sbuf_tensor(f"pcB{tag}", [128, 256], F32))

                ln_v = emit_layernorm(xn2, ready=("V", ready_v),
                                      stream_dram=xn_d)
                s_xnd = cnt["S"]

                @block.gpsimd
                def _(gpsimd):
                    w(gpsimd, "V", ready_v)
                    inc("G", gpsimd.dma_start(
                        out=Wr_sb[:].rearrange("p (c e) -> p c e", c=ND),
                        in_=w_router.rearrange("(c p) e -> p c e", p=128)), 16)
                g_wr = cnt["G"]

                with ExitStack() as rctx:
                    RE = rctx.enter_context
                    xnT = [RE(nc.sbuf_tensor(f"xnT{tag}{fd}", [128, T], BF16))
                           for fd in range(ND)]
                    ps_r = RE(nc.psum_tensor(f"ps_r{tag}", [16, T], F32))
                    ps_rt = RE(nc.psum_tensor(f"ps_rt{tag}", [128, 16], BF16))
                    logF = RE(nc.sbuf_tensor(f"logF{tag}", [16, T], BF16))

                    @block.sync
                    def _(sync):
                        w(sync, "S", s_xnd)
                        for fd in range(ND):
                            inc("L", sync.dma_start(
                                out=xnT[fd][:],
                                in_=xn_d[:, fd * 128:(fd + 1) * 128],
                                transpose=True), 16)
                    l_xnT = cnt["L"]

                    @block.tensor
                    def _(tensor):
                        w(tensor, "L", l_xnT)
                        w(tensor, "G", g_wr)
                        for q in range(4):
                            for fd in range(ND):
                                inc("T", tensor.matmul(
                                    ps_r[:, q * 512:(q + 1) * 512],
                                    Wr_sb[:, fd * 16:(fd + 1) * 16],
                                    xnT[fd][:, q * 512:(q + 1) * 512],
                                    start=(fd == 0), stop=(fd == ND - 1)))
                    t_rlog = cnt["T"]

                    @block.scalar
                    def _(scalar):
                        w(scalar, "T", t_rlog)
                        inc("A", scalar.activation(logF[:], ps_r[:], AF.Copy))
                    a_rlog = cnt["A"]

                    lastc = [0]
                    for i in range(NT):
                        @block.tensor
                        def _(tensor, i=i, a_rlog=a_rlog):
                            w(tensor, "A", a_rlog)
                            w(tensor, "V", lastc[0])
                            inc("T", tensor.transpose(
                                ps_rt[:], logF[:, i * 128:(i + 1) * 128],
                                ident[0:16, 0:16]))
                        t_tr = cnt["T"]

                        @block.vector
                        def _(vector, i=i, t_tr=t_tr):
                            w(vector, "T", t_tr)
                            inc("V", vector.tensor_copy(
                                logits_tm[:, i * 16:(i + 1) * 16], ps_rt[:]))
                        lastc[0] = cnt["V"]
                    v_log = cnt["V"]


                def grp(ap3):
                    return ap3.rearrange("p (g c) -> p g c", g=16)

                def bgrp(col16):
                    return col16.unsqueeze(2).broadcast_to([128, 16, 16])

                @block.vector
                def _(vector):
                    def flr():
                        inc("V", vector.tensor_copy(fscr[:], z16[:]))
                    w(vector, "V", v_log)
                    inc("V", vector.memset(z16[:], 0.0))
                    inc("V", vector.tensor_copy(lgf32[:], logits_tm[:]))
                    flr(); flr()
                    for i in range(NT):
                        inc("V", vector.max(m8all[:, 8 * i:8 * i + 8],
                                            lgf32[:, 16 * i:16 * i + 16]))
                    flr(); flr()
                    inc("V", vector.tensor_copy(v1a[:], m8all[:, 0:128:8]))
                    inc("V", vector.tensor_copy(v2a[:], m8all[:, 1:128:8]))
                    inc("V", vector.tensor_copy(v2a_raw[:], m8all[:, 1:128:8]))
                    flr(); flr()
                    inc("V", vector.tensor_tensor(grp(suba[:]), grp(lgf32[:]),
                                                  bgrp(v1a[:]), ALU.subtract))
                    inc("V", vector.tensor_sub(fscr[:], v2a[:], v1a[:]))
                v_sub = cnt["V"]

                @block.scalar
                def _(scalar):
                    w(scalar, "V", v_sub)
                    inc("A", scalar.activation(expa[:], suba[:], AF.Exp))
                    inc("A", scalar.activation(v2a[:], fscr[:], AF.Exp))
                a_exp = cnt["A"]

                @block.vector
                def _(vector):
                    def flr():
                        inc("V", vector.tensor_copy(fscr[:], z16[:]))
                    w(vector, "A", a_exp)
                    inc("V", vector.tensor_reduce(suma[:], grp(expa[:]),
                                                  AX.X, ALU.add))
                    flr(); flr()
                    inc("V", vector.reciprocal(tk_sb[:, 0:NT], suma[:]))
                    flr(); flr()
                    inc("V", vector.tensor_mul(tk_sb[:, NT:2 * NT], v2a[:],
                                               tk_sb[:, 0:NT]))
                    inc("V", vector.tensor_tensor(grp(I1_sb[:]), grp(lgf32[:]),
                                                  bgrp(v1a[:]), ALU.is_equal))
                    flr(); flr()
                    for i in range(NT):
                        inc("V", vector.tensor_tensor_scan(
                            cuma[:, 16 * i:16 * i + 16], I1_sb[:, 16 * i:16 * i + 16],
                            z16[:], 0.0, ALU.add, ALU.add))
                    flr(); flr()
                    inc("V", vector.tensor_scalar(cuma[:], cuma[:], 1.0, None,
                                                  ALU.is_equal))
                    flr(); flr()
                    inc("V", vector.tensor_mul(I1_sb[:], I1_sb[:], cuma[:]))
                    flr(); flr()
                    inc("V", vector.tensor_mul(cuma[:], I1_sb[:], iotat[:]))
                    flr(); flr()
                    inc("V", vector.tensor_reduce(ee_sb[:, 0:NT], grp(cuma[:]),
                                                  AX.X, ALU.add))
                    inc("V", vector.scalar_tensor_tensor(lgf32[:], I1_sb[:], -1e9,
                                                         lgf32[:], ALU.mult, ALU.add))
                    flr(); flr()
                    inc("V", vector.tensor_tensor(grp(I2_sb[:]), grp(lgf32[:]),
                                                  bgrp(v2a_raw[:]), ALU.is_equal))
                    flr(); flr()
                    for i in range(NT):
                        inc("V", vector.tensor_tensor_scan(
                            cuma[:, 16 * i:16 * i + 16], I2_sb[:, 16 * i:16 * i + 16],
                            z16[:], 0.0, ALU.add, ALU.add))
                    flr(); flr()
                    inc("V", vector.tensor_scalar(cuma[:], cuma[:], 1.0, None,
                                                  ALU.is_equal))
                    flr(); flr()
                    inc("V", vector.tensor_mul(I2_sb[:], I2_sb[:], cuma[:]))
                    flr(); flr()
                    inc("V", vector.tensor_mul(cuma[:], I2_sb[:], iotat[:]))
                    flr(); flr()
                    inc("V", vector.tensor_reduce(ee_sb[:, NT:2 * NT], grp(cuma[:]),
                                                  AX.X, ALU.add))
                v_route = cnt["V"]

                with nc.psum_tensor(f"ps_pos{tag}", [128, 16], F32) as ps_pos:
                    for k in range(2):
                        Ik = I1_sb if k == 0 else I2_sb
                        pcX = poscumA if k == 0 else poscumB

                        @block.vector
                        def _(vector, k=k):
                            if k == 0:
                                w(vector, "V", v_route)
                                inc("V", vector.memset(Irun[:], 0.0))
                        for i in range(NT):
                            @block.tensor
                            def _(tensor, k=k, i=i, Ik=Ik):
                                w(tensor, "V", cnt["V"])
                                inc("T", tensor.matmul(ps_pos[:], strictL32[:],
                                                       Ik[:, i * 16:(i + 1) * 16],
                                                       start=True, stop=False))
                                inc("T", tensor.matmul(ps_pos[:], ones128[:],
                                                       Irun[:],
                                                       start=False, stop=True))
                            t_p = cnt["T"]

                            @block.vector
                            def _(vector, k=k, i=i, Ik=Ik, pcX=pcX, t_p=t_p):
                                w(vector, "T", t_p)
                                inc("V", vector.tensor_copy(
                                    pcX[:, i * 16:(i + 1) * 16], ps_pos[:]))
                                inc("V", vector.tensor_add(Irun[:], Irun[:],
                                                           Ik[:, i * 16:(i + 1) * 16]))

                @block.vector
                def _(vector):
                    def flr():
                        inc("V", vector.tensor_copy(fscr[:], z16[:]))
                    flr(); flr()
                    inc("V", vector.tensor_mul(poscumA[:], poscumA[:], I1_sb[:]))
                    inc("V", vector.tensor_mul(poscumB[:], poscumB[:], I2_sb[:]))
                    flr(); flr()
                    inc("V", vector.tensor_reduce(pos_sb[:, 0:NT], grp(poscumA[:]),
                                                  AX.X, ALU.add))
                    inc("V", vector.tensor_reduce(pos_sb[:, NT:2 * NT], grp(poscumB[:]),
                                                  AX.X, ALU.add))
                    flr(); flr()
                    inc("V", vector.scalar_tensor_tensor(
                        poscumA[:, 0:32], ee_sb[:], 256.0, pos_sb[:],
                        ALU.mult, ALU.add))
                    inc("V", vector.tensor_scalar(
                        poscumA[:, 32:64], pos_sb[:], 255.5, None, ALU.is_gt))
                    flr(); flr()
                    inc("V", vector.tensor_scalar(
                        poscumA[:, 64:96], poscumA[:, 0:32], -1.0, 4096.0,
                        ALU.mult, ALU.add))
                    flr(); flr()
                    inc("V", vector.tensor_mul(poscumA[:, 64:96], poscumA[:, 64:96],
                                               poscumA[:, 32:64]))
                    flr(); flr()
                    inc("V", vector.tensor_add(poscumA[:, 0:32], poscumA[:, 0:32],
                                               poscumA[:, 64:96]))
                    flr(); flr()
                    inc("V", vector.tensor_copy(sidx[:], poscumA[:, 0:32]))
                    inc("V", vector.tensor_scalar(
                        poscumB[:, 0:32], pos_sb[:], 255.0, None, ALU.min))
                    flr(); flr()
                    inc("V", vector.scalar_tensor_tensor(
                        poscumB[:, 32:64], ee_sb[:], 256.0, poscumB[:, 0:32],
                        ALU.mult, ALU.add))
                    flr(); flr()
                    inc("V", vector.tensor_copy(gidx[:], poscumB[:, 32:64]))
                v_idx = cnt["V"]

                if f"route{tag}" in debug:
                    dbg(f"route{tag}_tk", (128, 2 * NT), force=True)
                    dbg(f"route{tag}_ee", (128, 2 * NT), force=True)
                    dbg(f"route{tag}_pos", (128, 2 * NT), force=True)
                    dbg(f"route{tag}_sidx", (128, 2 * NT), I32, force=True)
                    dbg(f"route{tag}_gidx", (128, 2 * NT), I32, force=True)
                    dbg(f"route{tag}_log", (128, 16 * NT), BF16, force=True)
                    @block.sync
                    def _(sync):
                        w(sync, "V", v_idx)
                        inc("S", sync.dma_start(out=dbg_outs[f"route{tag}_tk"], in_=tk_sb[:]), 16)
                        inc("S", sync.dma_start(out=dbg_outs[f"route{tag}_ee"], in_=ee_sb[:]), 16)
                        inc("S", sync.dma_start(out=dbg_outs[f"route{tag}_pos"], in_=pos_sb[:]), 16)
                        inc("S", sync.dma_start(out=dbg_outs[f"route{tag}_sidx"], in_=sidx[:]), 16)
                        inc("S", sync.dma_start(out=dbg_outs[f"route{tag}_gidx"], in_=gidx[:]), 16)
                        inc("S", sync.dma_start(out=dbg_outs[f"route{tag}_log"], in_=logits_tm[:]), 16)

                @block.gpsimd
                def _(gpsimd):
                    w(gpsimd, "V", v_idx)
                    w(gpsimd, "G", xtm_zeroed)
                    for k in range(2):
                        for i in range(NT):
                            c = k * NT + i
                            inc("G", gpsimd.indirect_dma_start(
                                out=Xtm_d,
                                out_offset=IndirectOffsetOnAxis(sidx[:, c:c + 1], 0),
                                in_=xn2[:, i * D:(i + 1) * D], in_offset=None), 16)
                g_scat = cnt["G"]
                eph_stack.close()

                with ExitStack() as ectx:
                    EE = ectx.enter_context
                    XT = [EE(nc.sbuf_tensor(f"XT{tag}{fd}", [128, E * CAP], BF16))
                          for fd in range(ND)]
                    Wg_sb = [EE(nc.sbuf_tensor(f"Wg{tag}{b}", [128, 8 * 512], BF16)) for b in range(2)]
                    Wf_sb = [EE(nc.sbuf_tensor(f"Wf{tag}{b}", [128, 8 * 512], BF16)) for b in range(2)]
                    Wp_sb = [EE(nc.sbuf_tensor(f"Wp{tag}{b}", [128, 4 * 1024], BF16)) for b in range(2)]
                    sg_sb = EE(nc.sbuf_tensor(f"sg{tag}", [128, 1024], BF16))
                    hh_sb = EE(nc.sbuf_tensor(f"hh{tag}", [128, 1024], BF16))
                    o_sb = [EE(nc.sbuf_tensor(f"o{tag}{b}", [128, 1024], BF16)) for b in range(2)]
                    ps_g = EE(nc.psum_tensor(f"ps_g{tag}", [128, 1024], F32))
                    ps_f = EE(nc.psum_tensor(f"ps_f{tag}", [128, 1024], F32))
                    ps_o2 = [EE(nc.psum_tensor(f"ps_o{tag}{b}", [128, 1024], F32)) for b in range(2)]

                    @block.sync
                    def _(sync):
                        w(sync, "G", g_scat)
                        for fd in range(ND):
                            inc("L", sync.dma_start(
                                out=XT[fd][:],
                                in_=Xtm_d[0:E * CAP, fd * 128:(fd + 1) * 128],
                                transpose=True), 16)
                    l_XT = cnt["L"]

                    a_prev_sil = [0]
                    v_prev_hh = [0]
                    t_prev_o = [0]
                    t_prev_o_guard = [0]
                    t_gf_mark = {}
                    g_we_marks = {}
                    s_o_marks = {}
                    base_TAV = (cnt["T"], cnt["A"], cnt["V"])
                    for e in range(E + 1):
                        if e < E:
                            @block.gpsimd
                            def _(gpsimd, e=e):
                                eb = e % 2
                                if e < 2:
                                    w(gpsimd, "T", base_TAV[0])
                                    w(gpsimd, "A", base_TAV[1])
                                    w(gpsimd, "V", base_TAV[2])
                                w(gpsimd, "T", t_gf_mark.get(e - 2, 0))
                                w(gpsimd, "T", t_prev_o[0] if e >= 2 else 0)
                                inc("G", gpsimd.dma_start(
                                    out=Wg_sb[e % 2][:], in_=w_gate[e]), 16)
                                inc("G", gpsimd.dma_start(
                                    out=Wf_sb[e % 2][:], in_=w_fc[e]), 16)
                                inc("G", gpsimd.dma_start(
                                    out=Wp_sb[e % 2][:], in_=w_proj[e]), 16)
                            g_we_marks[e] = cnt["G"]
                        if e == 0:
                            continue
                        e = e - 1
                        g_we = g_we_marks[e]

                        T_b = cnt["T"]
                        A_b = cnt["A"]
                        V_b = cnt["V"]
                        t_gf_c = {0: T_b + 32, 1: T_b + 64}

                        @block.tensor
                        def _(tensor, e=e, g_we=g_we, A_b=A_b, V_b=V_b):
                            def gf(C):
                                for ps_x, W_sb in ((ps_g, Wg_sb), (ps_f, Wf_sb)):
                                    for hc in (2 * C, 2 * C + 1):
                                        for c in range(ND):
                                            inc("T", tensor.matmul(
                                                ps_x[:, hc * 256:(hc + 1) * 256],
                                                W_sb[e % 2][:, c * 512 + hc * 128:
                                                     c * 512 + (hc + 1) * 128],
                                                XT[c][:, e * CAP:(e + 1) * CAP],
                                                start=(c == 0),
                                                stop=(c == ND - 1)))

                            def pj(C):
                                tensor.wait_ge(sem["V"], V_b + C + 1)
                                for hc in (2 * C, 2 * C + 1):
                                    for ss in range(2):
                                        for half in range(2):
                                            inc("T", tensor.matmul(
                                                ps_o2[ss][:, half * 512:(half + 1) * 512],
                                                hh_sb[:, hc * 256 + ss * 128:
                                                      hc * 256 + ss * 128 + 128],
                                                Wp_sb[e % 2][:, hc * 1024 + half * 512:
                                                      hc * 1024 + (half + 1) * 512],
                                                start=(hc == 0), stop=(hc == 3)))

                            w(tensor, "G", g_we)
                            if e == 0:
                                w(tensor, "L", l_XT)
                            w(tensor, "A", a_prev_sil[0])
                            w(tensor, "V", v_prev_hh[0])
                            gf(0); gf(1); pj(0); pj(1)
                        t_o_end = cnt["T"]
                        t_gf_mark[e] = t_o_end
                        t_prev_o[0] = t_o_end

                        @block.scalar
                        def _(scalar, e=e, t_gf_c=t_gf_c, t_o_end=t_o_end,
                              som0=s_o_marks.get((e - 1, 0), 0),
                              som1=s_o_marks.get((e - 1, 1), 0)):
                            for C in range(2):
                                scalar.wait_ge(sem["T"], t_gf_c[C])
                                inc("A", scalar.activation(
                                    sg_sb[:, C * 512:(C + 1) * 512],
                                    ps_g[:, C * 512:(C + 1) * 512], AF.Silu))
                            for ss in range(2):
                                scalar.wait_ge(sem["T"], t_o_end)
                                scalar.wait_ge(sem["S"], (som0, som1)[ss])
                                inc("A", scalar.activation(o_sb[ss][:], ps_o2[ss][:],
                                                           AF.Copy))
                        a_sil = cnt["A"] - 2
                        a_prev_sil[0] = a_sil
                        a_oev = cnt["A"]

                        @block.vector
                        def _(vector, e=e, A_b=A_b, tpo=t_prev_o_guard[0]):
                            w(vector, "T", tpo)
                            for C in range(2):
                                vector.wait_ge(sem["A"], A_b + C + 1)
                                inc("V", vector.tensor_mul(
                                    hh_sb[:, C * 512:(C + 1) * 512],
                                    sg_sb[:, C * 512:(C + 1) * 512],
                                    ps_f[:, C * 512:(C + 1) * 512]))
                        v_hh = cnt["V"]
                        v_prev_hh[0] = v_hh
                        t_prev_o_guard[0] = t_o_end

                        @block.sync
                        def _(sync, e=e, a_oev=a_oev):
                            for ss in range(2):
                                sync.wait_ge(sem["A"], a_oev - (1 - ss))
                                inc("S", sync.dma_start(
                                    out=o_d[e * CAP + ss * 128:e * CAP + ss * 128 + 128, :],
                                    in_=o_sb[ss][:]), 16)
                                s_o_marks[(e, ss)] = cnt["S"]
                    s_o = cnt["S"]

                A0 = FE(nc.sbuf_tensor(f"A0{tag}", [128, D], BF16))
                A1 = FE(nc.sbuf_tensor(f"A1{tag}", [128, D], BF16))
                lastcomb = [0]
                for i in range(NT):
                    @block.gpsimd
                    def _(gpsimd, i=i):
                        if i == 0:
                            w(gpsimd, "S", s_o)
                        w(gpsimd, "V", lastcomb[0])
                        inc("G", gpsimd.indirect_dma_start(
                            out=A0[:], out_offset=None, in_=o_d,
                            in_offset=IndirectOffsetOnAxis(gidx[:, i:i + 1], 0)), 16)
                        inc("G", gpsimd.indirect_dma_start(
                            out=A1[:], out_offset=None, in_=o_d,
                            in_offset=IndirectOffsetOnAxis(gidx[:, NT + i:NT + i + 1], 0)), 16)
                    g_ga = cnt["G"]

                    @block.vector
                    def _(vector, i=i, g_ga=g_ga):
                        w(vector, "G", g_ga)
                        inc("V", vector.scalar_tensor_tensor(
                            x_sb[:, i * D:(i + 1) * D], A0[:], tk_sb[:, i:i + 1],
                            x_sb[:, i * D:(i + 1) * D], ALU.mult, ALU.add))
                        inc("V", vector.scalar_tensor_tensor(
                            x_sb[:, i * D:(i + 1) * D], A1[:], tk_sb[:, NT + i:NT + i + 1],
                            x_sb[:, i * D:(i + 1) * D], ALU.mult, ALU.add))
                    lastcomb[0] = cnt["V"]
                moe_v = cnt["V"]

                @block.sync
                def _(sync):
                    w(sync, "V", moe_v)
                    w(sync, "T", cnt["T"])
                    w(sync, "A", cnt["A"])
                    w(sync, "G", cnt["G"])
            return moe_v


        def emit_attn(ready_v):
            COFF = CST["lnW_COFF"]
            with ExitStack() as actx:
                AE = actx.enter_context
                qstg = [AE(nc.sbuf_tensor(f"qstg{b}", [128, QT], BF16)) for b in range(2)]
                KT_sb = [AE(nc.sbuf_tensor(f"KTs{fd}", [128, T], BF16)) for fd in range(ND)]
                Vaug = AE(nc.sbuf_tensor("Vaug", [128, NT * NH * 65], BF16))
                negc = AE(nc.sbuf_tensor("negc", [128, 16], F32))
                nsc = AE(nc.sbuf_tensor("nsc", [128, 32], F32))
                cvec = AE(nc.sbuf_tensor("cvec", [128, D], F32))
                bvb16 = AE(nc.sbuf_tensor("bvb16", [128, 8], BF16))

                with nc.sbuf_tensor("xn3st", [128, 2 * D], BF16) as xn3st:
                    s_xn3 = emit_layernorm_dram(xn_d, xn3st, ready=("V", ready_v))

                @block.vector
                def _(vector):
                    w(vector, "V", ready_v)
                    inc("V", vector.tensor_scalar(nsc[:, 0:8], biases[:, 0:8],
                                                  0.125, None, ALU.mult))

                ps_n_ctx = nc.psum_tensor("ps_n", [16, T], F32)
                ps_n = ps_n_ctx.__enter__()
                with ExitStack() as pctx:
                    PE_ = pctx.enter_context
                    xnTs = [PE_(nc.sbuf_tensor(f"axnT{fd}", [128, QT], BF16))
                            for fd in range(ND)]
                    sq512 = PE_(nc.sbuf_tensor("sq512", [128, QT], BF16))
                    W_cur = PE_(nc.sbuf_tensor("W_cur", [128, ND * D], BF16))
                    ps_pj = PE_(nc.psum_tensor("ps_pj", [128, 1024], F32))

                    qst_marks = {}
                    qst_last = {}
                    for pj, Wsrc in enumerate([Wq, Wk, Wv]):
                        @block.gpsimd
                        def _(gpsimd, pj=pj, Wsrc=Wsrc):
                            w(gpsimd, "T", cnt["T"])
                            inc("G", gpsimd.dma_start(
                                out=W_cur[:], in_=Wsrc), 16)
                        g_w = cnt["G"]

                        for ts in range(4):
                            @block.sync
                            def _(sync, pj=pj, ts=ts):
                                if pj == 0 and ts == 0:
                                    w(sync, "S", s_xn3)
                                w(sync, "T", cnt["T"])
                                for fd in range(ND):
                                    inc("L", sync.dma_start(
                                        out=xnTs[fd][:],
                                        in_=xn_d[ts * QT:(ts + 1) * QT,
                                                 fd * 128:(fd + 1) * 128],
                                        transpose=True), 16)
                            l_xt = cnt["L"]

                            if pj < 2:
                                for od in range(ND):
                                    @block.tensor
                                    def _(tensor, pj=pj, ts=ts, od=od, l_xt=l_xt, g_w=g_w):
                                        w(tensor, "L", l_xt)
                                        w(tensor, "G", g_w)
                                        w(tensor, "A", cnt["A"])
                                        for fd in range(ND):
                                            inc("T", tensor.matmul(
                                                ps_pj[:, 0:QT],
                                                W_cur[:, fd * D + od * 128:
                                                      fd * D + od * 128 + 128],
                                                xnTs[fd][:],
                                                start=(fd == 0), stop=(fd == ND - 1)))
                                    t_pj = cnt["T"]

                                    @block.scalar
                                    def _(scalar, pj=pj, ts=ts, od=od, t_pj=t_pj):
                                        w(scalar, "T", t_pj)
                                        if pj == 0:
                                            w(scalar, "S", qst_last.get(od % 2, 0))
                                            inc("A", scalar.activation(
                                                qstg[od % 2][:], ps_pj[:, 0:QT],
                                                AF.Identity, bias=nsc[:, od:od + 1],
                                                scale=0.125))
                                        else:
                                            inc("A", scalar.activation(
                                                KT_sb[od][:, ts * QT:(ts + 1) * QT],
                                                ps_pj[:, 0:QT], AF.Identity,
                                                bias=biases[:, 8 + od:9 + od]))
                                    a_qe = cnt["A"]

                                    if pj == 0:
                                        @block.sync
                                        def _(sync, ts=ts, od=od, a_qe=a_qe):
                                            w(sync, "A", a_qe)
                                            inc("S", sync.dma_start(
                                                out=qT_d[od * 128:(od + 1) * 128,
                                                         ts * QT:(ts + 1) * QT],
                                                in_=qstg[od % 2][:]), 16)
                                        qst_last[od % 2] = cnt["S"]

                                        @block.scalar
                                        def _(scalar, ts=ts, od=od):
                                            w(scalar, "T", cnt["T"])
                                            inc("A", scalar.activation(
                                                sq512[:], qstg[od % 2][:], AF.Square))
                                        a_sq = cnt["A"]

                                        @block.tensor
                                        def _(tensor, ts=ts, od=od, a_sq=a_sq):
                                            w(tensor, "A", a_sq)
                                            inc("T", tensor.matmul(
                                                ps_n[:, ts * QT:(ts + 1) * QT],
                                                hmaps[:, od * 16:(od + 1) * 16],
                                                sq512[:],
                                                start=(od == 0), stop=(od == ND - 1)))
                            else:
                                for tsub in range(4):
                                    @block.tensor
                                    def _(tensor, ts=ts, tsub=tsub, l_xt=l_xt, g_w=g_w):
                                        w(tensor, "L", l_xt)
                                        w(tensor, "G", g_w)
                                        w(tensor, "A", cnt["A"])
                                        for fd in range(ND):
                                            for half in range(2):
                                                inc("T", tensor.matmul(
                                                    ps_pj[:, half * 512:(half + 1) * 512],
                                                    xnTs[fd][:, tsub * 128:(tsub + 1) * 128],
                                                    W_cur[:, fd * D + half * 512:
                                                          fd * D + (half + 1) * 512],
                                                    start=(fd == 0), stop=(fd == ND - 1)))
                                    t_v = cnt["T"]

                                    @block.scalar
                                    def _(scalar, ts=ts, tsub=tsub, t_v=t_v):
                                        w(scalar, "T", t_v)
                                        kt = ts * 4 + tsub
                                        dst = Vaug[:, kt * NH * 65:(kt + 1) * NH * 65]
                                        inc("A", scalar.activation(
                                            dst.rearrange("p (h c) -> p h c", h=NH)[:, :, 0:64],
                                            ps_pj[:].rearrange("p (h c) -> p h c", c=64),
                                            AF.Identity,
                                            bias=biases[:, 16 + (0):17]
                                            if False else 0.0))
                    t_proj = cnt["T"]
                    a_proj = cnt["A"]

                    @block.vector
                    def _(vector):
                        w(vector, "A", a_proj)
                        inc("V", vector.memset(
                            Vaug[:].rearrange("p (k h c) -> p k h c", k=NT, h=NH)
                            [:, :, :, 64:65], 1.0))
                    v_vones = cnt["V"]

                    with nc.psum_tensor("ps_bv", [1, 1024], F32) as ps_bv:
                        @block.gpsimd
                        def _(gpsimd):
                            w(gpsimd, "T", t_proj)
                            inc("G", gpsimd.dma_start(
                                out=W_cur[:], in_=Wo), 16)
                        g_wo = cnt["G"]

                        @block.vector
                        def _(vector):
                            inc("V", vector.tensor_copy(bvb16[:], biases[:, 16:24]))
                        v_bvb = cnt["V"]

                        @block.tensor
                        def _(tensor, g_wo=g_wo, v_bvb=v_bvb):
                            w(tensor, "G", g_wo)
                            w(tensor, "V", v_bvb)
                            for half in range(2):
                                for fd in range(ND):
                                    inc("T", tensor.matmul(
                                        ps_bv[:, half * 512:(half + 1) * 512],
                                        bvb16[:, fd:fd + 1],
                                        W_cur[:, fd * D + half * 512:
                                              fd * D + (half + 1) * 512],
                                        start=(fd == 0), stop=(fd == ND - 1)))
                        t_bv = cnt["T"]

                        @block.scalar
                        def _(scalar, t_bv=t_bv):
                            w(scalar, "T", t_bv)
                            inc("A", scalar.activation(cvec[0:1, :], ps_bv[:], AF.Copy))
                        a_bv = cnt["A"]

                    @block.gpsimd
                    def _(gpsimd):
                        inc("G", gpsimd.dma_start(
                            out=sq_scr[0:1, :],
                            in_=bo.rearrange("a b -> b a")), 16)
                    g_bo = cnt["G"]

                    @block.vector
                    def _(vector, a_bv=a_bv, g_bo=g_bo):
                        w(vector, "A", a_bv)
                        w(vector, "G", g_bo)
                        inc("V", vector.tensor_add(cvec[0:1, :], cvec[0:1, :],
                                                   sq_scr[0:1, :]))
                    v_cv0 = cnt["V"]

                    @block.sync
                    def _(sync, v_cv0=v_cv0):
                        w(sync, "V", v_cv0)
                        inc("S", sync.dma_start(out=brow_d[0:1, :], in_=cvec[0:1, :]), 16)
                    s_cv = cnt["S"]

                    @block.gpsimd
                    def _(gpsimd, s_cv=s_cv):
                        w(gpsimd, "S", s_cv)
                        inc("G", gpsimd.dma_start(
                            out=cvec[:], in_=brow_d.partition_broadcast(128)), 16)
                    g_cvec = cnt["G"]

                with nc.sbuf_tensor("sqs", [128, T], BF16) as sqs, \
                     nc.sbuf_tensor("nrm", [16, T], F32) as nrm:
                    @block.vector
                    def _(vector):
                        w(vector, "T", cnt["T"])
                        inc("V", vector.tensor_copy(nrm[:], ps_n[:]))
                        inc("V", vector.reduce_max(nsc[0:16, 16:17], nrm[:], axis=AX.X))
                    v_qn = cnt["V"]

                    for fd in range(ND):
                        @block.scalar
                        def _(scalar, fd=fd, v_qn=v_qn):
                            if fd == 0:
                                w(scalar, "V", v_qn)
                            w(scalar, "T", cnt["T"])
                            inc("A", scalar.activation(sqs[:], KT_sb[fd][:], AF.Square))
                        a_sq2 = cnt["A"]

                        @block.tensor
                        def _(tensor, fd=fd, a_sq2=a_sq2):
                            w(tensor, "A", a_sq2)
                            for half in range(4):
                                inc("T", tensor.matmul(
                                    ps_n[:, half * 512:(half + 1) * 512],
                                    hmaps[:, fd * 16:(fd + 1) * 16],
                                    sqs[:, half * 512:(half + 1) * 512],
                                    start=(fd == 0), stop=(fd == ND - 1)))
                    t_kn = cnt["T"]

                    @block.vector
                    def _(vector, t_kn=t_kn):
                        w(vector, "T", t_kn)
                        inc("V", vector.tensor_copy(nrm[:], ps_n[:]))
                        inc("V", vector.reduce_max(nsc[0:16, 17:18], nrm[:], axis=AX.X))
                    v_nrm = cnt["V"]

                    @block.scalar
                    def _(scalar, v_nrm=v_nrm):
                        w(scalar, "V", v_nrm)
                        inc("A", scalar.activation(nsc[0:16, 18:19],
                                                   nsc[0:16, 16:17], AF.Ln))
                        inc("A", scalar.activation(nsc[0:16, 19:20],
                                                   nsc[0:16, 17:18], AF.Ln))
                    a_ln2 = cnt["A"]

                    @block.vector
                    def _(vector, a_ln2=a_ln2):
                        w(vector, "A", a_ln2)
                        inc("V", vector.tensor_add(nsc[0:16, 20:21],
                                                   nsc[0:16, 18:19],
                                                   nsc[0:16, 19:20]))
                    v_ln2 = cnt["V"]

                    @block.scalar
                    def _(scalar, v_ln2=v_ln2):
                        w(scalar, "V", v_ln2)
                        inc("A", scalar.activation(nsc[0:16, 21:22],
                                                   nsc[0:16, 20:21], AF.Exp,
                                                   scale=kvec[0:16, 5:6]))
                    a_c = cnt["A"]

                    @block.vector
                    def _(vector, a_c=a_c):
                        w(vector, "A", a_c)
                        inc("V", vector.tensor_scalar(nsc[0:16, 22:23],
                                                      nsc[0:16, 21:22],
                                                      -0.125, -1.0,
                                                      ALU.mult, ALU.add))
                    v_c = cnt["V"]

                    @block.sync
                    def _(sync, v_c=v_c):
                        w(sync, "V", v_c)
                        inc("S", sync.dma_start(
                            out=brow2_d[0:1, 32:48].rearrange("a b -> b a"),
                            in_=nsc[0:16, 22:23]), 16)
                    s_nc = cnt["S"]

                    @block.gpsimd
                    def _(gpsimd, s_nc=s_nc):
                        w(gpsimd, "S", s_nc)
                        inc("G", gpsimd.dma_start(
                            out=negc[:],
                            in_=brow2_d[0:1, 32:48].partition_broadcast(128)), 16)
                    g_negc_l = cnt["G"]

                    @block.vector
                    def _(vector, g_negc_l=g_negc_l):
                        w(vector, "G", g_negc_l)
                        inc("V", vector.tensor_copy(nsc[:, 24:25], negc[:, 0:1]))
                        inc("V", vector.tensor_copy(nsc[:, 24:25], negc[:, 15:16]))
                    a_negc_v = cnt["V"]

                    @block.gpsimd
                    def _(gpsimd, a_negc_v=a_negc_v):
                        w(gpsimd, "V", a_negc_v)
                        inc("G", gpsimd.dma_start(out=nsc[:, 25:26],
                                                  in_=brow2_d[0:1, 32:33]
                                                  .partition_broadcast(128)), 16)
                    a_negc = cnt["G"]
                ps_n_ctx.__exit__(None, None, None)

                with ExitStack() as mctx2:
                    AE2 = mctx2.enter_context
                    outT = [AE2(nc.sbuf_tensor(f"outT{fd}", [128, QT], BF16))
                            for fd in range(ND)]
                    qsl = [AE2(nc.sbuf_tensor(f"qsl{fd}", [128, QT], BF16))
                           for fd in range(ND)]
                    expS = [AE2(nc.sbuf_tensor(f"expS{b}", [128, 1024], BF16))
                            for b in range(2)]
                    sums2 = AE2(nc.sbuf_tensor("sums2", [1, 1024], F32))
                    dumA = AE2(nc.sbuf_tensor("dumA", [1, 64], F32))
                    bcS = AE2(nc.sbuf_tensor("bcS", [64, 1024], F32))
                    Wo_sb = AE2(nc.sbuf_tensor("Wo_sb", [128, ND * D], BF16))

                    @block.gpsimd
                    def _(gpsimd):
                        w(gpsimd, "T", cnt["T"])
                        inc("G", gpsimd.dma_start(
                            out=Wo_sb[:], in_=Wo), 16)
                    g_wo2 = cnt["G"]

                    t_qk_last = [0]
                    a_exp_mark = {}
                    v_dec_mark = {}
                    t_pv_mark = {}
                    v_norm_mark = [0]
                    g_bc_mark = [0]
                    v_res_mark = [0]
                    t_wo_mark = [0]

                    for qs in range(4):
                        nkt = (qs + 1) * 4

                        @block.gpsimd
                        def _(gpsimd, qs=qs):
                            if qs == 0:
                                w(gpsimd, "S", cnt["S"])
                            else:
                                w(gpsimd, "T", t_qk_last[0])
                            for fd in range(ND):
                                inc("G", gpsimd.dma_start(
                                    out=qsl[fd][:],
                                    in_=qT_d[fd * 128:(fd + 1) * 128,
                                             qs * QT:(qs + 1) * QT]), 16)
                        g_q = cnt["G"]

                        with nc.psum_tensor(f"ps_sA{qs}", [128, 1024], F32) as psA, \
                             nc.psum_tensor(f"ps_sB{qs}", [128, 1024], F32) as psB, \
                             nc.psum_tensor(f"ps_o0{qs}", [65, QT], F32) as ps_o0, \
                             nc.psum_tensor(f"ps_o1{qs}", [65, QT], F32) as ps_o1, \
                             nc.psum_tensor(f"ps_bc{qs}", [64, 1024], F32) as ps_bc:
                            ps_s = [psA, psB]
                            ps_o = [ps_o0, ps_o1]
                            for fd in range(ND):
                                T0, A0, V0 = cnt["T"], cnt["A"], cnt["V"]
                                n_kt = nkt
                                t_qk_c, t_pv_c = {}, {}
                                tc = T0
                                order = [("qk", 0), ("qk", 1)] if n_kt >= 2 else [("qk", 0)]
                                for kt in range(2, n_kt):
                                    order += [("pv", kt - 2), ("qk", kt)]
                                order += [("pv", n_kt - 2), ("pv", n_kt - 1)] if n_kt >= 2 \
                                    else [("pv", 0)]
                                for kind, kt in order:
                                    tc += 2
                                    (t_qk_c if kind == "qk" else t_pv_c)[kt] = tc
                                t_qk_last[0] = t_qk_c[n_kt - 1]
                                a_exp_c = {kt: A0 + 2 * (kt + 1) for kt in range(n_kt)}
                                v_dec_c = {kt: V0 + kt + 1 for kt in range(n_kt)}
                                carryA = [a_exp_mark.get(0, 0), a_exp_mark.get(1, 0)]
                                carryT = [t_pv_mark.get(0, 0), t_pv_mark.get(1, 0)]

                                @block.tensor
                                def _(tensor, qs=qs, fd=fd, n_kt=n_kt, order=order,
                                      a_exp_c=a_exp_c, v_dec_c=v_dec_c,
                                      carryA=carryA, vnp=v_norm_mark[0],
                                      vres0=v_res_mark[0]):
                                    if (qs, fd) == (0, 0):
                                        w(tensor, "G", a_negc)
                                        w(tensor, "V", v_vones)
                                        w(tensor, "G", g_q)
                                    if fd == 0 and qs > 0:
                                        w(tensor, "G", g_q)
                                        w(tensor, "V", vres0)
                                    for kind, kt in order:
                                        b = kt % 2
                                        if kind == "qk":
                                            tensor.wait_ge(sem["A"],
                                                           a_exp_c[kt - 2] if kt >= 2
                                                           else carryA[b])
                                            for sub in range(2):
                                                inc("T", tensor.matmul(
                                                    ps_s[b][:, sub * 512:(sub + 1) * 512],
                                                    KT_sb[fd][sub * 64:sub * 64 + 64,
                                                              kt * 128:(kt + 1) * 128],
                                                    qsl[fd][sub * 64:sub * 64 + 64, :],
                                                    start=True, stop=True,
                                                    tile_position=(sub * 64, 0)))
                                        else:
                                            tensor.wait_ge(sem["V"], v_dec_c[kt])
                                            if kt <= 1:
                                                tensor.wait_ge(sem["V"], vnp)
                                            for sub in range(2):
                                                h = fd * 2 + sub
                                                inc("T", tensor.matmul(
                                                    ps_o[sub][:],
                                                    Vaug[:, kt * NH * 65 + h * 65:
                                                         kt * NH * 65 + (h + 1) * 65],
                                                    expS[b][:, sub * 512:(sub + 1) * 512],
                                                    start=(kt == 0),
                                                    stop=(kt == n_kt - 1)))

                                @block.scalar
                                def _(scalar, qs=qs, fd=fd, n_kt=n_kt,
                                      t_qk_c=t_qk_c, t_pv_c=t_pv_c, carryT=carryT):
                                    for kt in range(n_kt):
                                        b = kt % 2
                                        scalar.wait_ge(sem["T"], t_qk_c[kt])
                                        scalar.wait_ge(sem["T"],
                                                       t_pv_c.get(kt - 2, carryT[b]))
                                        for sub in range(2):
                                            h = fd * 2 + sub
                                            inc("A", scalar.activation(
                                                expS[b][:, sub * 512:(sub + 1) * 512],
                                                ps_s[b][:, sub * 512:(sub + 1) * 512],
                                                AF.Exp, bias=negc[:, h:h + 1]))

                                @block.vector
                                def _(vector, qs=qs, fd=fd, n_kt=n_kt,
                                      a_exp_c=a_exp_c, t_pv_c=t_pv_c, carryT=carryT):
                                    for kt in range(n_kt):
                                        b = kt % 2
                                        vector.wait_ge(sem["A"], a_exp_c[kt])
                                        vector.wait_ge(sem["T"],
                                                       t_pv_c.get(kt - 2, carryT[b]))
                                        off = qs * QT - kt * 128 + QT
                                        inc("V", vector.tensor_tensor(
                                            expS[b][:].rearrange("p (s q) -> p s q", s=2),
                                            expS[b][:].rearrange("p (s q) -> p s q", s=2),
                                            Wdec[:, off:off + 512]
                                            .unsqueeze(1).broadcast_to([128, 2, 512]),
                                            ALU.mult))
                                assert cnt["T"] == tc and cnt["A"] == a_exp_c[n_kt - 1] \
                                    and cnt["V"] == v_dec_c[n_kt - 1]
                                a_exp_mark[0] = a_exp_c[n_kt - 1 if (n_kt - 1) % 2 == 0
                                                        else n_kt - 2]
                                a_exp_mark[1] = a_exp_c[n_kt - 1 if (n_kt - 1) % 2 == 1
                                                        else n_kt - 2]
                                t_pv_mark[0] = t_pv_c[n_kt - 1 if (n_kt - 1) % 2 == 0
                                                      else n_kt - 2]
                                t_pv_mark[1] = t_pv_c[n_kt - 1 if (n_kt - 1) % 2 == 1
                                                      else n_kt - 2]
                                t_pv = cnt["T"]

                                @block.vector
                                def _(vector, qs=qs, fd=fd, t_pv=t_pv):
                                    w(vector, "T", t_pv)
                                    inc("V", vector.reciprocal(sums2[0:1, 0:QT],
                                                               ps_o[0][64:65, :]))
                                    inc("V", vector.reciprocal(sums2[0:1, QT:2 * QT],
                                                               ps_o[1][64:65, :]))
                                v_rs = cnt["V"]

                                @block.tensor
                                def _(tensor, qs=qs, fd=fd, v_rs=v_rs,
                                      vnp=v_norm_mark[0]):
                                    w(tensor, "V", v_rs)
                                    tensor.wait_ge(sem["V"], vnp)
                                    for hf in range(2):
                                        inc("T", tensor.matmul(
                                            ps_bc[:, hf * 512:(hf + 1) * 512],
                                            ones128[0:1, 0:64],
                                            sums2[0:1, hf * 512:(hf + 1) * 512],
                                            start=True, stop=True,
                                            skip_group_check=True))
                                t_bc = cnt["T"]

                                @block.vector
                                def _(vector, qs=qs, fd=fd, t_bc=t_bc):
                                    w(vector, "T", t_bc)
                                    inc("V", vector.tensor_copy(bcS[:], ps_bc[:]))
                                    inc("V", vector.tensor_mul(
                                        outT[fd][0:64, :], ps_o[0][0:64, :],
                                        bcS[:, 0:512]))
                                    inc("V", vector.tensor_mul(
                                        outT[fd][64:128, :], ps_o[1][0:64, :],
                                        bcS[:, 512:1024]))
                                v_norm_mark[0] = cnt["V"]
                            v_heads = cnt["V"]

                        with nc.psum_tensor(f"ps_y2{qs}", [128, 1024], F32) as ps_y:
                            for tsub in range(4):
                                @block.tensor
                                def _(tensor, qs=qs, tsub=tsub, v_heads=v_heads):
                                    w(tensor, "V", v_heads)
                                    if qs == 0 and tsub == 0:
                                        w(tensor, "G", g_wo2)
                                    w(tensor, "V", v_res_mark[0])
                                    for fd in range(ND):
                                        for half in range(2):
                                            inc("T", tensor.matmul(
                                                ps_y[:, half * 512:(half + 1) * 512],
                                                outT[fd][:, tsub * 128:(tsub + 1) * 128],
                                                Wo_sb[:, fd * D + half * 512:
                                                      fd * D + (half + 1) * 512],
                                                start=(fd == 0), stop=(fd == ND - 1)))
                                t_y2 = cnt["T"]
                                t_wo_mark[0] = t_y2

                                @block.vector
                                def _(vector, qs=qs, tsub=tsub, t_y2=t_y2):
                                    w(vector, "T", t_y2)
                                    i = qs * 4 + tsub
                                    inc("V", vector.tensor_add(
                                        x_sb[:, i * D:(i + 1) * D],
                                        x_sb[:, i * D:(i + 1) * D], ps_y[:]))
                                    inc("V", vector.tensor_add(
                                        x_sb[:, i * D:(i + 1) * D],
                                        x_sb[:, i * D:(i + 1) * D], cvec[:]))
                                v_res_mark[0] = cnt["V"]

                    attn_v = cnt["V"]
                    @block.sync
                    def _(sync):
                        w(sync, "V", attn_v)
                        w(sync, "T", cnt["T"])
                        w(sync, "A", cnt["A"])
                        w(sync, "G", cnt["G"])
            return attn_v

        phase_barrier()
        ffn1_v = emit_moe("f1", mamba_v)

        if "x_after_ffn1" in debug:
            dbg("x_after_ffn1", (T, D))
            @block.sync
            def _(sync):
                w(sync, "V", ffn1_v)
                for i in range(NT):
                    inc("S", sync.dma_start(
                        out=dbg_outs["x_after_ffn1"][i * 128:(i + 1) * 128, :],
                        in_=x_sb[:, i * D:(i + 1) * D]), 16)

        phase_barrier()
        attn_v = emit_attn(ffn1_v)

        if "x_after_attn" in debug:
            dbg("x_after_attn", (T, D))
            @block.sync
            def _(sync):
                w(sync, "V", attn_v)
                for i in range(NT):
                    inc("S", sync.dma_start(
                        out=dbg_outs["x_after_attn"][i * 128:(i + 1) * 128, :],
                        in_=x_sb[:, i * D:(i + 1) * D]), 16)

        phase_barrier()
        ffn2_v = emit_moe("f2", attn_v)

        @block.sync
        def _(sync):
            w(sync, "V", cnt["V"])
            for i in range(NT):
                inc("S", sync.dma_start(out=y_out[i * 128:(i + 1) * 128, :],
                                        in_=x_sb[:, i * D:(i + 1) * D]), 16)
            w(sync, "S", cnt["S"])

    return nc, dbg_outs


def _wtile(w):
    M = w.shape[1]
    return np.ascontiguousarray(
        w.reshape(ND, 128, M).transpose(1, 0, 2).reshape(128, ND * M))


def _prep_inputs(inputs, core):
    A = np.ascontiguousarray(inputs["L0_attn_a_A"].reshape(D, NST))
    A_cols = np.zeros((128, 128), dtype=np.float32)
    for db in range(ND):
        A_cols[:, db * 16:(db + 1) * 16] = A[db * 128:(db + 1) * 128, :]
    m = {
        "x": np.ascontiguousarray(inputs["L0_attn_a_x"][core]),
        "W_in": inputs["L0_attn_a_W_in"],
        "b_in": np.ascontiguousarray(inputs["L0_attn_a_b_in"].reshape(32, 1)),
        "W_dt1": inputs["L0_attn_a_W_dt1"],
        "b_dt1": np.ascontiguousarray(inputs["L0_attn_a_b_dt1"].reshape(64, 1)),
        "W_dt2": inputs["L0_attn_a_W_dt2"],
        "b_dt2": np.ascontiguousarray(inputs["L0_attn_a_b_dt2"].reshape(D, 1)),
        "A_cols": A_cols,
        "D_row": np.ascontiguousarray(inputs["L0_attn_a_D"].reshape(1, D)),
        "Wq": _wtile(inputs["L0_attn_b_Wq"]), "bq": np.ascontiguousarray(inputs["L0_attn_b_bq"].reshape(D, 1)),
        "Wk": _wtile(inputs["L0_attn_b_Wk"]), "bk": np.ascontiguousarray(inputs["L0_attn_b_bk"].reshape(D, 1)),
        "Wv": _wtile(inputs["L0_attn_b_Wv"]), "bv": np.ascontiguousarray(inputs["L0_attn_b_bv"].reshape(D, 1)),
        "Wo": _wtile(inputs["L0_attn_b_Wo"]), "bo": np.ascontiguousarray(inputs["L0_attn_b_bo"].reshape(D, 1)),
        "w_router": inputs["L0_ffn_w_router"],
        "w_fc": np.ascontiguousarray(
            inputs["L0_ffn_w_fc"].reshape(E, ND, 128, HID)
            .transpose(0, 2, 1, 3).reshape(E, 128, ND * HID)),
        "w_gate": np.ascontiguousarray(
            inputs["L0_ffn_w_gate"].reshape(E, ND, 128, HID)
            .transpose(0, 2, 1, 3).reshape(E, 128, ND * HID)),
        "w_proj": np.ascontiguousarray(
            inputs["L0_ffn_w_proj"].reshape(E, 4, 128, D)
            .transpose(0, 2, 1, 3).reshape(E, 128, 4 * D)),
        "ident": CST["ident"],
        "ident32": CST["ident32"],
        "strictL": CST["strictL"],
        "ones1x128": CST["ones1x128"],
        "ones128sq": CST["ones128sq"],
        "strictL32": CST["strictL32"],
        "iota16": CST["iota16"],
        "iotat": CST["iotat"],
        "lnW": CST["lnW"],
        "Wdecay": CST["Wdecay"],
        "hmaps": CST["hmaps"],
        "kvec": CST["kvec"],
    }
    return m


def kernel(**inputs):
    nc, _ = build_program()
    in_maps = [_prep_inputs(inputs, core) for core in range(8)]
    res = run_bass_kernel_spmd(nc, in_maps, core_ids=list(range(8)))
    return np.stack([r["y"] for r in res.results], axis=0)


# version 16
# speedup vs baseline: 1.0138x; 1.0080x over previous
import numpy as np
import ml_dtypes
from contextlib import ExitStack

import concourse.bass as bass
import concourse.mybir as mybir
from concourse.bass_utils import run_bass_kernel_spmd
from concourse.bass import IndirectOffsetOnAxis

F32 = mybir.dt.float32
BF16 = mybir.dt.bfloat16
I32 = mybir.dt.int32
AF = mybir.ActivationFunctionType
ALU = mybir.AluOpType
AX = mybir.AxisListType

T = 2048
D = 1024
NT = T // 128
ND = D // 128
NH = 16
DK = 64
NST = 16
E = 16
CAP = 256
HID = 512
QT = 512
DECAY = 0.25
NEG = -1e9
TRASH = E * CAP

BF = ml_dtypes.bfloat16


def _host_constants():
    c = {}
    c["ident"] = np.eye(128, dtype=np.float32).astype(BF)
    c["ident32"] = np.eye(128, dtype=np.float32)
    c["strictL"] = np.triu(np.ones((128, 128), dtype=np.float32), 1).astype(BF)
    c["ones1x128"] = np.ones((1, 128), dtype=np.float32).astype(BF)
    c["ones128sq"] = np.ones((128, 128), dtype=np.float32)
    c["strictL32"] = np.triu(np.ones((128, 128), dtype=np.float32), 1)
    c["iota16"] = np.tile(np.arange(16, dtype=np.float32), (128, 1))
    c["iotat"] = np.tile(np.arange(16, dtype=np.float32), (128, 16))
    COFF = QT
    width = (T - 128) + COFF + QT
    j = np.arange(width)[None, :]
    p = np.arange(128)[:, None]
    d = j - COFF - p
    lnw = np.where(d < 0, NEG, np.where(d == 0, 0.0,
                   -DECAY * np.log(np.maximum(d, 1)))).astype(np.float32)
    c["lnW"] = lnw.astype(BF)
    c["Wdecay"] = np.where(d < 0, 0.0, np.exp(np.where(d <= 0, 0.0,
                  -DECAY * np.log(np.maximum(d, 1))))).astype(np.float32).astype(BF)
    c["lnW_COFF"] = COFF
    maps = np.zeros((ND, 128, 16), dtype=np.float32)
    for fd in range(ND):
        maps[fd, 0:64, 2 * fd] = 1.0
        maps[fd, 64:128, 2 * fd + 1] = 1.0
    c["hmaps"] = maps.astype(BF)
    kv = np.zeros((128, 8), dtype=np.float32)
    kv[:, 0] = 1e-5; kv[:, 1] = 1.0; kv[:, 2] = -0.5; kv[:, 3] = -1.0 / D; kv[:, 4] = -1.0; kv[:, 5] = 0.5
    c["kvec"] = kv
    return c


CST = _host_constants()


def build_program(debug=()):
    nc = bass.Bass("TRN2", target_bir_lowering=False,
                   detect_race_conditions=False)

    din = {}
    def dram_in(name, shape, dtype=F32):
        din[name] = nc.dram_tensor(name, list(shape), dtype, kind="ExternalInput").ap()
        return din[name]

    x_in = dram_in("x", (T, D))
    W_in_ = dram_in("W_in", (D, 32))
    b_in_ = dram_in("b_in", (32, 1))
    W_dt1 = dram_in("W_dt1", (D, 64))
    b_dt1 = dram_in("b_dt1", (64, 1))
    W_dt2 = dram_in("W_dt2", (64, D))
    b_dt2 = dram_in("b_dt2", (D, 1))
    A_cols = dram_in("A_cols", (128, 128))
    D_row = dram_in("D_row", (1, D))
    Wq = dram_in("Wq", (128, ND * D)); bq = dram_in("bq", (D, 1))
    Wk = dram_in("Wk", (128, ND * D)); bk = dram_in("bk", (D, 1))
    Wv = dram_in("Wv", (128, ND * D)); bv = dram_in("bv", (D, 1))
    Wo = dram_in("Wo", (128, ND * D)); bo = dram_in("bo", (D, 1))
    w_router = dram_in("w_router", (D, 16))
    w_fc = dram_in("w_fc", (E, 128, ND * HID))
    w_gate = dram_in("w_gate", (E, 128, ND * HID))
    w_proj = dram_in("w_proj", (E, 128, 4 * D))
    ident_in = dram_in("ident", (128, 128), BF16)
    ident32_in = dram_in("ident32", (128, 128))
    strictL_in = dram_in("strictL", (128, 128), BF16)
    ones1x128_in = dram_in("ones1x128", (1, 128), BF16)
    ones128_in = dram_in("ones128sq", (128, 128))
    strictL32_in = dram_in("strictL32", (128, 128))
    iota16_in = dram_in("iota16", (128, 16))
    iotat_in = dram_in("iotat", (128, 256))
    lnW_in = dram_in("lnW", CST["lnW"].shape, BF16)
    Wdec_in = dram_in("Wdecay", CST["Wdecay"].shape, BF16)
    hmaps_in = dram_in("hmaps", (ND, 128, 16), BF16)
    kvec_in = dram_in("kvec", (128, 8))

    y_out = nc.dram_tensor("y", [T, D], F32, kind="ExternalOutput").ap()

    dbg_outs = {}
    def dbg(name, shape, dtype=F32, force=False):
        if name in debug or force:
            dbg_outs[name] = nc.dram_tensor("dbg_" + name, list(shape), dtype,
                                            kind="ExternalOutput").ap()
        return dbg_outs.get(name)

    x_d = nc.dram_tensor("x_spill", [T, D], F32, kind="Internal").ap()
    xn_d = nc.dram_tensor("xn_d", [T, D], BF16, kind="Internal").ap()
    Xtm_d = nc.dram_tensor("Xtm_d", [33 * 128, D], BF16, kind="Internal").ap()
    o_d = nc.dram_tensor("o_d", [E * CAP, D], BF16, kind="Internal").ap()
    bcast_d = nc.dram_tensor("bcast_d", [64, T], BF16, kind="Internal").ap()
    brow_d = nc.dram_tensor("brow_d", [1, D], F32, kind="Internal").ap()
    brow2_d = nc.dram_tensor("brow2_d", [1, 64], F32, kind="Internal").ap()
    srow_d = nc.dram_tensor("srow_d", [1, 1024], F32, kind="Internal").ap()
    dscr_d = nc.dram_tensor("dscr_d", [64, 1024], F32, kind="Internal").ap()
    qT_d = nc.dram_tensor("qT_d", [D, T], BF16, kind="Internal").ap()

    ctx = ExitStack()
    with ctx:
        EA = ctx.enter_context

        x_sb = EA(nc.sbuf_tensor("x_sb", [128, NT * D], F32))
        ident = EA(nc.sbuf_tensor("ident_s", [128, 128], BF16))
        ident32 = EA(nc.sbuf_tensor("ident32_s", [128, 128], F32))
        strictL = EA(nc.sbuf_tensor("strictL_s", [128, 128], BF16))
        ones1x128 = EA(nc.sbuf_tensor("ones1x128_s", [1, 128], BF16))
        ones128 = EA(nc.sbuf_tensor("ones128sq_s", [128, 128], F32))
        strictL32 = EA(nc.sbuf_tensor("strictL32_s", [128, 128], F32))
        zero_bf = EA(nc.sbuf_tensor("zero_bf", [128, 512], BF16))
        iotat = EA(nc.sbuf_tensor("iotat_s", [128, 256], F32))
        Wdec = EA(nc.sbuf_tensor("Wdec_s", list(CST["Wdecay"].shape), BF16))
        hmaps = EA(nc.sbuf_tensor("hmaps_s", [128, ND * 16], BF16))
        A_sb = EA(nc.sbuf_tensor("A_sb", [128, 128], F32))
        recipA = EA(nc.sbuf_tensor("recipA", [128, 128], F32))
        Dcol = EA(nc.sbuf_tensor("Dcol", [128, ND], F32))
        biases = EA(nc.sbuf_tensor("biases", [128, 48], F32))
        stats = EA(nc.sbuf_tensor("stats", [128, 16 * NT], F32))
        statsL = EA(nc.sbuf_tensor("statsL", [128, 2 * NT], F32))
        sq_scr = EA(nc.sbuf_tensor("sq_scr", [128, D], F32))
        kvec = EA(nc.sbuf_tensor("kvec_s", [128, 8], F32))

        block = EA(nc.Block())

        sem = {}
        cnt = {}
        for name in ["L", "T", "V", "A", "G", "S", "Q"]:
            sem[name] = EA(nc.semaphore("sem" + name))
            cnt[name] = 0

        def inc(name, instr, k=1):
            instr.then_inc(sem[name], k)
            cnt[name] += k
            return cnt[name]

        def w(eng, name, val=None):
            eng.wait_ge(sem[name], cnt[name] if val is None else val)

        def phase_barrier():
            snap = dict(cnt)
            for blk in [block.tensor, block.vector, block.scalar,
                        block.gpsimd, block.sync]:
                @blk
                def _(eng):
                    for k in ["T", "V", "A", "G", "S", "L"]:
                        eng.wait_ge(sem[k], snap[k])

        @block.sync
        def _(sync):
            for i in range(NT):
                inc("L", sync.dma_start(out=x_sb[:, i * D:(i + 1) * D],
                                        in_=x_in[i * 128:(i + 1) * 128, :]), 16)
            inc("L", sync.dma_start(out=ident[:], in_=ident_in), 16)
            inc("L", sync.dma_start(out=ident32[:], in_=ident32_in), 16)
            inc("L", sync.dma_start(out=strictL[:], in_=strictL_in), 16)
            inc("L", sync.dma_start(out=ones1x128[:], in_=ones1x128_in), 16)
            inc("L", sync.dma_start(out=ones128[:], in_=ones128_in), 16)
            inc("L", sync.dma_start(out=strictL32[:], in_=strictL32_in), 16)
            inc("L", sync.dma_start(out=iotat[:], in_=iotat_in), 16)
            inc("L", sync.dma_start(out=Wdec[:], in_=Wdec_in), 16)
            for fd in range(ND):
                inc("L", sync.dma_start(out=hmaps[:, fd * 16:(fd + 1) * 16],
                                        in_=hmaps_in[fd]), 16)
            inc("L", sync.dma_start(out=A_sb[:], in_=A_cols), 16)
            for fd in range(ND):
                inc("L", sync.dma_start(out=Dcol[:, fd:fd + 1],
                                        in_=D_row[0:1, fd * 128:(fd + 1) * 128]
                                        .rearrange("a b -> b a")), 16)
            for j, b in enumerate([bq, bk, bv, bo, b_dt2]):
                for fd in range(ND):
                    inc("L", sync.dma_start(out=biases[:, 8 * j + fd:8 * j + fd + 1],
                                            in_=b[fd * 128:(fd + 1) * 128, :]), 16)
            inc("L", sync.dma_start(out=biases[0:32, 40:41], in_=b_in_), 16)
            inc("L", sync.dma_start(out=biases[0:64, 41:42], in_=b_dt1), 16)
            inc("L", sync.dma_start(out=kvec[:], in_=kvec_in), 16)
        loads0 = cnt["L"]

        @block.vector
        def _(vector):
            w(vector, "L", loads0)
            inc("V", vector.reciprocal(recipA[:], A_sb[:]))
            inc("V", vector.memset(zero_bf[:], 0.0))
        recipA_done = cnt["V"]

        @block.gpsimd
        def _(gpsimd):
            w(gpsimd, "V", recipA_done)
            for r in range(33):
                for hf in range(2):
                    inc("G", gpsimd.dma_start(
                        out=Xtm_d[r * 128:(r + 1) * 128, hf * 512:(hf + 1) * 512],
                        in_=zero_bf[:]), 16)
        xtm_zeroed = cnt["G"]

        def emit_layernorm(xn_sb, ready=None, stream_dram=None,
                           tile_marks=None):
            @block.vector
            def _(vector):
                if ready is not None:
                    w(vector, ready[0], ready[1])
                for i in range(NT):
                    inc("V", vector.bn_stats(stats[:, 12 * i:12 * i + 6],
                                             x_sb[:, i * D:i * D + 512]))
                    inc("V", vector.bn_stats(stats[:, 12 * i + 6:12 * i + 12],
                                             x_sb[:, i * D + 512:(i + 1) * D]))
                vector.wait_ge(sem["V"], cnt["V"])
                for i in range(NT):
                    inc("V", vector.bn_aggr(stats[:, 12 * NT + 2 * i:12 * NT + 2 * i + 2],
                                            stats[:, 12 * i:12 * i + 12]))
            v_bn = cnt["V"]

            @block.scalar
            def _(scalar):
                w(scalar, "V", v_bn)
                w(scalar, "L", loads0)
                inc("A", scalar.activation(statsL[:, 0:NT],
                                           stats[:, 12 * NT:14 * NT:2],
                                           AF.Identity, scale=kvec[:, 4:5]))
                inc("A", scalar.activation(stats[:, 15 * NT:16 * NT],
                                           stats[:, 12 * NT + 1:14 * NT:2],
                                           AF.Ln, bias=kvec[:, 0:1]))
                inc("A", scalar.activation(statsL[:, NT:2 * NT],
                                           stats[:, 15 * NT:16 * NT],
                                           AF.Exp, scale=kvec[:, 2:3]))
            a_rsig = cnt["A"]

            @block.vector
            def _(vector):
                w(vector, "A", a_rsig)
            for i in range(NT):
                @block.vector
                def _(vector, i=i):
                    inc("V", vector.tensor_tensor(
                        sq_scr[:],
                        x_sb[:, i * D:(i + 1) * D],
                        statsL[:, i:i + 1].broadcast_to([128, D]),
                        ALU.add))
                    inc("V", vector.tensor_tensor(
                        xn_sb[:, i * D:(i + 1) * D],
                        sq_scr[:],
                        statsL[:, NT + i:NT + i + 1].broadcast_to([128, D]),
                        ALU.mult))
                v_tile = cnt["V"]
                if tile_marks is not None:
                    tile_marks.append(v_tile)
                if stream_dram is not None:
                    @block.sync
                    def _(sync, i=i, v_tile=v_tile):
                        w(sync, "V", v_tile)
                        inc("S", sync.dma_start(
                            out=stream_dram[i * 128:(i + 1) * 128, :],
                            in_=xn_sb[:, i * D:(i + 1) * D]), 16)
            return cnt["V"]

        def emit_layernorm_dram(dram_out, stage_sb, ready=None):
            @block.vector
            def _(vector):
                if ready is not None:
                    w(vector, ready[0], ready[1])
                for i in range(NT):
                    inc("V", vector.bn_stats(stats[:, 12 * i:12 * i + 6],
                                             x_sb[:, i * D:i * D + 512]))
                    inc("V", vector.bn_stats(stats[:, 12 * i + 6:12 * i + 12],
                                             x_sb[:, i * D + 512:(i + 1) * D]))
                vector.wait_ge(sem["V"], cnt["V"])
                for i in range(NT):
                    inc("V", vector.bn_aggr(stats[:, 12 * NT + 2 * i:12 * NT + 2 * i + 2],
                                            stats[:, 12 * i:12 * i + 12]))
            v_bn = cnt["V"]

            @block.scalar
            def _(scalar):
                w(scalar, "V", v_bn)
                inc("A", scalar.activation(statsL[:, 0:NT],
                                           stats[:, 12 * NT:14 * NT:2],
                                           AF.Identity, scale=kvec[:, 4:5]))
                inc("A", scalar.activation(stats[:, 15 * NT:16 * NT],
                                           stats[:, 12 * NT + 1:14 * NT:2],
                                           AF.Ln, bias=kvec[:, 0:1]))
                inc("A", scalar.activation(statsL[:, NT:2 * NT],
                                           stats[:, 15 * NT:16 * NT],
                                           AF.Exp, scale=kvec[:, 2:3]))
            a_rsig = cnt["A"]

            st_marks = {}
            @block.vector
            def _(vector):
                w(vector, "A", a_rsig)
            for i in range(NT):
                @block.vector
                def _(vector, i=i):
                    if i >= 2:
                        w(vector, "S", st_marks[i - 2])
                    inc("V", vector.tensor_tensor(
                        sq_scr[:], x_sb[:, i * D:(i + 1) * D],
                        statsL[:, i:i + 1].broadcast_to([128, D]), ALU.add))
                    inc("V", vector.tensor_tensor(
                        stage_sb[:, (i % 2) * D:(i % 2 + 1) * D], sq_scr[:],
                        statsL[:, NT + i:NT + i + 1].broadcast_to([128, D]),
                        ALU.mult))
                v_ap = cnt["V"]

                @block.sync
                def _(sync, i=i, v_ap=v_ap):
                    w(sync, "V", v_ap)
                    inc("S", sync.dma_start(out=dram_out[i * 128:(i + 1) * 128, :],
                                            in_=stage_sb[:, (i % 2) * D:(i % 2 + 1) * D]), 16)
                st_marks[i] = cnt["S"]
            return cnt["S"]

        def emit_tm_to_fm(src_sb, dst_fm, ps_pair, ready_v, tile_marks=None):
            copy_marks = {}
            for k, (fd, i) in enumerate([(fd, i) for fd in range(ND) for i in range(NT)]):
                pst = ps_pair[k % 2]

                @block.tensor
                def _(tensor, fd=fd, i=i, pst=pst, k=k):
                    w(tensor, "V", copy_marks.get(k - 2, 0))
                    w(tensor, "V", tile_marks[i] if tile_marks is not None
                      else ready_v)
                    inc("T", tensor.transpose(
                        pst[:], src_sb[:, i * D + fd * 128:i * D + fd * 128 + 128],
                        ident[:]))
                t_tr = cnt["T"]

                @block.vector
                def _(vector, fd=fd, i=i, pst=pst, t_tr=t_tr):
                    w(vector, "T", t_tr)
                    inc("V", vector.tensor_copy(dst_fm[fd][:, i * 128:(i + 1) * 128],
                                                pst[:]))
                copy_marks[k] = cnt["V"]
            return cnt["V"]

        with ExitStack() as mctx:
            ME = mctx.enter_context
            xn_fm = [ME(nc.sbuf_tensor(f"xn_fm{fd}", [128, T], BF16)) for fd in range(ND)]
            ps_p_ctx = nc.psum_tensor("ps_p", [64, T], F32)
            ps_p = ps_p_ctx.__enter__()
            Win_sb = ME(nc.sbuf_tensor("Win_sb", [128, ND * 32], BF16))
            Wdt1_sb = ME(nc.sbuf_tensor("Wdt1_sb", [128, ND * 64], BF16))
            Wdt2_sb = ME(nc.sbuf_tensor("Wdt2_sb", [64, D], BF16))
            BCu = ME(nc.sbuf_tensor("BCu", [32, T], BF16))
            u_sb = ME(nc.sbuf_tensor("u_sb", [64, T], BF16))
            with nc.sbuf_tensor("xn_tm", [128, NT * D], BF16) as xn_tm, \
                 nc.psum_tensor("ps_trA", [128, 128], BF16) as ps_trA, \
                 nc.psum_tensor("ps_trB", [128, 128], BF16) as ps_trB:
                ln1_marks = []
                ln1_v = emit_layernorm(xn_tm, ready=("L", loads0),
                                       tile_marks=ln1_marks)
                if "lnstats" in debug:
                    dbg("lnstats", (128, 2 * NT))
                    @block.sync
                    def _(sync):
                        w(sync, "V", ln1_v)
                        inc("S", sync.dma_start(out=dbg_outs["lnstats"], in_=statsL[:]), 16)
                if "xn_tm" in debug:
                    dbg("xn_tm", (128, NT * D), BF16)
                    @block.sync
                    def _(sync):
                        w(sync, "V", ln1_v)
                        inc("S", sync.dma_start(out=dbg_outs["xn_tm"], in_=xn_tm[:]), 16)
                tr1_v = emit_tm_to_fm(xn_tm, xn_fm, [ps_trA, ps_trB], ln1_v,
                                      tile_marks=ln1_marks)

            if "xn_fm" in debug:
                dbg("xn_fm", (D, T), BF16)
                @block.sync
                def _(sync):
                    w(sync, "V", tr1_v)
                    for fd in range(ND):
                        inc("S", sync.dma_start(out=dbg_outs["xn_fm"][fd * 128:(fd + 1) * 128, :],
                                                in_=xn_fm[fd][:]), 16)

            @block.gpsimd
            def _(gpsimd):
                for fd in range(ND):
                    inc("G", gpsimd.dma_start(out=Win_sb[:, fd * 32:(fd + 1) * 32],
                                              in_=W_in_[fd * 128:(fd + 1) * 128, :]), 16)
                    inc("G", gpsimd.dma_start(out=Wdt1_sb[:, fd * 64:(fd + 1) * 64],
                                              in_=W_dt1[fd * 128:(fd + 1) * 128, :]), 16)
                inc("G", gpsimd.dma_start(out=Wdt2_sb[:], in_=W_dt2), 16)
            mw_l = cnt["G"]

            if "Win" in debug:
                dbg("Win", (128, ND * 32), BF16)
                @block.sync
                def _(sync):
                    w(sync, "G", mw_l)
                    inc("S", sync.dma_start(out=dbg_outs["Win"], in_=Win_sb[:]), 16)


            @block.tensor
            def _(tensor):
                w(tensor, "G", mw_l)
                w(tensor, "V", tr1_v)
                for hf in range(4):
                    for fd in range(ND):
                        inc("T", tensor.matmul(ps_p[0:32, hf * 512:(hf + 1) * 512],
                                               Win_sb[:, fd * 32:(fd + 1) * 32],
                                               xn_fm[fd][:, hf * 512:(hf + 1) * 512],
                                               start=(fd == 0), stop=(fd == ND - 1)))
            t_bc = cnt["T"]

            @block.scalar
            def _(scalar):
                w(scalar, "A", 0)
                w(scalar, "T", t_bc)
                inc("A", scalar.activation(BCu[0:32, :], ps_p[0:32, :], AF.Identity,
                                           bias=biases[0:32, 40:41]))
            a_bc = cnt["A"]

            @block.tensor
            def _(tensor):
                w(tensor, "A", a_bc)
                for hf in range(4):
                    for fd in range(ND):
                        inc("T", tensor.matmul(ps_p[:, hf * 512:(hf + 1) * 512],
                                               Wdt1_sb[:, fd * 64:(fd + 1) * 64],
                                               xn_fm[fd][:, hf * 512:(hf + 1) * 512],
                                               start=(fd == 0), stop=(fd == ND - 1)))
            t_u = cnt["T"]

            @block.scalar
            def _(scalar):
                w(scalar, "T", t_u)
                inc("A", scalar.activation(u_sb[:], ps_p[:], AF.Identity,
                                           bias=biases[0:64, 41:42]))
            a_u = cnt["A"]

            @block.sync
            def _(sync):
                w(sync, "A", a_u)
                inc("S", sync.dma_start(out=bcast_d[0:32, :], in_=BCu[:]), 16)
            s_bc = cnt["S"]

            if "BCu" in debug:
                dbg("BCu", (96, T), BF16)
                @block.sync
                def _(sync):
                    w(sync, "A", a_u)
                    inc("S", sync.dma_start(out=dbg_outs["BCu"][0:32, :], in_=BCu[:]), 16)
                    inc("S", sync.dma_start(out=dbg_outs["BCu"][32:96, :], in_=u_sb[:]), 16)

            ps_p_ctx.__exit__(None, None, None)

            dt_sb = ME(nc.sbuf_tensor("dt_sb", [128, T], BF16))
            e_scr = ME(nc.sbuf_tensor("e_scr", [128, 1024], F32))
            abar = [ME(nc.sbuf_tensor(f"abar{i}", [128, T], F32)) for i in range(2)]
            wr_sb = [ME(nc.sbuf_tensor(f"wr_sb{i}", [128, T], BF16)) for i in range(2)]
            dum_s = ME(nc.sbuf_tensor("dum_s", [1, 64], BF16))
            bx_sb = [ME(nc.sbuf_tensor(f"bx_sb{i}", [128, T], BF16)) for i in range(2)]
            h_sb = [ME(nc.sbuf_tensor(f"h_sb{i}", [128, T], BF16)) for i in range(2)]
            z_sb = [ME(nc.sbuf_tensor(f"z_sb{i}", [128, T], BF16)) for i in range(2)]
            bmc = [ME(nc.sbuf_tensor(f"bmc{i}", [128, T], BF16)) for i in range(3)]
            bmcC = [ME(nc.sbuf_tensor(f"bmcC{i}", [128, T], BF16)) for i in range(3)]
            ps_dt = ME(nc.psum_tensor("ps_dt", [128, 1024], F32))
            ps_y = ME(nc.psum_tensor("ps_y", [128, T], F32))
            ps_yt = [ME(nc.psum_tensor(f"ps_yt{i}", [128, 128], F32)) for i in range(2)]

            bmc_marks, ab_marks, wr_marks, bx_marks = {}, {}, {}, {}
            scan_marks, z_marks, zmm_marks, ty_marks, add_marks = {}, {}, {}, {}, {}
            v_scan_prev = [0]

            if "dt" in debug:
                dbg("dt", (D, T), BF16)
            if "y_fm" in debug:
                dbg("y_fm", (D, T))
            if "abar0" in debug:
                dbg("abar0", (128, T))

            for db in range(ND):
                T0, A0, V0g = cnt["T"], cnt["A"], cnt["V"]

                @block.tensor
                def _(tensor, db=db, A0=A0):
                    if db == 0:
                        w(tensor, "A", a_u)
                    for hf in range(2):
                        if hf == 1:
                            tensor.wait_ge(sem["A"], A0 + 1)
                        else:
                            tensor.wait_ge(sem["A"], A0)
                        for qq in range(2):
                            inc("T", tensor.matmul(
                                ps_dt[:, qq * 512:(qq + 1) * 512],
                                Wdt2_sb[:, db * 128:(db + 1) * 128],
                                u_sb[:, hf * 1024 + qq * 512:hf * 1024 + (qq + 1) * 512],
                                start=True, stop=True))

                @block.scalar
                def _(scalar, db=db, T0=T0, A0=A0, V0g=V0g):
                    for hf in range(2):
                        scalar.wait_ge(sem["T"], T0 + 2 * (hf + 1))
                        inc("A", scalar.activation(e_scr[:], ps_dt[:], AF.Exp,
                                                   bias=biases[:, 32 + db:33 + db]))
                        inc("A", scalar.activation(dt_sb[:, hf * 1024:(hf + 1) * 1024],
                                                   e_scr[:], AF.Ln, bias=kvec[:, 1:2]))
                    scalar.wait_ge(sem["V"], V0g + 1)
                    for n in range(NST):
                        col = db * 16 + n
                        if n < 2:
                            scalar.wait_ge(sem["V"], add_marks.get(db - 1, 0))
                        else:
                            scalar.wait_ge(sem["V"], V0g + 1 + 5 * (n - 2) + 4)
                        inc("A", scalar.activation(abar[n % 2][:], dt_sb[:], AF.Exp,
                                                   scale=A_sb[:, col:col + 1]))
                a_ab_last = cnt["A"]

                @block.vector
                def _(vector, db=db, A0=A0):
                    w(vector, "A", A0 + 4)
                    inc("V", vector.tensor_scalar(dt_sb[:], dt_sb[:], 1e-7, 1e6,
                                                  ALU.max, ALU.min))
                v_dt = cnt["V"]

                if "dt" in debug:
                    @block.sync
                    def _(sync, db=db, v_dt=v_dt):
                        w(sync, "V", v_dt)
                        inc("S", sync.dma_start(out=dbg_outs["dt"][db * 128:(db + 1) * 128, :],
                                                in_=dt_sb[:]), 16)

                ab_base = a_ab_last - NST

                for n in range(NST):
                    col = db * 16 + n
                    g = db * NST + n
                    b2 = g % 2
                    b3 = g % 3
                    ab_marks[g] = ab_base + n + 1

                    @block.sync
                    def _(sync, db=db, n=n, g=g, b3=b3):
                        if db == 0 and n == 0:
                            w(sync, "S", s_bc)
                        w(sync, "V", wr_marks.get(g - 3, 0))
                        w(sync, "V", z_marks.get(g - 3, 0))
                        inc("S", sync.dma_start(
                            out=bmc[b3][:],
                            in_=bcast_d[n:n + 1, :].partition_broadcast(128)), 16)
                        inc("S", sync.dma_start(
                            out=bmcC[b3][:],
                            in_=bcast_d[16 + n:17 + n, :].partition_broadcast(128)), 16)
                    bmc_marks[g] = cnt["S"]

                    Vb = cnt["V"]
                    @block.vector
                    def _(vector, db=db, n=n, g=g, b2=b2, b3=b3, col=col):
                        w(vector, "S", bmc_marks[g] + 32)
                        inc("V", vector.tensor_mul(wr_sb[b2][:], xn_fm[db][:],
                                                   bmc[b3][:]))
                        w(vector, "A", ab_marks[g])
                        inc("V", vector.tensor_scalar(bx_sb[b2][:], abar[b2][:],
                                                      -1.0, recipA[:, col:col + 1],
                                                      ALU.add, ALU.mult))
                        inc("V", vector.tensor_mul(bx_sb[b2][:], bx_sb[b2][:],
                                                   wr_sb[b2][:]))
                    wr_marks[g] = Vb + 1
                    bx_marks[g] = cnt["V"]

                    @block.vector
                    def _(vector, g=g, b2=b2):
                        w(vector, "T", zmm_marks.get(g - 2, 0))
                        inc("V", vector.tensor_tensor_scan(
                            h_sb[b2][:], abar[b2][:], bx_sb[b2][:], 0.0,
                            ALU.mult, ALU.add))
                    scan_marks[g] = cnt["V"]

                    @block.vector
                    def _(vector, g=g, b2=b2, b3=b3, n=n):
                        w(vector, "T", zmm_marks.get(g - 2, 0))
                        if n <= 1:
                            w(vector, "T", ty_marks.get(g // NST - 1, 0))
                        inc("V", vector.tensor_mul(z_sb[b2][:], h_sb[b2][:],
                                                   bmcC[b3][:]))
                    z_marks[g] = cnt["V"]

                    @block.tensor
                    def _(tensor, db=db, n=n, g=g, b2=b2):
                        w(tensor, "V", z_marks[g])
                        for hf in range(4):
                            inc("T", tensor.matmul(ps_y[:, hf * 512:(hf + 1) * 512],
                                                   ident[:],
                                                   z_sb[b2][:, hf * 512:(hf + 1) * 512],
                                                   start=(n == 0), stop=False))
                    zmm_marks[g] = cnt["T"]

                if db == ND - 1:
                    @block.sync
                    def _(sync):
                        inc("S", sync.dma_start(out=dum_s[0:1, 0:64],
                                                in_=bcast_d[0:1, 0:64]), 16)
                        inc("S", sync.dma_start(out=dum_s[0:1, 0:64],
                                                in_=bcast_d[0:1, 0:64]), 16)

                @block.vector
                def _(vector, db=db):
                    glast = db * NST + NST - 1
                    w(vector, "T", zmm_marks.get(glast, 0))
                    w(vector, "T", zmm_marks.get(glast - 1, 0))
                    inc("V", vector.tensor_scalar(z_sb[0][:], xn_fm[db][:],
                                                  Dcol[:, db:db + 1], None, ALU.mult))
                    inc("V", vector.scalar_tensor_tensor(z_sb[1][:], xn_fm[db][:],
                                                         Dcol[:, db:db + 1], z_sb[0][:],
                                                         ALU.mult, ALU.subtract))
                v_dxn = cnt["V"]

                @block.tensor
                def _(tensor, db=db, v_dxn=v_dxn):
                    w(tensor, "V", v_dxn)
                    for hf in range(4):
                        inc("T", tensor.matmul(ps_y[:, hf * 512:(hf + 1) * 512],
                                               ident[:], z_sb[0][:, hf * 512:(hf + 1) * 512],
                                               start=False, stop=False))
                    for hf in range(4):
                        inc("T", tensor.matmul(ps_y[:, hf * 512:(hf + 1) * 512],
                                               ident[:], z_sb[1][:, hf * 512:(hf + 1) * 512],
                                               start=False, stop=True))
                t_y = cnt["T"]
                ty_marks[db] = t_y

                @block.scalar
                def _(scalar, db=db, t_y=t_y):
                    w(scalar, "T", t_y)
                    inc("A", scalar.activation(abar[0][:], ps_y[:], AF.Copy))
                a_y = cnt["A"]

                if "y_fm" in debug:
                    @block.sync
                    def _(sync, db=db, a_y=a_y):
                        w(sync, "A", a_y)
                        inc("S", sync.dma_start(out=dbg_outs["y_fm"][db * 128:(db + 1) * 128, :],
                                                in_=abar[0][:]), 16)

                last_add = [0]
                for i in range(NT):
                    pst = ps_yt[i % 2]

                    @block.tensor
                    def _(tensor, db=db, i=i, pst=pst, a_y=a_y):
                        w(tensor, "A", a_y)
                        w(tensor, "V", last_add[0])
                        inc("T", tensor.transpose(pst[:],
                                                  abar[0][:, i * 128:(i + 1) * 128],
                                                  ident32[:]))
                    t_tr = cnt["T"]

                    @block.vector
                    def _(vector, db=db, i=i, pst=pst, t_tr=t_tr):
                        w(vector, "T", t_tr)
                        inc("V", vector.tensor_add(
                            x_sb[:, i * D + db * 128:i * D + db * 128 + 128],
                            x_sb[:, i * D + db * 128:i * D + db * 128 + 128],
                            pst[:]))
                    last_add[0] = cnt["V"]
                add_marks[db] = cnt["V"]
                @block.tensor
                def _(tensor, a_y=a_y):
                    w(tensor, "A", a_y)

            mamba_v = cnt["V"]

            if "x_after_mamba" in debug:
                dbg("x_after_mamba", (T, D))
                @block.sync
                def _(sync):
                    w(sync, "V", mamba_v)
                    for i in range(NT):
                        inc("S", sync.dma_start(
                            out=dbg_outs["x_after_mamba"][i * 128:(i + 1) * 128, :],
                            in_=x_sb[:, i * D:(i + 1) * D]), 16)

            @block.sync
            def _(sync):
                w(sync, "V", mamba_v)
                w(sync, "T", cnt["T"])
                w(sync, "A", cnt["A"])
                w(sync, "G", cnt["G"])


        def emit_moe(tag, ready_v):
            with ExitStack() as fctx:
                FE = fctx.enter_context
                rsc = FE(nc.sbuf_tensor(f"rsc{tag}", [128, 64], F32))
                tk_sb = FE(nc.sbuf_tensor(f"tk{tag}", [128, 2 * NT], F32))
                ee_sb = FE(nc.sbuf_tensor(f"ee{tag}", [128, 2 * NT], F32))
                pos_sb = FE(nc.sbuf_tensor(f"pos{tag}", [128, 2 * NT], F32))
                I1_sb = FE(nc.sbuf_tensor(f"I1{tag}", [128, 16 * NT], F32))
                I2_sb = FE(nc.sbuf_tensor(f"I2{tag}", [128, 16 * NT], F32))
                Irun = FE(nc.sbuf_tensor(f"Irun{tag}", [128, 16], F32))
                poscum = FE(nc.sbuf_tensor(f"pc{tag}", [128, 16], F32))
                sidx = FE(nc.sbuf_tensor(f"sidx{tag}", [128, 2 * NT], I32))
                gidx = FE(nc.sbuf_tensor(f"gidx{tag}", [128, 2 * NT], I32))
                pexp = FE(nc.sbuf_tensor(f"pexp{tag}", [128, 16], F32))
                logits_tm = FE(nc.sbuf_tensor(f"ltm{tag}", [128, 16 * NT], BF16))
                Wr_sb = FE(nc.sbuf_tensor(f"Wr{tag}", [128, ND * 16], BF16))
                eph_stack = ExitStack()
                _EP = eph_stack.enter_context
                xn2 = _EP(nc.sbuf_tensor(f"xn2{tag}", [128, NT * D], BF16))
                lgf32 = _EP(nc.sbuf_tensor(f"lgf32{tag}", [128, 256], F32))
                m8all = _EP(nc.sbuf_tensor(f"m8a{tag}", [128, 128], F32))
                v1a = _EP(nc.sbuf_tensor(f"v1a{tag}", [128, 16], F32))
                v2a = _EP(nc.sbuf_tensor(f"v2a{tag}", [128, 16], F32))
                v2a_raw = _EP(nc.sbuf_tensor(f"v2ar{tag}", [128, 16], F32))
                suba = _EP(nc.sbuf_tensor(f"suba{tag}", [128, 256], F32))
                expa = _EP(nc.sbuf_tensor(f"expa{tag}", [128, 256], F32))
                suma = _EP(nc.sbuf_tensor(f"suma{tag}", [128, 16], F32))
                cuma = _EP(nc.sbuf_tensor(f"cuma{tag}", [128, 256], F32))
                z16 = _EP(nc.sbuf_tensor(f"z16{tag}", [128, 16], F32))
                fscr = _EP(nc.sbuf_tensor(f"fscr{tag}", [128, 16], F32))
                poscumA = _EP(nc.sbuf_tensor(f"pcA{tag}", [128, 256], F32))
                poscumB = _EP(nc.sbuf_tensor(f"pcB{tag}", [128, 256], F32))

                ln_v = emit_layernorm(xn2, ready=("V", ready_v),
                                      stream_dram=xn_d)
                s_xnd = cnt["S"]

                @block.gpsimd
                def _(gpsimd):
                    w(gpsimd, "V", ready_v)
                    inc("G", gpsimd.dma_start(
                        out=Wr_sb[:].rearrange("p (c e) -> p c e", c=ND),
                        in_=w_router.rearrange("(c p) e -> p c e", p=128)), 16)
                g_wr = cnt["G"]

                with ExitStack() as rctx:
                    RE = rctx.enter_context
                    xnT = [RE(nc.sbuf_tensor(f"xnT{tag}{fd}", [128, T], BF16))
                           for fd in range(ND)]
                    ps_r = RE(nc.psum_tensor(f"ps_r{tag}", [16, T], F32))
                    ps_rt = RE(nc.psum_tensor(f"ps_rt{tag}", [128, 16], BF16))
                    logF = RE(nc.sbuf_tensor(f"logF{tag}", [16, T], BF16))

                    @block.sync
                    def _(sync):
                        w(sync, "S", s_xnd)
                        for fd in range(ND):
                            inc("L", sync.dma_start(
                                out=xnT[fd][:],
                                in_=xn_d[:, fd * 128:(fd + 1) * 128],
                                transpose=True), 16)
                    l_xnT = cnt["L"]

                    @block.tensor
                    def _(tensor):
                        w(tensor, "L", l_xnT)
                        w(tensor, "G", g_wr)
                        for q in range(4):
                            for fd in range(ND):
                                inc("T", tensor.matmul(
                                    ps_r[:, q * 512:(q + 1) * 512],
                                    Wr_sb[:, fd * 16:(fd + 1) * 16],
                                    xnT[fd][:, q * 512:(q + 1) * 512],
                                    start=(fd == 0), stop=(fd == ND - 1)))
                    t_rlog = cnt["T"]

                    @block.scalar
                    def _(scalar):
                        w(scalar, "T", t_rlog)
                        inc("A", scalar.activation(logF[:], ps_r[:], AF.Copy))
                    a_rlog = cnt["A"]

                    lastc = [0]
                    for i in range(NT):
                        @block.tensor
                        def _(tensor, i=i, a_rlog=a_rlog):
                            w(tensor, "A", a_rlog)
                            w(tensor, "V", lastc[0])
                            inc("T", tensor.transpose(
                                ps_rt[:], logF[:, i * 128:(i + 1) * 128],
                                ident[0:16, 0:16]))
                        t_tr = cnt["T"]

                        @block.vector
                        def _(vector, i=i, t_tr=t_tr):
                            w(vector, "T", t_tr)
                            inc("V", vector.tensor_copy(
                                logits_tm[:, i * 16:(i + 1) * 16], ps_rt[:]))
                        lastc[0] = cnt["V"]
                    v_log = cnt["V"]


                def grp(ap3):
                    return ap3.rearrange("p (g c) -> p g c", g=16)

                def bgrp(col16):
                    return col16.unsqueeze(2).broadcast_to([128, 16, 16])

                @block.vector
                def _(vector):
                    def flr():
                        inc("V", vector.tensor_copy(fscr[:], z16[:]))
                    w(vector, "V", v_log)
                    inc("V", vector.memset(z16[:], 0.0))
                    inc("V", vector.tensor_copy(lgf32[:], logits_tm[:]))
                    flr(); flr()
                    for i in range(NT):
                        inc("V", vector.max(m8all[:, 8 * i:8 * i + 8],
                                            lgf32[:, 16 * i:16 * i + 16]))
                    flr(); flr()
                    inc("V", vector.tensor_copy(v1a[:], m8all[:, 0:128:8]))
                    inc("V", vector.tensor_copy(v2a[:], m8all[:, 1:128:8]))
                    inc("V", vector.tensor_copy(v2a_raw[:], m8all[:, 1:128:8]))
                    flr(); flr()
                    inc("V", vector.tensor_tensor(grp(suba[:]), grp(lgf32[:]),
                                                  bgrp(v1a[:]), ALU.subtract))
                    inc("V", vector.tensor_sub(fscr[:], v2a[:], v1a[:]))
                v_sub = cnt["V"]

                @block.scalar
                def _(scalar):
                    w(scalar, "V", v_sub)
                    inc("A", scalar.activation(expa[:], suba[:], AF.Exp))
                    inc("A", scalar.activation(v2a[:], fscr[:], AF.Exp))
                a_exp = cnt["A"]

                @block.vector
                def _(vector):
                    def flr():
                        inc("V", vector.tensor_copy(fscr[:], z16[:]))
                    w(vector, "A", a_exp)
                    inc("V", vector.tensor_reduce(suma[:], grp(expa[:]),
                                                  AX.X, ALU.add))
                    flr(); flr()
                    inc("V", vector.reciprocal(tk_sb[:, 0:NT], suma[:]))
                    flr(); flr()
                    inc("V", vector.tensor_mul(tk_sb[:, NT:2 * NT], v2a[:],
                                               tk_sb[:, 0:NT]))
                    inc("V", vector.tensor_tensor(grp(I1_sb[:]), grp(lgf32[:]),
                                                  bgrp(v1a[:]), ALU.is_equal))
                    flr(); flr()
                    for i in range(NT):
                        inc("V", vector.tensor_tensor_scan(
                            cuma[:, 16 * i:16 * i + 16], I1_sb[:, 16 * i:16 * i + 16],
                            z16[:], 0.0, ALU.add, ALU.add))
                    flr(); flr()
                    inc("V", vector.tensor_scalar(cuma[:], cuma[:], 1.0, None,
                                                  ALU.is_equal))
                    flr(); flr()
                    inc("V", vector.tensor_mul(I1_sb[:], I1_sb[:], cuma[:]))
                    flr(); flr()
                    inc("V", vector.tensor_mul(cuma[:], I1_sb[:], iotat[:]))
                    flr(); flr()
                    inc("V", vector.tensor_reduce(ee_sb[:, 0:NT], grp(cuma[:]),
                                                  AX.X, ALU.add))
                    inc("V", vector.scalar_tensor_tensor(lgf32[:], I1_sb[:], -1e9,
                                                         lgf32[:], ALU.mult, ALU.add))
                    flr(); flr()
                    inc("V", vector.tensor_tensor(grp(I2_sb[:]), grp(lgf32[:]),
                                                  bgrp(v2a_raw[:]), ALU.is_equal))
                    flr(); flr()
                    for i in range(NT):
                        inc("V", vector.tensor_tensor_scan(
                            cuma[:, 16 * i:16 * i + 16], I2_sb[:, 16 * i:16 * i + 16],
                            z16[:], 0.0, ALU.add, ALU.add))
                    flr(); flr()
                    inc("V", vector.tensor_scalar(cuma[:], cuma[:], 1.0, None,
                                                  ALU.is_equal))
                    flr(); flr()
                    inc("V", vector.tensor_mul(I2_sb[:], I2_sb[:], cuma[:]))
                    flr(); flr()
                    inc("V", vector.tensor_mul(cuma[:], I2_sb[:], iotat[:]))
                    flr(); flr()
                    inc("V", vector.tensor_reduce(ee_sb[:, NT:2 * NT], grp(cuma[:]),
                                                  AX.X, ALU.add))
                v_route = cnt["V"]

                with nc.psum_tensor(f"ps_pos{tag}", [128, 16], F32) as ps_pos:
                    for k in range(2):
                        Ik = I1_sb if k == 0 else I2_sb
                        pcX = poscumA if k == 0 else poscumB

                        @block.vector
                        def _(vector, k=k):
                            if k == 0:
                                w(vector, "V", v_route)
                                inc("V", vector.memset(Irun[:], 0.0))
                        for i in range(NT):
                            @block.tensor
                            def _(tensor, k=k, i=i, Ik=Ik):
                                w(tensor, "V", cnt["V"])
                                inc("T", tensor.matmul(ps_pos[:], strictL32[:],
                                                       Ik[:, i * 16:(i + 1) * 16],
                                                       start=True, stop=False))
                                inc("T", tensor.matmul(ps_pos[:], ones128[:],
                                                       Irun[:],
                                                       start=False, stop=True))
                            t_p = cnt["T"]

                            @block.vector
                            def _(vector, k=k, i=i, Ik=Ik, pcX=pcX, t_p=t_p):
                                w(vector, "T", t_p)
                                inc("V", vector.tensor_copy(
                                    pcX[:, i * 16:(i + 1) * 16], ps_pos[:]))
                                inc("V", vector.tensor_add(Irun[:], Irun[:],
                                                           Ik[:, i * 16:(i + 1) * 16]))

                @block.vector
                def _(vector):
                    def flr():
                        inc("V", vector.tensor_copy(fscr[:], z16[:]))
                    flr(); flr()
                    inc("V", vector.tensor_mul(poscumA[:], poscumA[:], I1_sb[:]))
                    inc("V", vector.tensor_mul(poscumB[:], poscumB[:], I2_sb[:]))
                    flr(); flr()
                    inc("V", vector.tensor_reduce(pos_sb[:, 0:NT], grp(poscumA[:]),
                                                  AX.X, ALU.add))
                    inc("V", vector.tensor_reduce(pos_sb[:, NT:2 * NT], grp(poscumB[:]),
                                                  AX.X, ALU.add))
                    flr(); flr()
                    inc("V", vector.scalar_tensor_tensor(
                        poscumA[:, 0:32], ee_sb[:], 256.0, pos_sb[:],
                        ALU.mult, ALU.add))
                    inc("V", vector.tensor_scalar(
                        poscumA[:, 32:64], pos_sb[:], 255.5, None, ALU.is_gt))
                    flr(); flr()
                    inc("V", vector.tensor_scalar(
                        poscumA[:, 64:96], poscumA[:, 0:32], -1.0, 4096.0,
                        ALU.mult, ALU.add))
                    flr(); flr()
                    inc("V", vector.tensor_mul(poscumA[:, 64:96], poscumA[:, 64:96],
                                               poscumA[:, 32:64]))
                    flr(); flr()
                    inc("V", vector.tensor_add(poscumA[:, 0:32], poscumA[:, 0:32],
                                               poscumA[:, 64:96]))
                    flr(); flr()
                    inc("V", vector.tensor_copy(sidx[:], poscumA[:, 0:32]))
                    inc("V", vector.tensor_scalar(
                        poscumB[:, 0:32], pos_sb[:], 255.0, None, ALU.min))
                    flr(); flr()
                    inc("V", vector.scalar_tensor_tensor(
                        poscumB[:, 32:64], ee_sb[:], 256.0, poscumB[:, 0:32],
                        ALU.mult, ALU.add))
                    flr(); flr()
                    inc("V", vector.tensor_copy(gidx[:], poscumB[:, 32:64]))
                v_idx = cnt["V"]

                if f"route{tag}" in debug:
                    dbg(f"route{tag}_tk", (128, 2 * NT), force=True)
                    dbg(f"route{tag}_ee", (128, 2 * NT), force=True)
                    dbg(f"route{tag}_pos", (128, 2 * NT), force=True)
                    dbg(f"route{tag}_sidx", (128, 2 * NT), I32, force=True)
                    dbg(f"route{tag}_gidx", (128, 2 * NT), I32, force=True)
                    dbg(f"route{tag}_log", (128, 16 * NT), BF16, force=True)
                    @block.sync
                    def _(sync):
                        w(sync, "V", v_idx)
                        inc("S", sync.dma_start(out=dbg_outs[f"route{tag}_tk"], in_=tk_sb[:]), 16)
                        inc("S", sync.dma_start(out=dbg_outs[f"route{tag}_ee"], in_=ee_sb[:]), 16)
                        inc("S", sync.dma_start(out=dbg_outs[f"route{tag}_pos"], in_=pos_sb[:]), 16)
                        inc("S", sync.dma_start(out=dbg_outs[f"route{tag}_sidx"], in_=sidx[:]), 16)
                        inc("S", sync.dma_start(out=dbg_outs[f"route{tag}_gidx"], in_=gidx[:]), 16)
                        inc("S", sync.dma_start(out=dbg_outs[f"route{tag}_log"], in_=logits_tm[:]), 16)

                @block.gpsimd
                def _(gpsimd):
                    w(gpsimd, "V", v_idx)
                    w(gpsimd, "G", xtm_zeroed)
                    for k in range(2):
                        for i in range(NT):
                            c = k * NT + i
                            inc("G", gpsimd.indirect_dma_start(
                                out=Xtm_d,
                                out_offset=IndirectOffsetOnAxis(sidx[:, c:c + 1], 0),
                                in_=xn2[:, i * D:(i + 1) * D], in_offset=None), 16)
                g_scat = cnt["G"]
                eph_stack.close()

                with ExitStack() as ectx:
                    EE = ectx.enter_context
                    XT = [EE(nc.sbuf_tensor(f"XT{tag}{fd}", [128, E * CAP], BF16))
                          for fd in range(ND)]
                    Wg_sb = [EE(nc.sbuf_tensor(f"Wg{tag}{b}", [128, 8 * 512], BF16)) for b in range(2)]
                    Wf_sb = [EE(nc.sbuf_tensor(f"Wf{tag}{b}", [128, 8 * 512], BF16)) for b in range(2)]
                    Wp_sb = [EE(nc.sbuf_tensor(f"Wp{tag}{b}", [128, 4 * 1024], BF16)) for b in range(2)]
                    sg_sb = EE(nc.sbuf_tensor(f"sg{tag}", [128, 1024], BF16))
                    hh_sb = EE(nc.sbuf_tensor(f"hh{tag}", [128, 1024], BF16))
                    o_sb = [EE(nc.sbuf_tensor(f"o{tag}{b}", [128, 1024], BF16)) for b in range(2)]
                    ps_g = EE(nc.psum_tensor(f"ps_g{tag}", [128, 1024], F32))
                    ps_f = EE(nc.psum_tensor(f"ps_f{tag}", [128, 1024], F32))
                    ps_o2 = [EE(nc.psum_tensor(f"ps_o{tag}{b}", [128, 1024], F32)) for b in range(2)]

                    @block.sync
                    def _(sync):
                        w(sync, "G", g_scat)
                        for fd in range(ND):
                            inc("L", sync.dma_start(
                                out=XT[fd][:],
                                in_=Xtm_d[0:E * CAP, fd * 128:(fd + 1) * 128],
                                transpose=True), 16)
                    l_XT = cnt["L"]

                    a_prev_sil = [0]
                    v_prev_hh = [0]
                    t_prev_o = [0]
                    t_prev_o_guard = [0]
                    t_gf_mark = {}
                    g_we_marks = {}
                    s_o_marks = {}
                    base_TAV = (cnt["T"], cnt["A"], cnt["V"])
                    for e in range(E + 1):
                        if e < E:
                            @block.gpsimd
                            def _(gpsimd, e=e):
                                eb = e % 2
                                if e < 2:
                                    w(gpsimd, "T", base_TAV[0])
                                    w(gpsimd, "A", base_TAV[1])
                                    w(gpsimd, "V", base_TAV[2])
                                w(gpsimd, "T", t_gf_mark.get(e - 2, 0))
                                w(gpsimd, "T", t_prev_o[0] if e >= 2 else 0)
                                inc("G", gpsimd.dma_start(
                                    out=Wg_sb[e % 2][:], in_=w_gate[e]), 16)
                                inc("G", gpsimd.dma_start(
                                    out=Wf_sb[e % 2][:], in_=w_fc[e]), 16)
                                inc("G", gpsimd.dma_start(
                                    out=Wp_sb[e % 2][:], in_=w_proj[e]), 16)
                            g_we_marks[e] = cnt["G"]
                        if e == 0:
                            continue
                        e = e - 1
                        g_we = g_we_marks[e]

                        T_b = cnt["T"]
                        A_b = cnt["A"]
                        V_b = cnt["V"]
                        t_gf_c = {0: T_b + 32, 1: T_b + 64}

                        @block.tensor
                        def _(tensor, e=e, g_we=g_we, A_b=A_b, V_b=V_b):
                            def gf(C):
                                for ps_x, W_sb in ((ps_g, Wg_sb), (ps_f, Wf_sb)):
                                    for hc in (2 * C, 2 * C + 1):
                                        for c in range(ND):
                                            inc("T", tensor.matmul(
                                                ps_x[:, hc * 256:(hc + 1) * 256],
                                                W_sb[e % 2][:, c * 512 + hc * 128:
                                                     c * 512 + (hc + 1) * 128],
                                                XT[c][:, e * CAP:(e + 1) * CAP],
                                                start=(c == 0),
                                                stop=(c == ND - 1)))

                            def pj(C):
                                tensor.wait_ge(sem["V"], V_b + C + 1)
                                for hc in (2 * C, 2 * C + 1):
                                    for ss in range(2):
                                        for half in range(2):
                                            inc("T", tensor.matmul(
                                                ps_o2[ss][:, half * 512:(half + 1) * 512],
                                                hh_sb[:, hc * 256 + ss * 128:
                                                      hc * 256 + ss * 128 + 128],
                                                Wp_sb[e % 2][:, hc * 1024 + half * 512:
                                                      hc * 1024 + (half + 1) * 512],
                                                start=(hc == 0), stop=(hc == 3)))

                            w(tensor, "G", g_we)
                            if e == 0:
                                w(tensor, "L", l_XT)
                            w(tensor, "A", a_prev_sil[0])
                            w(tensor, "V", v_prev_hh[0])
                            gf(0); gf(1); pj(0); pj(1)
                        t_o_end = cnt["T"]
                        t_gf_mark[e] = t_o_end
                        t_prev_o[0] = t_o_end

                        @block.scalar
                        def _(scalar, e=e, t_gf_c=t_gf_c, t_o_end=t_o_end,
                              som0=s_o_marks.get((e - 1, 0), 0),
                              som1=s_o_marks.get((e - 1, 1), 0)):
                            for C in range(2):
                                scalar.wait_ge(sem["T"], t_gf_c[C])
                                inc("A", scalar.activation(
                                    sg_sb[:, C * 512:(C + 1) * 512],
                                    ps_g[:, C * 512:(C + 1) * 512], AF.Silu))
                            for ss in range(2):
                                scalar.wait_ge(sem["T"], t_o_end)
                                scalar.wait_ge(sem["S"], (som0, som1)[ss])
                                inc("A", scalar.activation(o_sb[ss][:], ps_o2[ss][:],
                                                           AF.Copy))
                        a_sil = cnt["A"] - 2
                        a_prev_sil[0] = a_sil
                        a_oev = cnt["A"]

                        @block.vector
                        def _(vector, e=e, A_b=A_b, tpo=t_prev_o_guard[0]):
                            w(vector, "T", tpo)
                            for C in range(2):
                                vector.wait_ge(sem["A"], A_b + C + 1)
                                inc("V", vector.tensor_mul(
                                    hh_sb[:, C * 512:(C + 1) * 512],
                                    sg_sb[:, C * 512:(C + 1) * 512],
                                    ps_f[:, C * 512:(C + 1) * 512]))
                        v_hh = cnt["V"]
                        v_prev_hh[0] = v_hh
                        t_prev_o_guard[0] = t_o_end

                        @block.sync
                        def _(sync, e=e, a_oev=a_oev):
                            for ss in range(2):
                                sync.wait_ge(sem["A"], a_oev - (1 - ss))
                                inc("S", sync.dma_start(
                                    out=o_d[e * CAP + ss * 128:e * CAP + ss * 128 + 128, :],
                                    in_=o_sb[ss][:]), 16)
                                s_o_marks[(e, ss)] = cnt["S"]
                    s_o = cnt["S"]

                A0 = FE(nc.sbuf_tensor(f"A0{tag}", [128, D], BF16))
                A1 = FE(nc.sbuf_tensor(f"A1{tag}", [128, D], BF16))
                lastcomb = [0]
                for i in range(NT):
                    @block.gpsimd
                    def _(gpsimd, i=i):
                        if i == 0:
                            w(gpsimd, "S", s_o)
                        w(gpsimd, "V", lastcomb[0])
                        inc("G", gpsimd.indirect_dma_start(
                            out=A0[:], out_offset=None, in_=o_d,
                            in_offset=IndirectOffsetOnAxis(gidx[:, i:i + 1], 0)), 16)
                        inc("G", gpsimd.indirect_dma_start(
                            out=A1[:], out_offset=None, in_=o_d,
                            in_offset=IndirectOffsetOnAxis(gidx[:, NT + i:NT + i + 1], 0)), 16)
                    g_ga = cnt["G"]

                    @block.vector
                    def _(vector, i=i, g_ga=g_ga):
                        w(vector, "G", g_ga)
                        inc("V", vector.scalar_tensor_tensor(
                            x_sb[:, i * D:(i + 1) * D], A0[:], tk_sb[:, i:i + 1],
                            x_sb[:, i * D:(i + 1) * D], ALU.mult, ALU.add))
                        inc("V", vector.scalar_tensor_tensor(
                            x_sb[:, i * D:(i + 1) * D], A1[:], tk_sb[:, NT + i:NT + i + 1],
                            x_sb[:, i * D:(i + 1) * D], ALU.mult, ALU.add))
                    lastcomb[0] = cnt["V"]
                moe_v = cnt["V"]

                @block.sync
                def _(sync):
                    w(sync, "V", moe_v)
                    w(sync, "T", cnt["T"])
                    w(sync, "A", cnt["A"])
                    w(sync, "G", cnt["G"])
            return moe_v


        def emit_attn(ready_v):
            COFF = CST["lnW_COFF"]
            with ExitStack() as actx:
                AE = actx.enter_context
                qstg = [AE(nc.sbuf_tensor(f"qstg{b}", [128, QT], BF16)) for b in range(2)]
                KT_sb = [AE(nc.sbuf_tensor(f"KTs{fd}", [128, T], BF16)) for fd in range(ND)]
                Vaug = AE(nc.sbuf_tensor("Vaug", [128, NT * NH * 65], BF16))
                negc = AE(nc.sbuf_tensor("negc", [128, 16], F32))
                nsc = AE(nc.sbuf_tensor("nsc", [128, 32], F32))
                cvec = AE(nc.sbuf_tensor("cvec", [128, D], F32))
                bvb16 = AE(nc.sbuf_tensor("bvb16", [128, 8], BF16))

                with nc.sbuf_tensor("xn3st", [128, 2 * D], BF16) as xn3st:
                    s_xn3 = emit_layernorm_dram(xn_d, xn3st, ready=("V", ready_v))

                @block.vector
                def _(vector):
                    w(vector, "V", ready_v)
                    inc("V", vector.tensor_scalar(nsc[:, 0:8], biases[:, 0:8],
                                                  0.125, None, ALU.mult))

                ps_n_ctx = nc.psum_tensor("ps_n", [16, T], F32)
                ps_n = ps_n_ctx.__enter__()
                with ExitStack() as pctx:
                    PE_ = pctx.enter_context
                    xnTs = [PE_(nc.sbuf_tensor(f"axnT{fd}", [128, QT], BF16))
                            for fd in range(ND)]
                    sq512 = PE_(nc.sbuf_tensor("sq512", [128, QT], BF16))
                    W_cur = PE_(nc.sbuf_tensor("W_cur", [128, ND * D], BF16))
                    ps_pj = PE_(nc.psum_tensor("ps_pj", [128, 1024], F32))

                    qst_marks = {}
                    qst_last = {}
                    for pj, Wsrc in enumerate([Wq, Wk, Wv]):
                        @block.gpsimd
                        def _(gpsimd, pj=pj, Wsrc=Wsrc):
                            w(gpsimd, "T", cnt["T"])
                            inc("G", gpsimd.dma_start(
                                out=W_cur[:], in_=Wsrc), 16)
                        g_w = cnt["G"]

                        for ts in range(4):
                            @block.sync
                            def _(sync, pj=pj, ts=ts):
                                if pj == 0 and ts == 0:
                                    w(sync, "S", s_xn3)
                                w(sync, "T", cnt["T"])
                                for fd in range(ND):
                                    inc("L", sync.dma_start(
                                        out=xnTs[fd][:],
                                        in_=xn_d[ts * QT:(ts + 1) * QT,
                                                 fd * 128:(fd + 1) * 128],
                                        transpose=True), 16)
                            l_xt = cnt["L"]

                            if pj < 2:
                                for od in range(ND):
                                    @block.tensor
                                    def _(tensor, pj=pj, ts=ts, od=od, l_xt=l_xt, g_w=g_w):
                                        w(tensor, "L", l_xt)
                                        w(tensor, "G", g_w)
                                        w(tensor, "A", cnt["A"])
                                        for fd in range(ND):
                                            inc("T", tensor.matmul(
                                                ps_pj[:, 0:QT],
                                                W_cur[:, fd * D + od * 128:
                                                      fd * D + od * 128 + 128],
                                                xnTs[fd][:],
                                                start=(fd == 0), stop=(fd == ND - 1)))
                                    t_pj = cnt["T"]

                                    @block.scalar
                                    def _(scalar, pj=pj, ts=ts, od=od, t_pj=t_pj):
                                        w(scalar, "T", t_pj)
                                        if pj == 0:
                                            w(scalar, "S", qst_last.get(od % 2, 0))
                                            inc("A", scalar.activation(
                                                qstg[od % 2][:], ps_pj[:, 0:QT],
                                                AF.Identity, bias=nsc[:, od:od + 1],
                                                scale=0.125))
                                        else:
                                            inc("A", scalar.activation(
                                                KT_sb[od][:, ts * QT:(ts + 1) * QT],
                                                ps_pj[:, 0:QT], AF.Identity,
                                                bias=biases[:, 8 + od:9 + od]))
                                    a_qe = cnt["A"]

                                    if pj == 0:
                                        @block.sync
                                        def _(sync, ts=ts, od=od, a_qe=a_qe):
                                            w(sync, "A", a_qe)
                                            inc("S", sync.dma_start(
                                                out=qT_d[od * 128:(od + 1) * 128,
                                                         ts * QT:(ts + 1) * QT],
                                                in_=qstg[od % 2][:]), 16)
                                        qst_last[od % 2] = cnt["S"]

                                        @block.scalar
                                        def _(scalar, ts=ts, od=od):
                                            w(scalar, "T", cnt["T"])
                                            inc("A", scalar.activation(
                                                sq512[:], qstg[od % 2][:], AF.Square))
                                        a_sq = cnt["A"]

                                        @block.tensor
                                        def _(tensor, ts=ts, od=od, a_sq=a_sq):
                                            w(tensor, "A", a_sq)
                                            inc("T", tensor.matmul(
                                                ps_n[:, ts * QT:(ts + 1) * QT],
                                                hmaps[:, od * 16:(od + 1) * 16],
                                                sq512[:],
                                                start=(od == 0), stop=(od == ND - 1)))
                            else:
                                for tsub in range(4):
                                    @block.tensor
                                    def _(tensor, ts=ts, tsub=tsub, l_xt=l_xt, g_w=g_w):
                                        w(tensor, "L", l_xt)
                                        w(tensor, "G", g_w)
                                        w(tensor, "A", cnt["A"])
                                        for fd in range(ND):
                                            for half in range(2):
                                                inc("T", tensor.matmul(
                                                    ps_pj[:, half * 512:(half + 1) * 512],
                                                    xnTs[fd][:, tsub * 128:(tsub + 1) * 128],
                                                    W_cur[:, fd * D + half * 512:
                                                          fd * D + (half + 1) * 512],
                                                    start=(fd == 0), stop=(fd == ND - 1)))
                                    t_v = cnt["T"]

                                    @block.scalar
                                    def _(scalar, ts=ts, tsub=tsub, t_v=t_v):
                                        w(scalar, "T", t_v)
                                        kt = ts * 4 + tsub
                                        dst = Vaug[:, kt * NH * 65:(kt + 1) * NH * 65]
                                        inc("A", scalar.activation(
                                            dst.rearrange("p (h c) -> p h c", h=NH)[:, :, 0:64],
                                            ps_pj[:].rearrange("p (h c) -> p h c", c=64),
                                            AF.Identity,
                                            bias=biases[:, 16 + (0):17]
                                            if False else 0.0))
                    t_proj = cnt["T"]
                    a_proj = cnt["A"]

                    @block.vector
                    def _(vector):
                        w(vector, "A", a_proj)
                        inc("V", vector.memset(
                            Vaug[:].rearrange("p (k h c) -> p k h c", k=NT, h=NH)
                            [:, :, :, 64:65], 1.0))
                    v_vones = cnt["V"]

                    with nc.psum_tensor("ps_bv", [1, 1024], F32) as ps_bv:
                        @block.gpsimd
                        def _(gpsimd):
                            w(gpsimd, "T", t_proj)
                            inc("G", gpsimd.dma_start(
                                out=W_cur[:], in_=Wo), 16)
                        g_wo = cnt["G"]

                        @block.vector
                        def _(vector):
                            inc("V", vector.tensor_copy(bvb16[:], biases[:, 16:24]))
                        v_bvb = cnt["V"]

                        @block.tensor
                        def _(tensor, g_wo=g_wo, v_bvb=v_bvb):
                            w(tensor, "G", g_wo)
                            w(tensor, "V", v_bvb)
                            for half in range(2):
                                for fd in range(ND):
                                    inc("T", tensor.matmul(
                                        ps_bv[:, half * 512:(half + 1) * 512],
                                        bvb16[:, fd:fd + 1],
                                        W_cur[:, fd * D + half * 512:
                                              fd * D + (half + 1) * 512],
                                        start=(fd == 0), stop=(fd == ND - 1)))
                        t_bv = cnt["T"]

                        @block.scalar
                        def _(scalar, t_bv=t_bv):
                            w(scalar, "T", t_bv)
                            inc("A", scalar.activation(cvec[0:1, :], ps_bv[:], AF.Copy))
                        a_bv = cnt["A"]

                    @block.gpsimd
                    def _(gpsimd):
                        inc("G", gpsimd.dma_start(
                            out=sq_scr[0:1, :],
                            in_=bo.rearrange("a b -> b a")), 16)
                    g_bo = cnt["G"]

                    @block.vector
                    def _(vector, a_bv=a_bv, g_bo=g_bo):
                        w(vector, "A", a_bv)
                        w(vector, "G", g_bo)
                        inc("V", vector.tensor_add(cvec[0:1, :], cvec[0:1, :],
                                                   sq_scr[0:1, :]))
                    v_cv0 = cnt["V"]

                    @block.sync
                    def _(sync, v_cv0=v_cv0):
                        w(sync, "V", v_cv0)
                        inc("S", sync.dma_start(out=brow_d[0:1, :], in_=cvec[0:1, :]), 16)
                    s_cv = cnt["S"]

                    @block.gpsimd
                    def _(gpsimd, s_cv=s_cv):
                        w(gpsimd, "S", s_cv)
                        inc("G", gpsimd.dma_start(
                            out=cvec[:], in_=brow_d.partition_broadcast(128)), 16)
                    g_cvec = cnt["G"]

                with nc.sbuf_tensor("sqs", [128, T], BF16) as sqs, \
                     nc.sbuf_tensor("nrm", [16, T], F32) as nrm:
                    @block.vector
                    def _(vector):
                        w(vector, "T", cnt["T"])
                        inc("V", vector.tensor_copy(nrm[:], ps_n[:]))
                        inc("V", vector.reduce_max(nsc[0:16, 16:17], nrm[:], axis=AX.X))
                    v_qn = cnt["V"]

                    for fd in range(ND):
                        @block.scalar
                        def _(scalar, fd=fd, v_qn=v_qn):
                            if fd == 0:
                                w(scalar, "V", v_qn)
                            w(scalar, "T", cnt["T"])
                            inc("A", scalar.activation(sqs[:], KT_sb[fd][:], AF.Square))
                        a_sq2 = cnt["A"]

                        @block.tensor
                        def _(tensor, fd=fd, a_sq2=a_sq2):
                            w(tensor, "A", a_sq2)
                            for half in range(4):
                                inc("T", tensor.matmul(
                                    ps_n[:, half * 512:(half + 1) * 512],
                                    hmaps[:, fd * 16:(fd + 1) * 16],
                                    sqs[:, half * 512:(half + 1) * 512],
                                    start=(fd == 0), stop=(fd == ND - 1)))
                    t_kn = cnt["T"]

                    @block.vector
                    def _(vector, t_kn=t_kn):
                        w(vector, "T", t_kn)
                        inc("V", vector.tensor_copy(nrm[:], ps_n[:]))
                        inc("V", vector.reduce_max(nsc[0:16, 17:18], nrm[:], axis=AX.X))
                    v_nrm = cnt["V"]

                    @block.scalar
                    def _(scalar, v_nrm=v_nrm):
                        w(scalar, "V", v_nrm)
                        inc("A", scalar.activation(nsc[0:16, 18:19],
                                                   nsc[0:16, 16:17], AF.Ln))
                        inc("A", scalar.activation(nsc[0:16, 19:20],
                                                   nsc[0:16, 17:18], AF.Ln))
                    a_ln2 = cnt["A"]

                    @block.vector
                    def _(vector, a_ln2=a_ln2):
                        w(vector, "A", a_ln2)
                        inc("V", vector.tensor_add(nsc[0:16, 20:21],
                                                   nsc[0:16, 18:19],
                                                   nsc[0:16, 19:20]))
                    v_ln2 = cnt["V"]

                    @block.scalar
                    def _(scalar, v_ln2=v_ln2):
                        w(scalar, "V", v_ln2)
                        inc("A", scalar.activation(nsc[0:16, 21:22],
                                                   nsc[0:16, 20:21], AF.Exp,
                                                   scale=kvec[0:16, 5:6]))
                    a_c = cnt["A"]

                    @block.vector
                    def _(vector, a_c=a_c):
                        w(vector, "A", a_c)
                        inc("V", vector.tensor_scalar(nsc[0:16, 22:23],
                                                      nsc[0:16, 21:22],
                                                      -0.125, -1.0,
                                                      ALU.mult, ALU.add))
                    v_c = cnt["V"]

                    @block.sync
                    def _(sync, v_c=v_c):
                        w(sync, "V", v_c)
                        inc("S", sync.dma_start(
                            out=brow2_d[0:1, 32:48].rearrange("a b -> b a"),
                            in_=nsc[0:16, 22:23]), 16)
                    s_nc = cnt["S"]

                    @block.gpsimd
                    def _(gpsimd, s_nc=s_nc):
                        w(gpsimd, "S", s_nc)
                        inc("G", gpsimd.dma_start(
                            out=negc[:],
                            in_=brow2_d[0:1, 32:48].partition_broadcast(128)), 16)
                    g_negc_l = cnt["G"]

                    @block.vector
                    def _(vector, g_negc_l=g_negc_l):
                        w(vector, "G", g_negc_l)
                        inc("V", vector.tensor_copy(nsc[:, 24:25], negc[:, 0:1]))
                        inc("V", vector.tensor_copy(nsc[:, 24:25], negc[:, 15:16]))
                    a_negc_v = cnt["V"]

                    @block.gpsimd
                    def _(gpsimd, a_negc_v=a_negc_v):
                        w(gpsimd, "V", a_negc_v)
                        inc("G", gpsimd.dma_start(out=nsc[:, 25:26],
                                                  in_=brow2_d[0:1, 32:33]
                                                  .partition_broadcast(128)), 16)
                    a_negc = cnt["G"]
                ps_n_ctx.__exit__(None, None, None)

                with ExitStack() as mctx2:
                    AE2 = mctx2.enter_context
                    outT = [AE2(nc.sbuf_tensor(f"outT{fd}", [128, QT], BF16))
                            for fd in range(ND)]
                    qsl = [AE2(nc.sbuf_tensor(f"qsl{fd}", [128, QT], BF16))
                           for fd in range(ND)]
                    expS = [AE2(nc.sbuf_tensor(f"expS{b}", [128, 1024], BF16))
                            for b in range(2)]
                    sums2 = AE2(nc.sbuf_tensor("sums2", [1, 1024], F32))
                    dumA = AE2(nc.sbuf_tensor("dumA", [1, 64], F32))
                    bcS = AE2(nc.sbuf_tensor("bcS", [64, 1024], F32))
                    Wo_sb = AE2(nc.sbuf_tensor("Wo_sb", [128, ND * D], BF16))

                    @block.gpsimd
                    def _(gpsimd):
                        w(gpsimd, "T", cnt["T"])
                        inc("G", gpsimd.dma_start(
                            out=Wo_sb[:], in_=Wo), 16)
                    g_wo2 = cnt["G"]

                    t_qk_last = [0]
                    a_exp_mark = {}
                    v_dec_mark = {}
                    t_pv_mark = {}
                    v_norm_mark = [0]
                    g_bc_mark = [0]
                    v_res_mark = [0]
                    t_wo_mark = [0]

                    for qs in range(4):
                        nkt = (qs + 1) * 4

                        @block.gpsimd
                        def _(gpsimd, qs=qs):
                            if qs == 0:
                                w(gpsimd, "S", cnt["S"])
                            else:
                                w(gpsimd, "T", t_qk_last[0])
                            for fd in range(ND):
                                inc("G", gpsimd.dma_start(
                                    out=qsl[fd][:],
                                    in_=qT_d[fd * 128:(fd + 1) * 128,
                                             qs * QT:(qs + 1) * QT]), 16)
                        g_q = cnt["G"]

                        with nc.psum_tensor(f"ps_sA{qs}", [128, 1024], F32) as psA, \
                             nc.psum_tensor(f"ps_sB{qs}", [128, 1024], F32) as psB, \
                             nc.psum_tensor(f"ps_o0{qs}", [65, QT], F32) as ps_o0, \
                             nc.psum_tensor(f"ps_o1{qs}", [65, QT], F32) as ps_o1, \
                             nc.psum_tensor(f"ps_bc{qs}", [64, 1024], F32) as ps_bc:
                            ps_s = [psA, psB]
                            ps_o = [ps_o0, ps_o1]
                            for fd in range(ND):
                                T0, A0, V0 = cnt["T"], cnt["A"], cnt["V"]
                                n_kt = nkt
                                t_qk_c, t_pv_c = {}, {}
                                tc = T0
                                order = [("qk", 0), ("qk", 1)] if n_kt >= 2 else [("qk", 0)]
                                for kt in range(2, n_kt):
                                    order += [("pv", kt - 2), ("qk", kt)]
                                order += [("pv", n_kt - 2), ("pv", n_kt - 1)] if n_kt >= 2 \
                                    else [("pv", 0)]
                                for kind, kt in order:
                                    tc += 2
                                    (t_qk_c if kind == "qk" else t_pv_c)[kt] = tc
                                t_qk_last[0] = t_qk_c[n_kt - 1]
                                a_exp_c = {kt: A0 + 2 * (kt + 1) for kt in range(n_kt)}
                                v_dec_c = {kt: V0 + kt + 1 for kt in range(n_kt)}
                                carryA = [a_exp_mark.get(0, 0), a_exp_mark.get(1, 0)]
                                carryT = [t_pv_mark.get(0, 0), t_pv_mark.get(1, 0)]

                                @block.tensor
                                def _(tensor, qs=qs, fd=fd, n_kt=n_kt, order=order,
                                      a_exp_c=a_exp_c, v_dec_c=v_dec_c,
                                      carryA=carryA, vnp=v_norm_mark[0],
                                      vres0=v_res_mark[0]):
                                    if (qs, fd) == (0, 0):
                                        w(tensor, "G", a_negc)
                                        w(tensor, "V", v_vones)
                                        w(tensor, "G", g_q)
                                    if fd == 0 and qs > 0:
                                        w(tensor, "G", g_q)
                                        w(tensor, "V", vres0)
                                    for kind, kt in order:
                                        b = kt % 2
                                        if kind == "qk":
                                            tensor.wait_ge(sem["A"],
                                                           a_exp_c[kt - 2] if kt >= 2
                                                           else carryA[b])
                                            for sub in range(2):
                                                inc("T", tensor.matmul(
                                                    ps_s[b][:, sub * 512:(sub + 1) * 512],
                                                    KT_sb[fd][sub * 64:sub * 64 + 64,
                                                              kt * 128:(kt + 1) * 128],
                                                    qsl[fd][sub * 64:sub * 64 + 64, :],
                                                    start=True, stop=True,
                                                    tile_position=(sub * 64, 0)))
                                        else:
                                            tensor.wait_ge(sem["V"], v_dec_c[kt])
                                            if kt <= 1:
                                                tensor.wait_ge(sem["V"], vnp)
                                            for sub in range(2):
                                                h = fd * 2 + sub
                                                inc("T", tensor.matmul(
                                                    ps_o[sub][:],
                                                    Vaug[:, kt * NH * 65 + h * 65:
                                                         kt * NH * 65 + (h + 1) * 65],
                                                    expS[b][:, sub * 512:(sub + 1) * 512],
                                                    start=(kt == 0),
                                                    stop=(kt == n_kt - 1)))

                                @block.scalar
                                def _(scalar, qs=qs, fd=fd, n_kt=n_kt,
                                      t_qk_c=t_qk_c, t_pv_c=t_pv_c, carryT=carryT):
                                    for kt in range(n_kt):
                                        b = kt % 2
                                        scalar.wait_ge(sem["T"], t_qk_c[kt])
                                        scalar.wait_ge(sem["T"],
                                                       t_pv_c.get(kt - 2, carryT[b]))
                                        for sub in range(2):
                                            h = fd * 2 + sub
                                            inc("A", scalar.activation(
                                                expS[b][:, sub * 512:(sub + 1) * 512],
                                                ps_s[b][:, sub * 512:(sub + 1) * 512],
                                                AF.Exp, bias=negc[:, h:h + 1]))

                                @block.vector
                                def _(vector, qs=qs, fd=fd, n_kt=n_kt,
                                      a_exp_c=a_exp_c, t_pv_c=t_pv_c, carryT=carryT):
                                    for kt in range(n_kt):
                                        b = kt % 2
                                        vector.wait_ge(sem["A"], a_exp_c[kt])
                                        vector.wait_ge(sem["T"],
                                                       t_pv_c.get(kt - 2, carryT[b]))
                                        off = qs * QT - kt * 128 + QT
                                        inc("V", vector.tensor_tensor(
                                            expS[b][:].rearrange("p (s q) -> p s q", s=2),
                                            expS[b][:].rearrange("p (s q) -> p s q", s=2),
                                            Wdec[:, off:off + 512]
                                            .unsqueeze(1).broadcast_to([128, 2, 512]),
                                            ALU.mult))
                                assert cnt["T"] == tc and cnt["A"] == a_exp_c[n_kt - 1] \
                                    and cnt["V"] == v_dec_c[n_kt - 1]
                                a_exp_mark[0] = a_exp_c[n_kt - 1 if (n_kt - 1) % 2 == 0
                                                        else n_kt - 2]
                                a_exp_mark[1] = a_exp_c[n_kt - 1 if (n_kt - 1) % 2 == 1
                                                        else n_kt - 2]
                                t_pv_mark[0] = t_pv_c[n_kt - 1 if (n_kt - 1) % 2 == 0
                                                      else n_kt - 2]
                                t_pv_mark[1] = t_pv_c[n_kt - 1 if (n_kt - 1) % 2 == 1
                                                      else n_kt - 2]
                                t_pv = cnt["T"]

                                @block.vector
                                def _(vector, qs=qs, fd=fd, t_pv=t_pv):
                                    w(vector, "T", t_pv)
                                    inc("V", vector.reciprocal(sums2[0:1, 0:QT],
                                                               ps_o[0][64:65, :]))
                                    inc("V", vector.reciprocal(sums2[0:1, QT:2 * QT],
                                                               ps_o[1][64:65, :]))
                                v_rs = cnt["V"]

                                @block.tensor
                                def _(tensor, qs=qs, fd=fd, v_rs=v_rs,
                                      vnp=v_norm_mark[0]):
                                    w(tensor, "V", v_rs)
                                    tensor.wait_ge(sem["V"], vnp)
                                    for hf in range(2):
                                        inc("T", tensor.matmul(
                                            ps_bc[:, hf * 512:(hf + 1) * 512],
                                            ones128[0:1, 0:64],
                                            sums2[0:1, hf * 512:(hf + 1) * 512],
                                            start=True, stop=True,
                                            skip_group_check=True))
                                t_bc = cnt["T"]

                                @block.vector
                                def _(vector, qs=qs, fd=fd, t_bc=t_bc):
                                    w(vector, "T", t_bc)
                                    inc("V", vector.tensor_copy(bcS[:], ps_bc[:]))
                                    inc("V", vector.tensor_mul(
                                        outT[fd][0:64, :], ps_o[0][0:64, :],
                                        bcS[:, 0:512]))
                                    inc("V", vector.tensor_mul(
                                        outT[fd][64:128, :], ps_o[1][0:64, :],
                                        bcS[:, 512:1024]))
                                v_norm_mark[0] = cnt["V"]
                            v_heads = cnt["V"]

                        with nc.psum_tensor(f"ps_y2{qs}", [128, 1024], F32) as ps_y:
                            for tsub in range(4):
                                @block.tensor
                                def _(tensor, qs=qs, tsub=tsub, v_heads=v_heads):
                                    w(tensor, "V", v_heads)
                                    if qs == 0 and tsub == 0:
                                        w(tensor, "G", g_wo2)
                                    w(tensor, "V", v_res_mark[0])
                                    for fd in range(ND):
                                        for half in range(2):
                                            inc("T", tensor.matmul(
                                                ps_y[:, half * 512:(half + 1) * 512],
                                                outT[fd][:, tsub * 128:(tsub + 1) * 128],
                                                Wo_sb[:, fd * D + half * 512:
                                                      fd * D + (half + 1) * 512],
                                                start=(fd == 0), stop=(fd == ND - 1)))
                                t_y2 = cnt["T"]
                                t_wo_mark[0] = t_y2

                                @block.vector
                                def _(vector, qs=qs, tsub=tsub, t_y2=t_y2):
                                    w(vector, "T", t_y2)
                                    i = qs * 4 + tsub
                                    inc("V", vector.tensor_add(
                                        x_sb[:, i * D:(i + 1) * D],
                                        x_sb[:, i * D:(i + 1) * D], ps_y[:]))
                                    inc("V", vector.tensor_add(
                                        x_sb[:, i * D:(i + 1) * D],
                                        x_sb[:, i * D:(i + 1) * D], cvec[:]))
                                v_res_mark[0] = cnt["V"]

                    attn_v = cnt["V"]
                    @block.sync
                    def _(sync):
                        w(sync, "V", attn_v)
                        w(sync, "T", cnt["T"])
                        w(sync, "A", cnt["A"])
                        w(sync, "G", cnt["G"])
            return attn_v

        phase_barrier()
        ffn1_v = emit_moe("f1", mamba_v)

        if "x_after_ffn1" in debug:
            dbg("x_after_ffn1", (T, D))
            @block.sync
            def _(sync):
                w(sync, "V", ffn1_v)
                for i in range(NT):
                    inc("S", sync.dma_start(
                        out=dbg_outs["x_after_ffn1"][i * 128:(i + 1) * 128, :],
                        in_=x_sb[:, i * D:(i + 1) * D]), 16)

        phase_barrier()
        attn_v = emit_attn(ffn1_v)

        if "x_after_attn" in debug:
            dbg("x_after_attn", (T, D))
            @block.sync
            def _(sync):
                w(sync, "V", attn_v)
                for i in range(NT):
                    inc("S", sync.dma_start(
                        out=dbg_outs["x_after_attn"][i * 128:(i + 1) * 128, :],
                        in_=x_sb[:, i * D:(i + 1) * D]), 16)

        phase_barrier()
        ffn2_v = emit_moe("f2", attn_v)

        @block.sync
        def _(sync):
            w(sync, "V", cnt["V"])
            for i in range(NT):
                inc("S", sync.dma_start(out=y_out[i * 128:(i + 1) * 128, :],
                                        in_=x_sb[:, i * D:(i + 1) * D]), 16)
            w(sync, "S", cnt["S"])

    return nc, dbg_outs


def _wtile(w):
    M = w.shape[1]
    return np.ascontiguousarray(
        w.reshape(ND, 128, M).transpose(1, 0, 2).reshape(128, ND * M))


def _prep_inputs(inputs, core):
    A = np.ascontiguousarray(inputs["L0_attn_a_A"].reshape(D, NST))
    A_cols = np.zeros((128, 128), dtype=np.float32)
    for db in range(ND):
        A_cols[:, db * 16:(db + 1) * 16] = A[db * 128:(db + 1) * 128, :]
    m = {
        "x": np.ascontiguousarray(inputs["L0_attn_a_x"][core]),
        "W_in": inputs["L0_attn_a_W_in"],
        "b_in": np.ascontiguousarray(inputs["L0_attn_a_b_in"].reshape(32, 1)),
        "W_dt1": inputs["L0_attn_a_W_dt1"],
        "b_dt1": np.ascontiguousarray(inputs["L0_attn_a_b_dt1"].reshape(64, 1)),
        "W_dt2": inputs["L0_attn_a_W_dt2"],
        "b_dt2": np.ascontiguousarray(inputs["L0_attn_a_b_dt2"].reshape(D, 1)),
        "A_cols": A_cols,
        "D_row": np.ascontiguousarray(inputs["L0_attn_a_D"].reshape(1, D)),
        "Wq": _wtile(inputs["L0_attn_b_Wq"]), "bq": np.ascontiguousarray(inputs["L0_attn_b_bq"].reshape(D, 1)),
        "Wk": _wtile(inputs["L0_attn_b_Wk"]), "bk": np.ascontiguousarray(inputs["L0_attn_b_bk"].reshape(D, 1)),
        "Wv": _wtile(inputs["L0_attn_b_Wv"]), "bv": np.ascontiguousarray(inputs["L0_attn_b_bv"].reshape(D, 1)),
        "Wo": _wtile(inputs["L0_attn_b_Wo"]), "bo": np.ascontiguousarray(inputs["L0_attn_b_bo"].reshape(D, 1)),
        "w_router": inputs["L0_ffn_w_router"],
        "w_fc": np.ascontiguousarray(
            inputs["L0_ffn_w_fc"].reshape(E, ND, 128, HID)
            .transpose(0, 2, 1, 3).reshape(E, 128, ND * HID)),
        "w_gate": np.ascontiguousarray(
            inputs["L0_ffn_w_gate"].reshape(E, ND, 128, HID)
            .transpose(0, 2, 1, 3).reshape(E, 128, ND * HID)),
        "w_proj": np.ascontiguousarray(
            inputs["L0_ffn_w_proj"].reshape(E, 4, 128, D)
            .transpose(0, 2, 1, 3).reshape(E, 128, 4 * D)),
        "ident": CST["ident"],
        "ident32": CST["ident32"],
        "strictL": CST["strictL"],
        "ones1x128": CST["ones1x128"],
        "ones128sq": CST["ones128sq"],
        "strictL32": CST["strictL32"],
        "iota16": CST["iota16"],
        "iotat": CST["iotat"],
        "lnW": CST["lnW"],
        "Wdecay": CST["Wdecay"],
        "hmaps": CST["hmaps"],
        "kvec": CST["kvec"],
    }
    return m


def kernel(**inputs):
    nc, _ = build_program()
    in_maps = [_prep_inputs(inputs, core) for core in range(8)]
    res = run_bass_kernel_spmd(nc, in_maps, core_ids=list(range(8)))
    return np.stack([r["y"] for r in res.results], axis=0)
